# version 06
# baseline (speedup 1.0000x reference)
import numpy as np
import ml_dtypes
from contextlib import ExitStack

import concourse.bass as bass
import concourse.mybir as mybir

F32 = mybir.dt.float32
BF16 = mybir.dt.bfloat16
I32 = mybir.dt.int32
AF = mybir.ActivationFunctionType
ALU = mybir.AluOpType
AX = mybir.AxisListType

C = 1024
C8 = 8


class Ctx:
    def __init__(self, nc, es):
        self.nc = nc
        self.es = es
        self.chain = es.enter_context(nc.semaphore("chain"))
        self.n = 0
        self.block = es.enter_context(nc.Block())

    def inc(self, instr, k=1):
        instr.then_inc(self.chain, k)
        self.n += k
        return instr

    def gate(self, eng, val=None):
        eng.wait_ge(self.chain, self.n if val is None else val)


def dcast_load(ctx, g, dst_sb, src_dram_ap):
    ctx.inc(g.dma_start(out=dst_sb, in_=src_dram_ap), 16)


def dcast_load_w(ctx, g, dst_sb, w_dram, dout=C):
    src = w_dram.rearrange("(c p) o -> p c o", p=128)
    dst = dst_sb.rearrange("p (c o) -> p c o", c=C8)
    ctx.inc(g.dma_start(out=dst, in_=src), 16)


def emit_ln(ctx, x_view, out_view, trash, cols, eps=1e-5, nfree=C):
    block = ctx.block

    @block.scalar
    def _(s):
        ctx.gate(s)
        s.activation(trash[:, 0:nfree], x_view, AF.Identity, accum_out=cols[:, 0:1])
        ctx.inc(s.activation(trash[:, 0:nfree], x_view, AF.Square, accum_out=cols[:, 1:2]))

    @block.vector
    def _(v):
        ctx.gate(v)
        ctx.inc(v.tensor_scalar(cols[:, 2:3], cols[:, 0:1], -1.0 / nfree, None, ALU.mult))
        ctx.gate(v)
        ctx.inc(v.tensor_tensor(cols[:, 3:4], cols[:, 2:3], cols[:, 2:3], ALU.mult))
        ctx.gate(v)
        ctx.inc(v.scalar_tensor_tensor(cols[:, 4:5], cols[:, 1:2], 1.0 / nfree,
                                       cols[:, 3:4], ALU.mult, ALU.subtract))
        ctx.gate(v)
        ctx.inc(v.tensor_scalar(cols[:, 4:5], cols[:, 4:5], eps, None, ALU.add))

    @block.scalar
    def _(s):
        ctx.gate(s)
        ctx.inc(s.activation(cols[:, 5:6], cols[:, 4:5], AF.Ln))
        ctx.gate(s)
        ctx.inc(s.activation(cols[:, 5:6], cols[:, 5:6], AF.Exp, scale=-0.5))

    @block.vector
    def _(v):
        ctx.gate(v)
        ctx.inc(v.tensor_tensor(cols[:, 6:7], cols[:, 2:3], cols[:, 5:6], ALU.mult))
        ctx.gate(v)
        ctx.inc(v.tensor_scalar(out_view, x_view, cols[:, 5:6], cols[:, 6:7],
                                ALU.mult, ALU.add))


def emit_transposes(ctx, src_views, psT, dst_views, idn, per_batch=4):
    block = ctx.block
    nb = len(src_views)
    i = 0
    while i < nb:
        batch = min(per_batch, nb - i)

        @block.tensor
        def _(te, i=i, batch=batch):
            ctx.gate(te)
            ins = None
            for j in range(batch):
                sv = src_views[i + j]
                p, f = sv.shape[0], sv.shape[1]
                ins = te.transpose(psT[0:f, j * 128: j * 128 + p], sv,
                                   idn[0:p, 0:p])
            ctx.inc(ins)

        @block.scalar
        def _(s, i=i, batch=batch):
            ctx.gate(s)
            last = None
            for j in range(batch):
                p = dst_views[i + j].shape[0]
                last = s.copy(dst_views[i + j], psT[0:p, j * 128:(j + 1) * 128])
            ctx.inc(last)
        i += batch


def emit_transposes3(ctx, src_views, psT, dst3, idn, per_batch=8, evac="scalar"):
    block = ctx.block
    nb = len(src_views)
    p = src_views[0].shape[0]
    f = src_views[0].shape[1]
    psv = psT[0:f, 0:per_batch * p].rearrange("q (b c) -> q b c", c=p)
    i = 0
    while i < nb:
        batch = min(per_batch, nb - i)

        @block.tensor
        def _(te, i=i, batch=batch):
            ctx.gate(te)
            ins = None
            for j in range(batch):
                ins = te.transpose(psT[0:f, j * p: (j + 1) * p], src_views[i + j],
                                   idn[0:p, 0:p])
            ctx.inc(ins)

        if evac == "scalar":
            @block.scalar
            def _(e, i=i, batch=batch):
                ctx.gate(e)
                ctx.inc(e.copy(dst3[:, i:i + batch, :], psv[:, 0:batch, :]))
        else:
            @block.vector
            def _(e, i=i, batch=batch):
                ctx.gate(e)
                ctx.inc(e.tensor_copy(dst3[:, i:i + batch, :], psv[:, 0:batch, :]))
        i += batch


def emit_stageA(ctx, S, d):
    nc = ctx.nc
    block = ctx.block
    T = S // 128
    NG = T // 4
    KB = S // 512
    assert T % 4 == 0 and S % 512 == 0

    es = ExitStack()
    with es:
        KT = es.enter_context(nc.sbuf_tensor("A_KT", [128, C8 * S], BF16))
        V = es.enter_context(nc.sbuf_tensor("A_V", [128, T * C], BF16))
        wA = es.enter_context(nc.sbuf_tensor("A_wA", [128, C8 * C], BF16))
        wB = es.enter_context(nc.sbuf_tensor("A_wB", [128, C8 * C], BF16))
        scores = es.enter_context(nc.sbuf_tensor("A_sc", [128, max(S, 2 * C)], F32))
        ebuf = es.enter_context(nc.sbuf_tensor("A_e", [128, max(S, C8 * 512)], BF16))
        QTg = es.enter_context(nc.sbuf_tensor("A_QTg", [128, C8 * 512], BF16))
        attnT = es.enter_context(nc.sbuf_tensor("A_aT", [128, S], BF16))
        x0t = scores[:, C:2 * C]
        lnb = es.enter_context(nc.sbuf_tensor("A_lnb", [128, C], BF16))
        trash = scores
        EB = max(S, C8 * 512)
        y1n = ebuf[:, EB - C:EB]
        y1t = ebuf[:, EB - 2 * C:EB - C]
        x1b = scores[:, 0:C]
        cols = es.enter_context(nc.sbuf_tensor("A_cols", [128, 16], F32))
        bqc = es.enter_context(nc.sbuf_tensor("A_bqc", [128, 8], F32))
        bkc = es.enter_context(nc.sbuf_tensor("A_bkc", [128, 8], F32))
        brow = es.enter_context(nc.sbuf_tensor("A_brow", [1, 2 * C], BF16))
        ones1 = es.enter_context(nc.sbuf_tensor("A_ones", [1, 128], BF16))
        idn = es.enter_context(nc.sbuf_tensor("A_idn", [128, 128], BF16))
        dmA = es.enter_context(nc.sbuf_tensor("A_dmA", [128, 256], F32))
        psS = es.enter_context(nc.psum_tensor("A_psS", [128, 2048], F32))
        psT = es.enter_context(nc.psum_tensor("A_psT", [128, 1024], BF16))
        psAV = es.enter_context(nc.psum_tensor("A_psAV", [128, 1024], F32))
        psX = es.enter_context(nc.psum_tensor("A_psX", [128, 512], F32))

        @block.sync
        def _(sy):
            ctx.gate(sy)
            ctx.inc(sy.dma_start(out=bqc[:], in_=d["bq_pc"]), 16)
            ctx.inc(sy.dma_start(out=bkc[:], in_=d["bk_pc"]), 16)
            ctx.inc(sy.dma_start(out=idn[:], in_=d["idn"]), 16)
            ctx.inc(sy.dma_start(out=dmA[:], in_=d["dmA"]), 16)

        @block.gpsimd
        def _(g):
            ctx.gate(g)
            dcast_load_w(ctx, g, wA, d["wk"])
            dcast_load_w(ctx, g, wB, d["wv"])
            dcast_load(ctx, g, brow[0:1, 0:C], d["bv_row"])
            dcast_load(ctx, g, brow[0:1, C:2 * C], d["bo_row"])
            ctx.inc(g.memset(ones1[:], 1.0))

        for g_i in range(NG):
            for j in range(4):
                t = 4 * g_i + j

                @block.sync
                def _(sy, t=t):
                    ctx.gate(sy)
                    ctx.inc(sy.dma_start(out=x0t, in_=d["x0"][t * 128:(t + 1) * 128, :]), 16)

                emit_ln(ctx, x0t, lnb[:], trash, cols)
                emit_transposes3(ctx, [lnb[:, c * 128:(c + 1) * 128] for c in range(C8)],
                                 psT,
                                 ebuf[:, 0:C8 * 512].rearrange("p (c t) -> p c t", c=C8)
                                 [:, :, j * 128:(j + 1) * 128],
                                 idn, evac="vector")

            for kd in range(C8):
                @block.tensor
                def _(te, kd=kd):
                    ctx.gate(te)
                    ins = None
                    for ci in range(C8):
                        ins = te.matmul(psX[:, 0:512],
                                        wA[:, ci * C + kd * 128: ci * C + (kd + 1) * 128],
                                        ebuf[:, ci * 512:(ci + 1) * 512],
                                        start=(ci == 0), stop=(ci == C8 - 1))
                    ctx.inc(ins)

                @block.scalar
                def _(s, kd=kd, g_i=g_i):
                    ctx.gate(s)
                    ctx.inc(s.activation(KT[:, kd * S + g_i * 512: kd * S + g_i * 512 + 512],
                                         psX[:, 0:512], AF.Identity,
                                         bias=bkc[:, kd:kd + 1]))

            for j in range(4):
                t = 4 * g_i + j
                for vb in range(2):
                    @block.tensor
                    def _(te, j=j, vb=vb):
                        ctx.gate(te)
                        for ci in range(C8):
                            te.matmul(psAV[:, vb * 512:(vb + 1) * 512],
                                      ebuf[:, ci * 512 + j * 128: ci * 512 + j * 128 + 128],
                                      wB[:, ci * C + vb * 512: ci * C + (vb + 1) * 512],
                                      start=(ci == 0), stop=False)
                        ins = te.matmul(psAV[:, vb * 512:(vb + 1) * 512],
                                        ones1[0:1, :],
                                        brow[0:1, vb * 512:(vb + 1) * 512],
                                        start=False, stop=True)
                        ctx.inc(ins)

                    @block.scalar
                    def _(s, t=t, vb=vb):
                        ctx.gate(s)
                        ctx.inc(s.copy(V[:, t * C + vb * 512: t * C + (vb + 1) * 512],
                                       psAV[:, vb * 512:(vb + 1) * 512]))

        @block.gpsimd
        def _(g):
            ctx.gate(g)
            dcast_load_w(ctx, g, wA, d["wq"])
            dcast_load_w(ctx, g, wB, d["wo"])

        for g_i in range(NG):
            for j in range(4):
                t = 4 * g_i + j

                @block.sync
                def _(sy, t=t):
                    ctx.gate(sy)
                    ctx.inc(sy.dma_start(out=x0t, in_=d["x0"][t * 128:(t + 1) * 128, :]), 16)

                emit_ln(ctx, x0t, lnb[:], trash, cols)
                emit_transposes3(ctx, [lnb[:, c * 128:(c + 1) * 128] for c in range(C8)],
                                 psT,
                                 ebuf[:, 0:C8 * 512].rearrange("p (c t) -> p c t", c=C8)
                                 [:, :, j * 128:(j + 1) * 128],
                                 idn, evac="vector")

            for kd in range(C8):
                @block.tensor
                def _(te, kd=kd):
                    ctx.gate(te)
                    ins = None
                    for ci in range(C8):
                        ins = te.matmul(psX[:, 0:512],
                                        wA[:, ci * C + kd * 128: ci * C + (kd + 1) * 128],
                                        ebuf[:, ci * 512:(ci + 1) * 512],
                                        start=(ci == 0), stop=(ci == C8 - 1))
                    ctx.inc(ins)

                @block.scalar
                def _(s, kd=kd):
                    ctx.gate(s)
                    ctx.inc(s.activation(QTg[:, kd * 512:(kd + 1) * 512], psX[:, 0:512],
                                         AF.Identity, bias=bqc[:, kd:kd + 1]))

            for j in range(4):
                t = 4 * g_i + j

                kb_lo = (t * 128) // 512 if t < T - 1 else 0
                kcl = kb_lo * 4
                for kb0 in range(kb_lo, KB, 4):
                    kbn = min(4, KB - kb0)

                    @block.tensor
                    def _(te, j=j, kb0=kb0, kbn=kbn):
                        ctx.gate(te)
                        ins = None
                        for kk in range(kbn):
                            kb = kb0 + kk
                            for kd in range(C8):
                                ins = te.matmul(
                                    psS[:, kk * 512:(kk + 1) * 512],
                                    QTg[:, kd * 512 + j * 128: kd * 512 + j * 128 + 128],
                                    KT[:, kd * S + kb * 512: kd * S + (kb + 1) * 512],
                                    start=(kd == 0), stop=(kd == C8 - 1))
                        ctx.inc(ins)

                    @block.scalar
                    def _(s, kb0=kb0, kbn=kbn):
                        ctx.gate(s)
                        ctx.inc(s.activation(scores[:, kb0 * 512: (kb0 + kbn) * 512],
                                             psS[:, 0:kbn * 512], AF.Identity,
                                             scale=1.0 / 32.0))

                if t * 128 > kb_lo * 512:
                    @block.gpsimd
                    def _(g, t=t, kb_lo=kb_lo):
                        ctx.gate(g)
                        ctx.inc(g.memset(scores[:, kb_lo * 512:t * 128], -1e30))
                dw = 256 if t < T - 1 else 128

                @block.vector
                def _(v, t=t, dw=dw, kb_lo=kb_lo):
                    ctx.gate(v)
                    ctx.inc(v.tensor_tensor(scores[:, t * 128:t * 128 + dw],
                                            scores[:, t * 128:t * 128 + dw],
                                            dmA[:, 0:dw], ALU.add))
                    ctx.gate(v)
                    ctx.inc(v.reduce_max(cols[:, 8:9], scores[:, kb_lo * 512:S],
                                         axis=AX.X, negate=True))

                @block.scalar
                def _(s, kb_lo=kb_lo):
                    ctx.gate(s)
                    ctx.inc(s.activation(ebuf[:, kb_lo * 512:S],
                                         scores[:, kb_lo * 512:S], AF.Exp,
                                         bias=cols[:, 8:9], scale=1.0,
                                         accum_out=cols[:, 9:10]))

                @block.vector
                def _(v):
                    ctx.gate(v)
                    ctx.inc(v.reciprocal(cols[:, 10:11], cols[:, 9:10]))

                emit_transposes3(ctx,
                                 [ebuf[:, kc * 128:(kc + 1) * 128] for kc in range(kcl, T)],
                                 psT,
                                 attnT[:, kcl * 128:S].rearrange("p (k c) -> p k c", c=128),
                                 idn, evac="vector")

                @block.tensor
                def _(te, kcl=kcl):
                    ctx.gate(te)
                    ins = None
                    for vb in range(2):
                        for kc in range(kcl, T):
                            ins = te.matmul(psAV[:, vb * 512:(vb + 1) * 512],
                                            attnT[:, kc * 128:(kc + 1) * 128],
                                            V[:, kc * C + vb * 512: kc * C + (vb + 1) * 512],
                                            start=(kc == kcl), stop=(kc == T - 1))
                    ctx.inc(ins)

                @block.vector
                def _(v):
                    ctx.gate(v)
                    ctx.inc(v.tensor_scalar(y1n, psAV[:, 0:C], cols[:, 10:11],
                                            None, ALU.mult))

                emit_transposes3(ctx,
                                 [y1n[:, c * 128:(c + 1) * 128] for c in range(C8)],
                                 psT,
                                 y1t.rearrange("p (c x) -> p c x", c=C8),
                                 idn, evac="vector")

                @block.sync
                def _(sy, t=t):
                    ctx.gate(sy)
                    ctx.inc(sy.dma_start(out=x0t, in_=d["x0"][t * 128:(t + 1) * 128, :]), 16)

                for ob in range(2):
                    @block.tensor
                    def _(te, ob=ob):
                        ctx.gate(te)
                        for ci in range(C8):
                            te.matmul(psX[:, 0:512],
                                      y1t[:, ci * 128:(ci + 1) * 128],
                                      wB[:, ci * C + ob * 512: ci * C + (ob + 1) * 512],
                                      start=(ci == 0), stop=False)
                        ins = te.matmul(psX[:, 0:512], ones1[0:1, :],
                                        brow[0:1, C + ob * 512: C + (ob + 1) * 512],
                                        start=False, stop=True)
                        ctx.inc(ins)

                    @block.vector
                    def _(v, ob=ob):
                        ctx.gate(v)
                        ctx.inc(v.scalar_tensor_tensor(
                            x1b[:, ob * 512:(ob + 1) * 512], psX[:, 0:512], 1.0,
                            x0t[:, ob * 512:(ob + 1) * 512], ALU.mult, ALU.add))

                @block.sync
                def _(sy, t=t):
                    ctx.gate(sy)
                    ctx.inc(sy.dma_start(out=d["x1"][t * 128:(t + 1) * 128, :],
                                         in_=x1b[:]), 16)


def prep_stageA_inputs(S, x0, wq, bq, wk, bk, wv, bv, wo, bo):
    f = np.float32
    return {
        "x0": np.ascontiguousarray(x0, f),
        "wq": np.ascontiguousarray(wq, f),
        "wk": np.ascontiguousarray(wk, f),
        "wv": np.ascontiguousarray(wv, f),
        "wo": np.ascontiguousarray(wo, f),
        "bq_pc": np.ascontiguousarray(np.asarray(bq, f).reshape(8, 128).T),
        "bk_pc": np.ascontiguousarray(np.asarray(bk, f).reshape(8, 128).T),
        "bv_row": np.ascontiguousarray(np.asarray(bv, f).reshape(1, C)),
        "bo_row": np.ascontiguousarray(np.asarray(bo, f).reshape(1, C)),
        "idn": np.eye(128).astype(ml_dtypes.bfloat16),
        "dmA": _dmaskA(),
    }


def _dmaskA():
    p = np.arange(128)[:, None]
    j = np.arange(256)[None, :]
    return np.where(j <= p + 1, -1e30, 0.0).astype(np.float32)


def declare_stageA(nc, S, kind_in="ExternalInput", kind_out="ExternalOutput"):
    return {
        "x0": nc.dram_tensor("x0", [S, C], F32, kind=kind_in).ap(),
        "wq": nc.dram_tensor("wq", [C, C], F32, kind=kind_in).ap(),
        "wk": nc.dram_tensor("wk", [C, C], F32, kind=kind_in).ap(),
        "wv": nc.dram_tensor("wv", [C, C], F32, kind=kind_in).ap(),
        "wo": nc.dram_tensor("wo", [C, C], F32, kind=kind_in).ap(),
        "bq_pc": nc.dram_tensor("bq_pc", [128, 8], F32, kind=kind_in).ap(),
        "bk_pc": nc.dram_tensor("bk_pc", [128, 8], F32, kind=kind_in).ap(),
        "bv_row": nc.dram_tensor("bv_row", [1, C], F32, kind=kind_in).ap(),
        "bo_row": nc.dram_tensor("bo_row", [1, C], F32, kind=kind_in).ap(),
        "idn": nc.dram_tensor("idn", [128, 128], BF16, kind=kind_in).ap(),
        "dmA": nc.dram_tensor("dmA", [128, 256], F32, kind=kind_in).ap(),
        "x1": nc.dram_tensor("x1", [S, C], F32, kind=kind_out).ap(),
    }


def build_stageA(S):
    nc = bass.Bass("TRN2", target_bir_lowering=False)
    d = declare_stageA(nc, S)
    with ExitStack() as es:
        ctx = Ctx(nc, es)
        emit_stageA(ctx, S, d)

        @ctx.block.sync
        def _(sy):
            ctx.gate(sy)
    return nc


def emit_stageB(ctx, S, d):
    nc = ctx.nc
    block = ctx.block
    T = S // 128
    H = 16

    es = ExitStack()
    with es:
        KTr = es.enter_context(nc.sbuf_tensor("B_KT", [128, C8 * S], BF16))
        Vb = es.enter_context(nc.sbuf_tensor("B_V", [128, T * C], BF16))
        wbuf = es.enter_context(nc.sbuf_tensor("B_w", [128, 16 * C], BF16))
        scores = es.enter_context(nc.sbuf_tensor("B_sc", [128, max(S, 4 * C)], F32))
        ebuf = es.enter_context(nc.sbuf_tensor("B_e", [128, max(S, C8 * 512)], BF16))
        attnT = es.enter_context(nc.sbuf_tensor("B_aT", [128, max(S, 2 * C)], BF16))
        qtb = es.enter_context(nc.sbuf_tensor("B_qt", [128, C], BF16))
        r1b = es.enter_context(nc.sbuf_tensor("B_r1", [128, C], BF16))
        krb = es.enter_context(nc.sbuf_tensor("B_kr", [128, C], BF16))
        n1w = es.enter_context(nc.sbuf_tensor("B_n1w", [128, C], BF16))
        cost = es.enter_context(nc.sbuf_tensor("B_cos", [128, C], BF16))
        sint = es.enter_context(nc.sbuf_tensor("B_sin", [128, C], BF16))
        cols = es.enter_context(nc.sbuf_tensor("B_cols", [128, 48], F32))
        idn = es.enter_context(nc.sbuf_tensor("B_idn", [128, 128], BF16))
        dmB = es.enter_context(nc.sbuf_tensor("B_dmB", [128, 128], F32))
        dmB2 = es.enter_context(nc.sbuf_tensor("B_dmB2", [128, 512], BF16))
        sbSC = es.enter_context(nc.semaphore("B_sc"))
        sbEV = es.enter_context(nc.semaphore("B_ev"))
        sbMX = es.enter_context(nc.semaphore("B_mx"))
        sbEX = es.enter_context(nc.semaphore("B_ex"))
        sbTT = es.enter_context(nc.semaphore("B_tt"))
        sbTE = es.enter_context(nc.semaphore("B_te"))
        sbAV = es.enter_context(nc.semaphore("B_av"))
        psS = es.enter_context(nc.psum_tensor("B_psS", [128, 2048], F32))
        psT = es.enter_context(nc.psum_tensor("B_psT", [128, 1024], BF16))
        psAV = es.enter_context(nc.psum_tensor("B_psAV", [128, 1024], F32))
        psX = es.enter_context(nc.psum_tensor("B_psX", [128, 512], F32))

        x1t = scores[:, 0:C]
        ub = scores[:, C:2 * C]
        trash = scores[:, 2 * C:3 * C]
        ktm = scores[:, 3 * C:4 * C]
        rot = scores[:, 0:C]
        y1n = attnT[:, 0:C]
        y1t = attnT[:, C:2 * C]

        def emit_ln_u(t):
            @block.sync
            def _(sy):
                ctx.gate(sy)
                ctx.inc(sy.dma_start(out=x1t, in_=d["x1"][t * 128:(t + 1) * 128, :]), 16)
            emit_ln(ctx, x1t, ub, scores[:, 2 * C:], cols)

        def emit_rms_r1():
            @block.scalar
            def _(s):
                ctx.gate(s)
                ctx.inc(s.activation(trash, ub, AF.Square, accum_out=cols[:, 0:1]))

            @block.vector
            def _(v):
                ctx.gate(v)
                ctx.inc(v.tensor_scalar(cols[:, 2:3], cols[:, 0:1], 1.0 / C, None,
                                        ALU.mult))
                ctx.gate(v)
                ctx.inc(v.tensor_scalar(cols[:, 2:3], cols[:, 2:3], 1e-6, None, ALU.add))

            @block.scalar
            def _(s):
                ctx.gate(s)
                ctx.inc(s.activation(cols[:, 3:4], cols[:, 2:3], AF.Ln))
                ctx.gate(s)
                ctx.inc(s.activation(cols[:, 3:4], cols[:, 3:4], AF.Exp, scale=-0.5))

            @block.vector
            def _(v):
                ctx.gate(v)
                ctx.inc(v.tensor_scalar(trash, ub, cols[:, 3:4], None, ALU.mult))
                ctx.gate(v)
                ctx.inc(v.tensor_tensor(r1b[:], trash, n1w[:], ALU.mult))

        def emit_rope(t, src_f32, dst_bf):
            @block.sync
            def _(sy):
                ctx.gate(sy)
                ctx.inc(sy.dma_start(out=cost[:], in_=d["cosb"][t * 128:(t + 1) * 128, :]), 16)
                ctx.inc(sy.dma_start(out=sint[:], in_=d["sinb"][t * 128:(t + 1) * 128, :]), 16)

            sv = src_f32.rearrange("p (h x j) -> p h x j", h=H, x=2)
            rv = rot.rearrange("p (h x j) -> p h x j", h=H, x=2)

            @block.vector
            def _(v):
                ctx.gate(v)
                v.tensor_scalar(rv[:, :, 0, :], sv[:, :, 1, :], -1.0, None, ALU.mult)
                ctx.inc(v.tensor_copy(rv[:, :, 1, :], sv[:, :, 0, :]))
                ctx.gate(v)
                ctx.inc(v.tensor_tensor(trash, src_f32, cost[:], ALU.mult))
                ctx.gate(v)
                ctx.inc(v.tensor_tensor(rot, rot, sint[:], ALU.mult))
                ctx.gate(v)
                ctx.inc(v.tensor_tensor(dst_bf, trash, rot, ALU.add))

        @block.sync
        def _(sy):
            ctx.gate(sy)
            ctx.inc(sy.dma_start(out=idn[:], in_=d["idn"]), 16)
            ctx.inc(sy.dma_start(out=dmB[:], in_=d["dmB"]), 16)

        @block.gpsimd
        def _(g):
            ctx.gate(g)
            dcast_load(ctx, g, dmB2[:], d["dmB2"])

        @block.gpsimd
        def _(g):
            ctx.gate(g)
            src = d["qkvT"].rearrange("(c p) o -> p c o", p=128)[:, :, C:3 * C]
            dst = wbuf.rearrange("p (c o) -> p c o", c=C8)
            ctx.inc(g.dma_start(out=dst, in_=src), 16)
            dcast_load(ctx, g, n1w[:], d["n1w_r"])

        for t in range(T):
            emit_ln_u(t)
            emit_rms_r1()
            emit_transposes3(ctx, [r1b[:, c * 128:(c + 1) * 128] for c in range(C8)],
                             psT,
                             ebuf[:, 0:C8 * 512].rearrange("p (c t) -> p c t", c=C8)
                             [:, :, (t % 4) * 128:(t % 4) * 128 + 128],
                             idn, evac="vector")
            jj = t % 4

            for vb in range(2):
                @block.tensor
                def _(te, jj=jj, vb=vb):
                    ctx.gate(te)
                    ins = None
                    for ci in range(C8):
                        ins = te.matmul(psAV[:, vb * 512:(vb + 1) * 512],
                                        ebuf[:, ci * 512 + jj * 128: ci * 512 + jj * 128 + 128],
                                        wbuf[:, ci * 2 * C + vb * 512: ci * 2 * C + (vb + 1) * 512],
                                        start=(ci == 0), stop=(ci == C8 - 1))
                    ctx.inc(ins)

            @block.scalar
            def _(s):
                ctx.gate(s)
                ctx.inc(s.copy(ktm, psAV[:, 0:C]))

            emit_rope(t, ktm, krb[:])
            emit_transposes3(ctx, [krb[:, c * 128:(c + 1) * 128] for c in range(C8)],
                             psT,
                             KTr.rearrange("p (c s) -> p c s", c=C8)
                             [:, :, t * 128:(t + 1) * 128],
                             idn, evac="vector")

            for vb in range(2):
                @block.tensor
                def _(te, jj=jj, vb=vb):
                    ctx.gate(te)
                    ins = None
                    for ci in range(C8):
                        ins = te.matmul(psAV[:, vb * 512:(vb + 1) * 512],
                                        ebuf[:, ci * 512 + jj * 128: ci * 512 + jj * 128 + 128],
                                        wbuf[:, ci * 2 * C + C + vb * 512: ci * 2 * C + C + (vb + 1) * 512],
                                        start=(ci == 0), stop=(ci == C8 - 1))
                    ctx.inc(ins)

                @block.scalar
                def _(s, t=t, vb=vb):
                    ctx.gate(s)
                    ctx.inc(s.copy(Vb[:, t * C + vb * 512: t * C + (vb + 1) * 512],
                                   psAV[:, vb * 512:(vb + 1) * 512]))

        @block.gpsimd
        def _(g):
            ctx.gate(g)
            src = d["qkvT"].rearrange("(c p) o -> p c o", p=128)[:, :, 0:C]
            dst = wbuf[:, 0:C8 * C].rearrange("p (c o) -> p c o", c=C8)
            ctx.inc(g.dma_start(out=dst, in_=src), 16)
            dcast_load_w(ctx, g, wbuf[:, C8 * C:16 * C], d["projT"])

        for t in range(T):
            kmax = (t + 1) * 128
            KBt = (kmax + 511) // 512

            emit_ln_u(t)
            emit_rms_r1()
            emit_transposes3(ctx, [r1b[:, c * 128:(c + 1) * 128] for c in range(C8)],
                             psT,
                             ebuf[:, 0:C8 * 512].rearrange("p (c t) -> p c t", c=C8)
                             [:, :, 0:128],
                             idn, evac="vector")

            for vb in range(2):
                @block.tensor
                def _(te, vb=vb):
                    ctx.gate(te)
                    ins = None
                    for ci in range(C8):
                        ins = te.matmul(psAV[:, vb * 512:(vb + 1) * 512],
                                        ebuf[:, ci * 512: ci * 512 + 128],
                                        wbuf[:, ci * C + vb * 512: ci * C + (vb + 1) * 512],
                                        start=(ci == 0), stop=(ci == C8 - 1))
                    ctx.inc(ins)

            @block.scalar
            def _(s):
                ctx.gate(s)
                ctx.inc(s.copy(ktm, psAV[:, 0:C]))

            emit_rope(t, ktm, krb[:])
            emit_transposes3(ctx, [krb[:, c * 128:(c + 1) * 128] for c in range(C8)],
                             psT,
                             qtb.rearrange("p (c x) -> p c x", c=C8),
                             idn, evac="vector")

            if t == 0:
                ctx.b_g = 0
                ctx.b_b = 0
                ctx.b_u = 0

            def b_groups(KBt):
                gs = []
                kb0 = 0
                while kb0 < KBt - 1:
                    kbn = min(4, KBt - 1 - kb0)
                    gs.append((kb0, kbn))
                    kb0 += kbn
                gs.append((KBt - 1, 1))
                return gs

            def emit_scores_te(u, h, first, g_base, kmax=kmax, KBt=KBt):
                pb = (h % 2) * 64
                ch = h // 2
                for gg, (kb0, kbn) in enumerate(b_groups(KBt)):
                    g_i = g_base + gg

                    @block.tensor
                    def _(te, h=h, pb=pb, ch=ch, kb0=kb0, kbn=kbn, g_i=g_i,
                          first=first, kmax=kmax):
                        if first and kb0 == 0:
                            ctx.gate(te)
                        if g_i >= 1:
                            te.wait_ge(sbEV, g_i)
                        ins = None
                        for kk in range(kbn):
                            kb = kb0 + kk
                            w = min(512, kmax - kb * 512)
                            ins = te.matmul(
                                psS[:, kk * 512: kk * 512 + w],
                                qtb[pb:pb + 64, ch * 128:(ch + 1) * 128],
                                KTr[pb:pb + 64, ch * S + kb * 512: ch * S + kb * 512 + w],
                                start=True, stop=True)
                        ins.then_inc(sbSC, 1)

            def emit_scores_evac(u, g_base, kmax=kmax, KBt=KBt):
                gl = b_groups(KBt)
                for gg, (kb0, kbn) in enumerate(gl):
                    g_i = g_base + gg
                    last = (gg == len(gl) - 1)
                    if not last:
                        @block.scalar
                        def _(s, kb0=kb0, kbn=kbn, g_i=g_i, kmax=kmax, u=u):
                            s.wait_ge(sbSC, g_i + 1)
                            if kb0 == 0 and u >= 1:
                                s.wait_ge(sbEX, u)
                            w = min(kbn * 512, kmax - kb0 * 512)
                            s.activation(scores[:, kb0 * 512: kb0 * 512 + w],
                                         psS[:, 0:w], AF.Identity,
                                         scale=1.0 / 8.0).then_inc(sbEV, 1)
                    else:
                        @block.vector
                        def _(v, kb0=kb0, g_i=g_i, kmax=kmax, u=u):
                            v.wait_ge(sbSC, g_i + 1)
                            if kb0 == 0 and u >= 1:
                                v.wait_ge(sbEX, u)
                            w = kmax - kb0 * 512
                            v.scalar_tensor_tensor(
                                scores[:, kb0 * 512: kmax], psS[:, 0:w], 1.0 / 8.0,
                                dmB2[:, 512 - w:512], ALU.mult,
                                ALU.add).then_inc(sbEV, 1)

            def emit_soft_mask(u, h, g_end, t=t, kmax=kmax):
                @block.vector
                def _(v, u=u, g_end=g_end, t=t, kmax=kmax):
                    v.wait_ge(sbEV, g_end)
                    v.reduce_max(cols[:, 8:9], scores[:, 0:kmax], axis=AX.X,
                                 negate=True).then_inc(sbMX, 1)

                @block.scalar
                def _(s, u=u, h=h, b_prev=ctx.b_b, kmax=kmax):
                    s.wait_ge(sbMX, u + 1)
                    s.wait_ge(sbTE, b_prev)
                    s.activation(ebuf[:, 0:kmax], scores[:, 0:kmax], AF.Exp,
                                 bias=cols[:, 8:9],
                                 accum_out=cols[:, 16 + h:17 + h]).then_inc(sbEX, 1)

            def emit_tp_av(u, h, t=t, kmax=kmax):
                nb = t + 1
                psv = psT[0:128, 0:8 * 128].rearrange("q (b c) -> q b c", c=128)
                dst3 = attnT[:, 0:nb * 128].rearrange("p (k c) -> p k c", c=128)
                i = 0
                while i < nb:
                    batch = min(8, nb - i)
                    b_i = ctx.b_b
                    ctx.b_b += 1

                    @block.tensor
                    def _(te, u=u, i=i, batch=batch, b_i=b_i):
                        te.wait_ge(sbEX, u + 1)
                        if b_i >= 1:
                            te.wait_ge(sbTE, b_i)
                        ins = None
                        for j in range(batch):
                            ins = te.transpose(psT[0:128, j * 128:(j + 1) * 128],
                                               ebuf[:, (i + j) * 128:(i + j + 1) * 128],
                                               idn[:, 0:128])
                        ins.then_inc(sbTT, 1)

                    @block.vector
                    def _(v, u=u, i=i, batch=batch, b_i=b_i):
                        if i == 0:
                            v.wait_ge(sbAV, u)
                        v.wait_ge(sbTT, b_i + 1)
                        v.tensor_copy(dst3[:, i:i + batch, :],
                                      psv[:, 0:batch, :]).then_inc(sbTE, 1)
                    i += batch

                @block.tensor
                def _(te, t=t, h=h, b_end=ctx.b_b):
                    te.wait_ge(sbTE, b_end)
                    ins = None
                    for kc in range(t + 1):
                        ins = te.matmul(psAV[:, h * 64:(h + 1) * 64],
                                        attnT[:, kc * 128:(kc + 1) * 128],
                                        Vb[:, kc * C + h * 64: kc * C + h * 64 + 64],
                                        start=(kc == 0), stop=(kc == t))
                    ins.then_inc(sbAV, 1)

            ngrp = len(b_groups(KBt))
            g0 = ctx.b_g
            ctx.b_g += H * ngrp
            emit_scores_te(ctx.b_u, 0, True, g0)
            emit_scores_evac(ctx.b_u, g0)
            for h in range(H):
                u = ctx.b_u
                if h + 1 < H:
                    emit_scores_te(u + 1, h + 1, False, g0 + (h + 1) * ngrp)
                emit_soft_mask(u, h, g0 + (h + 1) * ngrp)
                if h + 1 < H:
                    emit_scores_evac(u + 1, g0 + (h + 1) * ngrp)
                emit_tp_av(u, h)
                ctx.b_u += 1

            @block.vector
            def _(v, t=t):
                v.wait_ge(sbAV, 16 * (t + 1))
                ctx.gate(v)
                ctx.inc(v.reciprocal(cols[:, 32:48], cols[:, 16:32]))

            for h in range(H):
                @block.vector
                def _(v, h=h):
                    ctx.gate(v)
                    ctx.inc(v.tensor_scalar(y1n[:, h * 64:(h + 1) * 64],
                                            psAV[:, h * 64:(h + 1) * 64],
                                            cols[:, 32 + h:33 + h], None, ALU.mult))

            emit_transposes3(ctx, [y1n[:, c * 128:(c + 1) * 128] for c in range(C8)],
                             psT,
                             y1t.rearrange("p (c x) -> p c x", c=C8),
                             idn, evac="vector")

            emit_ln_u(t)

            for ob in range(2):
                @block.tensor
                def _(te, ob=ob):
                    ctx.gate(te)
                    ins = None
                    for ci in range(C8):
                        ins = te.matmul(psX[:, 0:512],
                                        y1t[:, ci * 128:(ci + 1) * 128],
                                        wbuf[:, C8 * C + ci * C + ob * 512: C8 * C + ci * C + (ob + 1) * 512],
                                        start=(ci == 0), stop=(ci == C8 - 1))
                    ctx.inc(ins)

                @block.vector
                def _(v, ob=ob):
                    ctx.gate(v)
                    ctx.inc(v.scalar_tensor_tensor(
                        trash[:, ob * 512:(ob + 1) * 512] if False else scores[:, 2 * C + ob * 512: 2 * C + (ob + 1) * 512],
                        psX[:, 0:512], 1.0,
                        ub[:, ob * 512:(ob + 1) * 512], ALU.mult, ALU.add))

            @block.sync
            def _(sy, t=t):
                ctx.gate(sy)
                ctx.inc(sy.dma_start(out=d["a"][t * 128:(t + 1) * 128, :],
                                     in_=scores[:, 2 * C:3 * C]), 16)


def prep_stageB_inputs(S, qkv_w, proj_w, norm1_w):
    f = np.float32
    hd = 64
    inv = 1.0 / (10000.0 ** (np.arange(0, hd, 2) / hd))
    pos = np.arange(S)[:, None].astype(np.float64)
    fr = pos * inv[None, :]
    emb = np.concatenate([fr, fr], axis=1)
    cos = np.cos(emb)
    sin = np.sin(emb)
    cosb = np.tile(cos, (1, 16)).astype(ml_dtypes.bfloat16)
    sinb = np.tile(sin, (1, 16)).astype(ml_dtypes.bfloat16)
    return {
        "qkvT": np.ascontiguousarray(np.asarray(qkv_w, f).T),
        "projT": np.ascontiguousarray(np.asarray(proj_w, f).T),
        "n1w_r": np.ascontiguousarray(np.broadcast_to(np.asarray(norm1_w, f), (128, C))),
        "cosb": np.ascontiguousarray(cosb),
        "sinb": np.ascontiguousarray(sinb),
        "idn_b": np.eye(128).astype(ml_dtypes.bfloat16),
        "dmB": _dmaskB(),
        "dmB2": np.concatenate([np.zeros((128, 384), np.float32), _dmaskB()], axis=1),
    }


def _dmaskB():
    p = np.arange(128)[:, None]
    j = np.arange(128)[None, :]
    return np.where(j <= p, 0.0, -1e30).astype(np.float32)


def declare_stageB(nc, S, kind_in="ExternalInput", kind_out="ExternalOutput",
                   x1_kind=None):
    return {
        "x1": nc.dram_tensor("x1", [S, C], F32, kind=x1_kind or kind_in).ap(),
        "qkvT": nc.dram_tensor("qkvT", [C, 3 * C], F32, kind=kind_in).ap(),
        "projT": nc.dram_tensor("projT", [C, C], F32, kind=kind_in).ap(),
        "n1w_r": nc.dram_tensor("n1w_r", [128, C], F32, kind=kind_in).ap(),
        "cosb": nc.dram_tensor("cosb", [S, C], BF16, kind=kind_in).ap(),
        "sinb": nc.dram_tensor("sinb", [S, C], BF16, kind=kind_in).ap(),
        "idn": nc.dram_tensor("idn_b", [128, 128], BF16, kind=kind_in).ap(),
        "dmB": nc.dram_tensor("dmB", [128, 128], F32, kind=kind_in).ap(),
        "dmB2": nc.dram_tensor("dmB2", [128, 512], F32, kind=kind_in).ap(),
        "a": nc.dram_tensor("a", [S, C], F32, kind=kind_out).ap(),
    }


def build_stageB(S):
    nc = bass.Bass("TRN2", target_bir_lowering=False)
    d = declare_stageB(nc, S)
    with ExitStack() as es:
        ctx = Ctx(nc, es)
        emit_stageB(ctx, S, d)

        @ctx.block.sync
        def _(sy):
            ctx.gate(sy)
    return nc


def emit_stageC(ctx, S, d):
    nc = ctx.nc
    block = ctx.block
    T = S // 128
    E, HF = 8, 4096
    TGT = min(8, T)
    TGN = T // TGT
    assert T % TGT == 0

    es = ExitStack()
    with es:
        r2Tg = es.enter_context(nc.sbuf_tensor("C_r2T", [128, C8 * TGT * 128], BF16))
        hbuf = es.enter_context(nc.sbuf_tensor("C_h", [128, 32 * TGT * 128], BF16))
        wgc = [es.enter_context(nc.sbuf_tensor(f"C_wg{i}", [128, C8 * 512], BF16))
               for i in range(2)]
        wuc = [es.enter_context(nc.sbuf_tensor(f"C_wu{i}", [128, C8 * 512], BF16))
               for i in range(2)]
        wdc = [es.enter_context(nc.sbuf_tensor(f"C_wd{i}", [128, 8 * C], BF16))
               for i in range(2)]
        s_guld = es.enter_context(nc.semaphore("C_guld"))
        sP = es.enter_context(nc.semaphore("C_sP"))
        sA = es.enter_context(nc.semaphore("C_sA"))
        sD = es.enter_context(nc.semaphore("C_sD"))
        s_guus = es.enter_context(nc.semaphore("C_guus"))
        s_wdld = es.enter_context(nc.semaphore("C_wdld"))
        s_wdus = es.enter_context(nc.semaphore("C_wdus"))
        acc = es.enter_context(nc.sbuf_tensor("C_acc", [128, TGT * C], F32))
        gts = es.enter_context(nc.sbuf_tensor("C_gts", [128, TGT * 8], F32))
        rw = es.enter_context(nc.sbuf_tensor("C_rw", [128, C8 * 8], F32))
        n2w = es.enter_context(nc.sbuf_tensor("C_n2w", [128, C], BF16))
        n2wf = es.enter_context(nc.sbuf_tensor("C_n2wf", [128, C], F32))
        r2f = es.enter_context(nc.sbuf_tensor("C_r2f", [128, C], F32))
        r2Tf = es.enter_context(nc.sbuf_tensor("C_r2Tf", [128, C], F32))
        idnf = es.enter_context(nc.sbuf_tensor("C_idnf", [128, 128], F32))
        at = es.enter_context(nc.sbuf_tensor("C_at", [128, C], F32))
        x1t = es.enter_context(nc.sbuf_tensor("C_x1t", [128, C], F32))
        r2b = es.enter_context(nc.sbuf_tensor("C_r2b", [128, C], BF16))
        gtmp = es.enter_context(nc.sbuf_tensor("C_gtmp", [128, C], BF16))
        trash = es.enter_context(nc.sbuf_tensor("C_tr", [128, C], F32))
        lg = es.enter_context(nc.sbuf_tensor("C_lg", [128, 8], F32))
        l2 = es.enter_context(nc.sbuf_tensor("C_l2", [128, 8], F32))
        msk1 = es.enter_context(nc.sbuf_tensor("C_m1", [128, 8], F32))
        cols = es.enter_context(nc.sbuf_tensor("C_cols", [128, 16], F32))
        idn = es.enter_context(nc.sbuf_tensor("C_idn", [128, 128], BF16))
        psALL = es.enter_context(nc.psum_tensor("C_psA", [128, 3584], F32))
        psT = es.enter_context(nc.psum_tensor("C_psT", [128, 1024], BF16))
        psG = psALL[:, 0:1024]
        psU = psALL[:, 1024:2048]
        psR = psALL[:, 3072:3080]

        @block.gpsimd
        def _(g):
            ctx.gate(g)
            if "cast_done" not in d:
                for e in range(E):
                    ctx.inc(g.dma_start(out=d["wgT_bf"][e], in_=d["wgT"][e]), 16)
                    ctx.inc(g.dma_start(out=d["wuT_bf"][e], in_=d["wuT"][e]), 16)
                    ctx.inc(g.dma_start(out=d["wdT_bf"][e], in_=d["wdT"][e]), 16)
            dcast_load(ctx, g, n2w[:], d["n2w_r"])

        @block.sync
        def _(sy):
            ctx.gate(sy)
            src = d["rwT"].rearrange("(c p) o -> p c o", p=128)
            ctx.inc(sy.dma_start(out=rw.rearrange("p (c o) -> p c o", c=C8), in_=src), 16)
            ctx.inc(sy.dma_start(out=n2wf[:], in_=d["n2w_r"]), 16)
            ctx.inc(sy.dma_start(out=idnf[:], in_=d["idn_f"]), 16)

        @block.sync
        def _(sy):
            ctx.gate(sy)
            ctx.inc(sy.dma_start(out=idn[:], in_=d["idn"]), 16)

        for tg in range(TGN):
            for tt in range(TGT):
                t = tg * TGT + tt

                @block.sync
                def _(sy, t=t):
                    ctx.gate(sy)
                    ctx.inc(sy.dma_start(out=at[:], in_=d["a"][t * 128:(t + 1) * 128, :]), 16)

                @block.scalar
                def _(s):
                    ctx.gate(s)
                    ctx.inc(s.activation(trash[:], at[:], AF.Square, accum_out=cols[:, 0:1]))

                @block.vector
                def _(v):
                    ctx.gate(v)
                    ctx.inc(v.tensor_scalar(cols[:, 1:2], cols[:, 0:1], 1.0 / C, None, ALU.mult))
                    ctx.gate(v)
                    ctx.inc(v.tensor_scalar(cols[:, 1:2], cols[:, 1:2], 1e-6, None, ALU.add))

                @block.scalar
                def _(s):
                    ctx.gate(s)
                    ctx.inc(s.activation(cols[:, 2:3], cols[:, 1:2], AF.Ln))
                    ctx.gate(s)
                    ctx.inc(s.activation(cols[:, 2:3], cols[:, 2:3], AF.Exp, scale=-0.5))

                @block.vector
                def _(v):
                    ctx.gate(v)
                    ctx.inc(v.tensor_scalar(trash[:], at[:], cols[:, 2:3], None, ALU.mult))
                    ctx.gate(v)
                    ctx.inc(v.tensor_tensor(r2f[:], trash[:], n2wf[:], ALU.mult))
                    ctx.gate(v)
                    ctx.inc(v.tensor_copy(r2b[:], r2f[:]))

                emit_transposes3(ctx, [r2f[:, c * 128:(c + 1) * 128] for c in range(C8)],
                                 psALL[:, 2048:2560],
                                 r2Tf.rearrange("p (c x) -> p c x", c=C8),
                                 idnf, per_batch=4, evac="vector")
                emit_transposes3(ctx, [r2b[:, c * 128:(c + 1) * 128] for c in range(C8)],
                                 psT,
                                 r2Tg.rearrange("p (c t) -> p c t", c=C8)
                                 [:, :, tt * 128:(tt + 1) * 128],
                                 idn, evac="vector")

                @block.tensor
                def _(te):
                    ctx.gate(te)
                    ins = None
                    for ci in range(C8):
                        ins = te.matmul(psR,
                                        r2Tf[:, ci * 128:(ci + 1) * 128],
                                        rw[:, ci * 8:(ci + 1) * 8],
                                        start=(ci == 0), stop=(ci == C8 - 1))
                    ctx.inc(ins)

                @block.scalar
                def _(s):
                    ctx.gate(s)
                    ctx.inc(s.copy(lg[:], psR))

                @block.vector
                def _(v):
                    ctx.gate(v)
                    ctx.inc(v.reduce_max(cols[:, 4:5], lg[:], axis=AX.X))
                    ctx.gate(v)
                    ctx.inc(v.tensor_scalar(msk1[:], lg[:], cols[:, 4:5], None, ALU.is_ge))
                    ctx.gate(v)
                    ctx.inc(v.scalar_tensor_tensor(l2[:], msk1[:], -1e30, lg[:],
                                                   ALU.mult, ALU.add))
                    ctx.gate(v)
                    ctx.inc(v.reduce_max(cols[:, 5:6], l2[:], axis=AX.X))
                    ctx.gate(v)
                    ctx.inc(v.tensor_tensor(cols[:, 6:7], cols[:, 5:6], cols[:, 4:5],
                                            ALU.subtract))

                @block.scalar
                def _(s):
                    ctx.gate(s)
                    ctx.inc(s.activation(cols[:, 7:8], cols[:, 6:7], AF.Exp))

                if "dbgC" in d:
                    @block.sync
                    def _(sy, t=t):
                        ctx.gate(sy)
                        ctx.inc(sy.dma_start(out=d["dbgC"][0, t * 128:(t + 1) * 128, 0:8],
                                             in_=lg[:]), 16)

                @block.vector
                def _(v, tt=tt):
                    ctx.gate(v)
                    ctx.inc(v.tensor_scalar(cols[:, 8:9], cols[:, 7:8], 1.0, None, ALU.add))
                    ctx.gate(v)
                    ctx.inc(v.reciprocal(cols[:, 9:10], cols[:, 8:9]))
                    ctx.gate(v)
                    ctx.inc(v.tensor_tensor(cols[:, 10:11], cols[:, 7:8], cols[:, 9:10],
                                            ALU.mult))
                    ctx.gate(v)
                    ctx.inc(v.tensor_scalar(msk1[:], msk1[:], cols[:, 9:10], None, ALU.mult))
                    ctx.gate(v)
                    ctx.inc(v.tensor_scalar(l2[:], l2[:], cols[:, 5:6], None, ALU.is_ge))
                    ctx.gate(v)
                    ctx.inc(v.scalar_tensor_tensor(gts[:, tt * 8:(tt + 1) * 8], l2[:],
                                                   cols[:, 10:11], msk1[:],
                                                   ALU.mult, ALU.add))

                if "dbgC" in d:
                    @block.sync
                    def _(sy, t=t, tt=tt):
                        ctx.gate(sy)
                        ctx.inc(sy.dma_start(out=d["dbgC"][1, t * 128:(t + 1) * 128, 0:8],
                                             in_=gts[:, tt * 8:(tt + 1) * 8]), 16)

            gu_idx = tg * E * 8
            n_tsg = (TGT + 2) // 3
            wd_idx = tg * E * n_tsg * 4

            @block.sync
            def _(sy, tg=tg, gu_idx=gu_idx, wd_idx=wd_idx):
                ctx.gate(sy)
                if tg == 0 and "cast_done" in d:
                    sy.wait_ge(d["cast_done"][0], d["cast_done"][1])
                for e in range(E):
                    for hb in range(8):
                        k = gu_idx + e * 8 + hb
                        if k >= 2:
                            sy.wait_ge(s_guus, k - 1)
                        srcg = d["wgT_bf"][e].rearrange("(c p) h -> p c h", p=128)[:, :, hb * 512:(hb + 1) * 512]
                        sy.dma_start(out=wgc[k % 2].rearrange("p (c h) -> p c h", c=C8),
                                     in_=srcg).then_inc(s_guld, 16)
                        srcu = d["wuT_bf"][e].rearrange("(c p) h -> p c h", p=128)[:, :, hb * 512:(hb + 1) * 512]
                        sy.dma_start(out=wuc[k % 2].rearrange("p (c h) -> p c h", c=C8),
                                     in_=srcu).then_inc(s_guld, 16)
                    for i in range(n_tsg * 4):
                        lci = i % 4
                        k = wd_idx + e * n_tsg * 4 + i
                        if k >= 2:
                            sy.wait_ge(s_wdus, k - 1)
                        sy.dma_start(out=wdc[k % 2].rearrange("p (hc c) -> p hc c", hc=8),
                                     in_=d["wdT_bf"][e][lci * 1024:(lci + 1) * 1024, :]
                                     .rearrange("(hc p) c -> p hc c", p=128)
                                     ).then_inc(s_wdld, 16)

            H2 = TGT * 128 // 512
            units_per_e = 8 * 4 * H2
            for e in range(E):
                for hb in range(8):
                    k = gu_idx + e * 8 + hb
                    for hs in range(4):
                        hc = hb * 4 + hs
                        for half in range(H2):
                            u = ctx.c_unit = getattr(ctx, "c_unit", -1) + 1
                            par = u % 2
                            first_e = (hb == 0 and hs == 0 and half == 0)
                            first_hb = (hs == 0 and half == 0)
                            last_hb = (hs == 3 and half == H2 - 1)
                            last_e = (hb == 7 and last_hb)
                            gs = psALL[:, par * 1024: par * 1024 + 512]
                            us_ = psALL[:, par * 1024 + 512:(par + 1) * 1024]
                            gt = gtmp[:, par * 512:(par + 1) * 512]

                            @block.tensor
                            def _(te, hs=hs, half=half, k=k, u=u, gs=gs, us_=us_,
                                  first_e=first_e, first_hb=first_hb):
                                if first_e:
                                    ctx.gate(te)
                                if first_hb:
                                    te.wait_ge(s_guld, 32 * (k + 1))
                                if u >= 2:
                                    te.wait_ge(sD, u - 1)
                                ins = None
                                for ci in range(C8):
                                    ins = te.matmul(
                                        gs,
                                        wgc[k % 2][:, ci * 512 + hs * 128: ci * 512 + hs * 128 + 128],
                                        r2Tg[:, ci * TGT * 128 + half * 512: ci * TGT * 128 + (half + 1) * 512],
                                        start=(ci == 0), stop=(ci == C8 - 1))
                                for ci in range(C8):
                                    ins = te.matmul(
                                        us_,
                                        wuc[k % 2][:, ci * 512 + hs * 128: ci * 512 + hs * 128 + 128],
                                        r2Tg[:, ci * TGT * 128 + half * 512: ci * TGT * 128 + (half + 1) * 512],
                                        start=(ci == 0), stop=(ci == C8 - 1))
                                ins.then_inc(sP, 1)

                            @block.scalar
                            def _(s, u=u, gs=gs, gt=gt):
                                s.wait_ge(sP, u + 1)
                                s.activation(gt, gs, AF.Silu).then_inc(sA, 1)

                            @block.vector
                            def _(v, u=u, hc=hc, half=half, gt=gt, us_=us_,
                                  first_e=first_e, last_hb=last_hb, last_e=last_e):
                                if first_e:
                                    ctx.gate(v)
                                v.wait_ge(sA, u + 1)
                                if last_hb:
                                    v.engine_nop().then_inc(s_guus, 1)
                                ins = v.tensor_tensor(
                                    hbuf[:, hc * TGT * 128 + half * 512: hc * TGT * 128 + (half + 1) * 512],
                                    gt, us_, ALU.mult)
                                ins.then_inc(sD, 1)
                                if last_e:
                                    v.wait_ge(sD, u + 1)
                                    ctx.inc(v.engine_nop())

                ts0 = 0
                tsg_i = 0
                while ts0 < TGT:
                    nts = min(3, TGT - ts0)
                    for lci in range(4):
                        k = wd_idx + (e * n_tsg + tsg_i) * 4 + lci

                        @block.tensor
                        def _(te, lci=lci, ts0=ts0, nts=nts, k=k):
                            if lci == 0:
                                ctx.gate(te)
                            te.wait_ge(s_wdld, 16 * (k + 1))
                            ins = None
                            for hs in range(8):
                                hc = lci * 8 + hs
                                for tsl in range(nts):
                                    ts = ts0 + tsl
                                    for cb in range(2):
                                        ins = te.matmul(
                                            psALL[:, tsl * 1024 + cb * 512: tsl * 1024 + (cb + 1) * 512],
                                            hbuf[:, hc * TGT * 128 + ts * 128: hc * TGT * 128 + ts * 128 + 128],
                                            wdc[k % 2][:, hs * C + cb * 512: hs * C + (cb + 1) * 512],
                                            start=(hc == 0), stop=(hc == 31))
                            if lci == 3:
                                ctx.inc(ins)
                            else:
                                ins.then_inc(s_wdus, 1)

                    for tsl in range(nts):
                        ts = ts0 + tsl

                        @block.vector
                        def _(v, e=e, ts=ts, tsl=tsl, first=(tsl == 0)):
                            ctx.gate(v)
                            if first:
                                v.engine_nop().then_inc(s_wdus, 1)
                            gcol = gts[:, ts * 8 + e: ts * 8 + e + 1]
                            if e == 0:
                                ctx.inc(v.tensor_scalar(
                                    acc[:, ts * C:(ts + 1) * C],
                                    psALL[:, tsl * 1024:(tsl + 1) * 1024],
                                    gcol, None, ALU.mult))
                            else:
                                ctx.inc(v.scalar_tensor_tensor(
                                    acc[:, ts * C:(ts + 1) * C],
                                    psALL[:, tsl * 1024:(tsl + 1) * 1024],
                                    gcol, acc[:, ts * C:(ts + 1) * C],
                                    ALU.mult, ALU.add))
                    ts0 += nts
                    tsg_i += 1

                if "dbgC" in d and tg == 0 and e == 0:
                    @block.sync
                    def _(sy):
                        ctx.gate(sy)
                        for tt2 in range(TGT):
                            ctx.inc(sy.dma_start(
                                out=d["dbgC4"][tt2 * 128:(tt2 + 1) * 128, :],
                                in_=acc[:, tt2 * C:(tt2 + 1) * C]), 16)

            if "dbgC" in d and tg == 0:
                @block.sync
                def _(sy):
                    ctx.gate(sy)
                    for tt2 in range(TGT):
                        ctx.inc(sy.dma_start(
                            out=d["dbgC2"][tt2 * 128:(tt2 + 1) * 128, :],
                            in_=acc[:, tt2 * C:(tt2 + 1) * C]), 16)

                @block.gpsimd
                def _(g):
                    ctx.gate(g)
                    ctx.inc(g.dma_start(out=d["dbgC3"],
                                        in_=hbuf[:, 0:8 * TGT * 128]), 16)

            for tt in range(TGT):
                t = tg * TGT + tt

                @block.sync
                def _(sy, t=t):
                    ctx.gate(sy)
                    ctx.inc(sy.dma_start(out=x1t[:], in_=d["x1"][t * 128:(t + 1) * 128, :]), 16)
                    ctx.inc(sy.dma_start(out=at[:], in_=d["a"][t * 128:(t + 1) * 128, :]), 16)

                @block.vector
                def _(v, tt=tt):
                    ctx.gate(v)
                    ctx.inc(v.tensor_tensor(x1t[:], x1t[:], at[:], ALU.add))
                    ctx.gate(v)
                    ctx.inc(v.tensor_tensor(x1t[:], x1t[:], acc[:, tt * C:(tt + 1) * C],
                                            ALU.add))

                @block.sync
                def _(sy, t=t):
                    ctx.gate(sy)
                    ctx.inc(sy.dma_start(out=d["x2"][t * 128:(t + 1) * 128, :],
                                         in_=x1t[:]), 16)


def prep_stageC_inputs(S, norm2_w, router_w, wg, wu, wd):
    f = np.float32
    return {
        "n2w_r": np.ascontiguousarray(np.broadcast_to(np.asarray(norm2_w, f), (128, C))),
        "rwT": np.ascontiguousarray(np.asarray(router_w, f).T),
        "wgT": np.ascontiguousarray(np.asarray(wg, f).transpose(0, 2, 1)),
        "wuT": np.ascontiguousarray(np.asarray(wu, f).transpose(0, 2, 1)),
        "wdT": np.ascontiguousarray(np.asarray(wd, f).transpose(0, 2, 1)),
        "idn_c": np.eye(128).astype(ml_dtypes.bfloat16),
        "idn_f": np.eye(128, dtype=np.float32),
    }


def declare_stageC(nc, S, kind_in="ExternalInput", kind_out="ExternalOutput",
                   act_kind=None):
    HF = 4096
    return {
        "x1": nc.dram_tensor("x1c", [S, C], F32, kind=act_kind or kind_in).ap(),
        "a": nc.dram_tensor("a", [S, C], F32, kind=act_kind or kind_in).ap(),
        "n2w_r": nc.dram_tensor("n2w_r", [128, C], F32, kind=kind_in).ap(),
        "rwT": nc.dram_tensor("rwT", [C, 8], F32, kind=kind_in).ap(),
        "wgT": nc.dram_tensor("wgT", [8, C, HF], F32, kind=kind_in).ap(),
        "wuT": nc.dram_tensor("wuT", [8, C, HF], F32, kind=kind_in).ap(),
        "wdT": nc.dram_tensor("wdT", [8, HF, C], F32, kind=kind_in).ap(),
        "idn": nc.dram_tensor("idn_c", [128, 128], BF16, kind=kind_in).ap(),
        "idn_f": nc.dram_tensor("idn_f", [128, 128], F32, kind=kind_in).ap(),
        "wgT_bf": nc.dram_tensor("wgT_bf", [8, C, HF], BF16, kind="Internal").ap(),
        "wuT_bf": nc.dram_tensor("wuT_bf", [8, C, HF], BF16, kind="Internal").ap(),
        "wdT_bf": nc.dram_tensor("wdT_bf", [8, HF, C], BF16, kind="Internal").ap(),
        "x2": nc.dram_tensor("x2", [S, C], F32, kind=kind_out).ap(),
    }


def build_stageC(S):
    nc = bass.Bass("TRN2", target_bir_lowering=False)
    d = declare_stageC(nc, S)
    with ExitStack() as es:
        ctx = Ctx(nc, es)
        emit_stageC(ctx, S, d)

        @ctx.block.sync
        def _(sy):
            ctx.gate(sy)
    return nc


def emit_stageD(ctx, S, d):
    nc = ctx.nc
    block = ctx.block
    T = S // 128
    H = 16

    es = ExitStack()
    with es:
        wqkv = es.enter_context(nc.sbuf_tensor("D_wqkv", [128, C8 * 3 * C], BF16))
        wo = es.enter_context(nc.sbuf_tensor("D_wo", [128, C8 * C], BF16))
        Qbuf = es.enter_context(nc.sbuf_tensor("D_Q", [128, T * C], BF16))
        x2t = es.enter_context(nc.sbuf_tensor("D_x2t", [128, C], F32))
        lnb = es.enter_context(nc.sbuf_tensor("D_lnb", [128, C], BF16))
        lnT = es.enter_context(nc.sbuf_tensor("D_lnT", [128, C], BF16))
        trash = es.enter_context(nc.sbuf_tensor("D_tr", [128, C], F32))
        tmpf = es.enter_context(nc.sbuf_tensor("D_tmpf", [128, C], F32))
        k1 = es.enter_context(nc.sbuf_tensor("D_k1", [128, C], BF16))
        v1 = es.enter_context(nc.sbuf_tensor("D_v1", [128, C], BF16))
        kvb = es.enter_context(nc.sbuf_tensor("D_kv", [64, C], BF16))
        ksb = es.enter_context(nc.sbuf_tensor("D_ks", [64, 16], BF16))
        qT = es.enter_context(nc.sbuf_tensor("D_qT", [64, H * 128], BF16))
        y1n = es.enter_context(nc.sbuf_tensor("D_y1n", [128, C], BF16))
        y1t = es.enter_context(nc.sbuf_tensor("D_y1t", [128, C], BF16))
        nrm = es.enter_context(nc.sbuf_tensor("D_nrm", [128, 32], F32))
        cols = es.enter_context(nc.sbuf_tensor("D_cols", [128, 16], F32))
        ones_c = es.enter_context(nc.sbuf_tensor("D_ones", [128, 1], BF16))
        idn = es.enter_context(nc.sbuf_tensor("D_idn", [128, 128], BF16))
        psALL = es.enter_context(nc.psum_tensor("D_psA", [128, 2048], F32))
        psKV = es.enter_context(nc.psum_tensor("D_psKV", [64, 1536], F32))
        psT = es.enter_context(nc.psum_tensor("D_psT", [128, 1024], BF16))

        @block.sync
        def _(sy):
            ctx.gate(sy)
            ctx.inc(sy.dma_start(out=idn[:], in_=d["idn"]), 16)

        @block.gpsimd
        def _(g):
            ctx.gate(g)
            dcast_load_w(ctx, g, wqkv, d["Wqkv"], dout=3 * C)
            dcast_load_w(ctx, g, wo, d["Wo"])
            ctx.inc(g.memset(ones_c[:], 1.0))

        for t in range(T):
            @block.sync
            def _(sy, t=t):
                ctx.gate(sy)
                ctx.inc(sy.dma_start(out=x2t[:], in_=d["x2"][t * 128:(t + 1) * 128, :]), 16)

            emit_ln(ctx, x2t[:], lnb[:], trash, cols)
            emit_transposes3(ctx, [lnb[:, c * 128:(c + 1) * 128] for c in range(C8)],
                             psT,
                             lnT.rearrange("p (c x) -> p c x", c=C8),
                             idn, evac="vector")

            @block.tensor
            def _(te):
                ctx.gate(te)
                ins = None
                for qb in range(4):
                    for ci in range(C8):
                        ins = te.matmul(psALL[:, qb * 512:(qb + 1) * 512],
                                        lnT[:, ci * 128:(ci + 1) * 128],
                                        wqkv[:, ci * 3 * C + qb * 512: ci * 3 * C + (qb + 1) * 512],
                                        start=(ci == 0), stop=(ci == C8 - 1))
                ctx.inc(ins)

            for part, dst in ((0, None), (1, None)):
                @block.vector
                def _(v, part=part):
                    ctx.gate(v)
                    ctx.inc(v.tensor_scalar(tmpf[:], psALL[:, part * C:(part + 1) * C],
                                            0.0, None, ALU.min))

                @block.scalar
                def _(s):
                    ctx.gate(s)
                    ctx.inc(s.activation(trash[:], tmpf[:], AF.Exp))

                @block.vector
                def _(v, part=part, t=t):
                    ctx.gate(v)
                    out = Qbuf[:, t * C:(t + 1) * C] if part == 0 else k1[:]
                    ctx.inc(v.scalar_tensor_tensor(out, psALL[:, part * C:(part + 1) * C],
                                                   0.0, trash[:], ALU.max, ALU.add))

            @block.tensor
            def _(te):
                ctx.gate(te)
                ins = None
                for qb in range(2):
                    for ci in range(C8):
                        ins = te.matmul(psALL[:, qb * 512:(qb + 1) * 512],
                                        lnT[:, ci * 128:(ci + 1) * 128],
                                        wqkv[:, ci * 3 * C + 2 * C + qb * 512: ci * 3 * C + 2 * C + (qb + 1) * 512],
                                        start=(ci == 0), stop=(ci == C8 - 1))
                ctx.inc(ins)

            @block.scalar
            def _(s):
                ctx.gate(s)
                ctx.inc(s.copy(v1[:], psALL[:, 0:C]))

            @block.tensor
            def _(te, t=t):
                ctx.gate(te)
                ins = None
                for h in range(H):
                    ins = te.matmul(psKV[0:64, h * 64:(h + 1) * 64],
                                    k1[:, h * 64:(h + 1) * 64],
                                    v1[:, h * 64:(h + 1) * 64],
                                    start=(t == 0 and h % 8 == 0),
                                    stop=(t == T - 1))
                for h in range(H):
                    ins = te.matmul(psKV[0:64, 1024 + h:1025 + h],
                                    k1[:, h * 64:(h + 1) * 64],
                                    ones_c[:],
                                    start=(t == 0 and h == 0), stop=(t == T - 1))
                ctx.inc(ins)

        @block.scalar
        def _(s):
            ctx.gate(s)
            ctx.inc(s.copy(kvb[:], psKV[0:64, 0:C]))
            ctx.gate(s)
            ctx.inc(s.copy(ksb[:], psKV[0:64, 1024:1040]))

        if "dbgD" in d:
            @block.gpsimd
            def _(g):
                ctx.gate(g)
                ctx.inc(g.dma_start(out=d["dbgD"][0:64, 0:C], in_=kvb[:]), 16)
                ctx.inc(g.dma_start(out=d["dbgD"][64:128, 0:16], in_=ksb[:]), 16)
                ctx.inc(g.dma_start(out=d["dbgD"][64:128, 16:16 + C],
                                    in_=Qbuf[:, 0:C][64:128, :]), 16)

        for t in range(T):
            emit_transposes3(ctx,
                             [Qbuf[:, t * C + h * 64: t * C + (h + 1) * 64] for h in range(H)],
                             psT,
                             qT[0:64, :].rearrange("q (h x) -> q h x", x=128),
                             idn, evac="vector")

            @block.tensor
            def _(te):
                ctx.gate(te)
                ins = None
                for h in range(H):
                    ins = te.matmul(psALL[:, h * 64:(h + 1) * 64],
                                    qT[0:64, h * 128:(h + 1) * 128],
                                    kvb[0:64, h * 64:(h + 1) * 64],
                                    start=True, stop=True)
                for h in range(H):
                    ins = te.matmul(psALL[:, 1024 + h:1025 + h],
                                    qT[0:64, h * 128:(h + 1) * 128],
                                    ksb[0:64, h:h + 1],
                                    start=True, stop=True)
                ctx.inc(ins)

            @block.vector
            def _(v):
                ctx.gate(v)
                ctx.inc(v.tensor_scalar(nrm[:, 0:16], psALL[:, 1024:1040], 1e-6,
                                        None, ALU.add))
                ctx.gate(v)
                ctx.inc(v.reciprocal(nrm[:, 16:32], nrm[:, 0:16]))

            for h in range(H):
                @block.vector
                def _(v, h=h):
                    ctx.gate(v)
                    ctx.inc(v.tensor_scalar(y1n[:, h * 64:(h + 1) * 64],
                                            psALL[:, h * 64:(h + 1) * 64],
                                            nrm[:, 16 + h:17 + h], None, ALU.mult))

            emit_transposes3(ctx, [y1n[:, c * 128:(c + 1) * 128] for c in range(C8)],
                             psT,
                             y1t.rearrange("p (c x) -> p c x", c=C8),
                             idn, evac="vector")

            @block.sync
            def _(sy, t=t):
                ctx.gate(sy)
                ctx.inc(sy.dma_start(out=x2t[:], in_=d["x2"][t * 128:(t + 1) * 128, :]), 16)

            for ob in range(2):
                @block.tensor
                def _(te, ob=ob):
                    ctx.gate(te)
                    ins = None
                    for ci in range(C8):
                        ins = te.matmul(psALL[:, 1536:2048],
                                        y1t[:, ci * 128:(ci + 1) * 128],
                                        wo[:, ci * C + ob * 512: ci * C + (ob + 1) * 512],
                                        start=(ci == 0), stop=(ci == C8 - 1))
                    ctx.inc(ins)

                @block.vector
                def _(v, ob=ob):
                    ctx.gate(v)
                    ctx.inc(v.scalar_tensor_tensor(
                        trash[:, ob * 512:(ob + 1) * 512], psALL[:, 1536:2048], 1.0,
                        x2t[:, ob * 512:(ob + 1) * 512], ALU.mult, ALU.add))

            @block.sync
            def _(sy, t=t):
                ctx.gate(sy)
                ctx.inc(sy.dma_start(out=d["x3"][t * 128:(t + 1) * 128, :],
                                     in_=trash[:]), 16)


def declare_stageD(nc, S, kind_in="ExternalInput", kind_out="ExternalOutput",
                   act_kind=None):
    return {
        "x2": nc.dram_tensor("x2d", [S, C], F32, kind=act_kind or kind_in).ap(),
        "Wqkv": nc.dram_tensor("Wqkv", [C, 3 * C], F32, kind=kind_in).ap(),
        "Wo": nc.dram_tensor("Wo", [C, C], F32, kind=kind_in).ap(),
        "idn": nc.dram_tensor("idn_d", [128, 128], BF16, kind=kind_in).ap(),
        "x3": nc.dram_tensor("x3", [S, C], F32, kind=kind_out).ap(),
    }


def build_stageD(S):
    nc = bass.Bass("TRN2", target_bir_lowering=False)
    d = declare_stageD(nc, S)
    with ExitStack() as es:
        ctx = Ctx(nc, es)
        emit_stageD(ctx, S, d)

        @ctx.block.sync
        def _(sy):
            ctx.gate(sy)
    return nc


def emit_stageE(ctx, S, d):
    nc = ctx.nc
    block = ctx.block
    T = S // 128

    es = ExitStack()
    with es:
        wr = es.enter_context(nc.sbuf_tensor("E_wr", [128, C8 * C], BF16))
        wk = es.enter_context(nc.sbuf_tensor("E_wk", [128, C8 * C], BF16))
        wv = es.enter_context(nc.sbuf_tensor("E_wv", [128, C8 * C], BF16))
        mur = es.enter_context(nc.sbuf_tensor("E_mur", [128, C], F32))
        muk = es.enter_context(nc.sbuf_tensor("E_muk", [128, C], F32))
        mur1 = es.enter_context(nc.sbuf_tensor("E_mur1", [128, C], F32))
        muk1 = es.enter_context(nc.sbuf_tensor("E_muk1", [128, C], F32))
        x3t = es.enter_context(nc.sbuf_tensor("E_x3t", [128, C], F32))
        x3p = es.enter_context(nc.sbuf_tensor("E_x3p", [128, C], F32))
        ln3 = es.enter_context(nc.sbuf_tensor("E_ln3", [128, C], F32))
        ln3p = es.enter_context(nc.sbuf_tensor("E_ln3p", [128, C], F32))
        trash = es.enter_context(nc.sbuf_tensor("E_tr", [128, C], F32))
        tmpf = es.enter_context(nc.sbuf_tensor("E_tmpf", [128, C], F32))
        xb = es.enter_context(nc.sbuf_tensor("E_xb", [128, C], BF16))
        xbT = es.enter_context(nc.sbuf_tensor("E_xbT", [128, C], BF16))
        rs = es.enter_context(nc.sbuf_tensor("E_rs", [128, C], F32))
        cols = es.enter_context(nc.sbuf_tensor("E_cols", [128, 16], F32))
        idn = es.enter_context(nc.sbuf_tensor("E_idn", [128, 128], BF16))
        psM = es.enter_context(nc.psum_tensor("E_psM", [128, 1024], F32))
        psT = es.enter_context(nc.psum_tensor("E_psT", [128, 1024], BF16))

        @block.gpsimd
        def _(g):
            ctx.gate(g)
            dcast_load_w(ctx, g, wr, d["WrT"])
            dcast_load_w(ctx, g, wk, d["WkT"])
            dcast_load_w(ctx, g, wv, d["WvT"])

        @block.sync
        def _(sy):
            ctx.gate(sy)
            ctx.inc(sy.dma_start(out=idn[:], in_=d["idn"]), 16)
            ctx.inc(sy.dma_start(out=mur[:], in_=d["mu_r_r"]), 16)
            ctx.inc(sy.dma_start(out=muk[:], in_=d["mu_k_r"]), 16)

        @block.vector
        def _(v):
            ctx.gate(v)
            v.tensor_scalar(mur1[:], mur[:], -1.0, 1.0, ALU.mult, ALU.add)
            ctx.inc(v.tensor_scalar(muk1[:], muk[:], -1.0, 1.0, ALU.mult, ALU.add))

        for t in range(T):
            @block.sync
            def _(sy, t=t):
                ctx.gate(sy)
                ctx.inc(sy.dma_start(out=x3t[:], in_=d["x3"][t * 128:(t + 1) * 128, :]), 16)
                if t == 0:
                    ctx.inc(sy.dma_start(out=x3p[1:128, :], in_=d["x3"][0:127, :]), 16)
                else:
                    ctx.inc(sy.dma_start(out=x3p[:],
                                         in_=d["x3"][t * 128 - 1: (t + 1) * 128 - 1, :]), 16)

            if t == 0:
                @block.vector
                def _(v):
                    ctx.gate(v)
                    ctx.inc(v.memset(x3p[0:1, :], 0.0))

            emit_ln(ctx, x3t[:], ln3[:], trash, cols)
            emit_ln(ctx, x3p[:], ln3p[:], trash, cols)

            def mix_project(mu, mu1, w_sb, label):
                @block.vector
                def _(v, mu=mu, mu1=mu1):
                    ctx.gate(v)
                    ctx.inc(v.tensor_tensor(trash[:], ln3[:], mu, ALU.mult))
                    ctx.gate(v)
                    ctx.inc(v.tensor_tensor(tmpf[:], ln3p[:], mu1, ALU.mult))
                    ctx.gate(v)
                    ctx.inc(v.tensor_tensor(xb[:], trash[:], tmpf[:], ALU.add))
                emit_transposes3(ctx, [xb[:, c * 128:(c + 1) * 128] for c in range(C8)],
                                 psT,
                                 xbT.rearrange("p (c x) -> p c x", c=C8),
                                 idn, evac="vector")

                @block.tensor
                def _(te, w_sb=w_sb):
                    ctx.gate(te)
                    ins = None
                    for ob in range(2):
                        for ci in range(C8):
                            ins = te.matmul(psM[:, ob * 512:(ob + 1) * 512],
                                            xbT[:, ci * 128:(ci + 1) * 128],
                                            w_sb[:, ci * C + ob * 512: ci * C + (ob + 1) * 512],
                                            start=(ci == 0), stop=(ci == C8 - 1))
                    ctx.inc(ins)

            mix_project(mur[:], mur1[:], wr, "r")

            @block.scalar
            def _(s):
                ctx.gate(s)
                ctx.inc(s.activation(rs[:], psM[:, 0:C], AF.Sigmoid))

            mix_project(muk[:], muk1[:], wk, "k")

            @block.vector
            def _(v):
                ctx.gate(v)
                ctx.inc(v.tensor_scalar(tmpf[:], psM[:, 0:C], 0.0, None, ALU.max))

            @block.scalar
            def _(s):
                ctx.gate(s)
                ctx.inc(s.activation(xb[:], tmpf[:], AF.Square))

            emit_transposes3(ctx, [xb[:, c * 128:(c + 1) * 128] for c in range(C8)],
                             psT,
                             xbT.rearrange("p (c x) -> p c x", c=C8),
                             idn, evac="vector")

            @block.tensor
            def _(te):
                ctx.gate(te)
                ins = None
                for ob in range(2):
                    for ci in range(C8):
                        ins = te.matmul(psM[:, ob * 512:(ob + 1) * 512],
                                        xbT[:, ci * 128:(ci + 1) * 128],
                                        wv[:, ci * C + ob * 512: ci * C + (ob + 1) * 512],
                                        start=(ci == 0), stop=(ci == C8 - 1))
                ctx.inc(ins)

            @block.vector
            def _(v):
                ctx.gate(v)
                ctx.inc(v.tensor_tensor(tmpf[:], rs[:], psM[:, 0:C], ALU.mult))
                ctx.gate(v)
                ctx.inc(v.tensor_tensor(tmpf[:], tmpf[:], x3t[:], ALU.add))

            @block.sync
            def _(sy, t=t):
                ctx.gate(sy)
                ctx.inc(sy.dma_start(out=d["y"][t * 128:(t + 1) * 128, :],
                                     in_=tmpf[:]), 16)


def declare_stageE(nc, S, kind_in="ExternalInput", kind_out="ExternalOutput",
                   act_kind=None):
    return {
        "x3": nc.dram_tensor("x3e", [S, C], F32, kind=act_kind or kind_in).ap(),
        "WrT": nc.dram_tensor("WrT", [C, C], F32, kind=kind_in).ap(),
        "WkT": nc.dram_tensor("WkT", [C, C], F32, kind=kind_in).ap(),
        "WvT": nc.dram_tensor("WvT", [C, C], F32, kind=kind_in).ap(),
        "mu_r_r": nc.dram_tensor("mu_r_r", [128, C], F32, kind=kind_in).ap(),
        "mu_k_r": nc.dram_tensor("mu_k_r", [128, C], F32, kind=kind_in).ap(),
        "idn": nc.dram_tensor("idn_e", [128, 128], BF16, kind=kind_in).ap(),
        "y": nc.dram_tensor("y", [S, C], F32, kind=kind_out).ap(),
    }


def prep_stageDE_inputs(Wqkv, Wo, W_r, W_k, W_v, mu_r, mu_k):
    f = np.float32
    return {
        "Wqkv": np.ascontiguousarray(np.asarray(Wqkv, f)),
        "Wo": np.ascontiguousarray(np.asarray(Wo, f)),
        "idn_d": np.eye(128).astype(ml_dtypes.bfloat16),
        "WrT": np.ascontiguousarray(np.asarray(W_r, f).T),
        "WkT": np.ascontiguousarray(np.asarray(W_k, f).T),
        "WvT": np.ascontiguousarray(np.asarray(W_v, f).T),
        "mu_r_r": np.ascontiguousarray(np.broadcast_to(np.asarray(mu_r, f), (128, C))),
        "mu_k_r": np.ascontiguousarray(np.broadcast_to(np.asarray(mu_k, f), (128, C))),
        "idn_e": np.eye(128).astype(ml_dtypes.bfloat16),
    }


def build_stageE(S):
    nc = bass.Bass("TRN2", target_bir_lowering=False)
    d = declare_stageE(nc, S)
    with ExitStack() as es:
        ctx = Ctx(nc, es)
        emit_stageE(ctx, S, d)

        @ctx.block.sync
        def _(sy):
            ctx.gate(sy)
    return nc


def declare_fused(nc, S):
    HF = 4096
    I = "ExternalInput"
    x1s = nc.dram_tensor("x1", [S, C], F32, kind="Internal").ap()
    a_s = nc.dram_tensor("a", [S, C], F32, kind="Internal").ap()
    x2s = nc.dram_tensor("x2", [S, C], F32, kind="Internal").ap()
    x3s = nc.dram_tensor("x3", [S, C], F32, kind="Internal").ap()
    y = nc.dram_tensor("y", [S, C], F32, kind="ExternalOutput").ap()
    dA = {
        "x0": nc.dram_tensor("x0", [S, C], F32, kind=I).ap(),
        "wq": nc.dram_tensor("wq", [C, C], F32, kind=I).ap(),
        "wk": nc.dram_tensor("wk", [C, C], F32, kind=I).ap(),
        "wv": nc.dram_tensor("wv", [C, C], F32, kind=I).ap(),
        "wo": nc.dram_tensor("wo", [C, C], F32, kind=I).ap(),
        "bq_pc": nc.dram_tensor("bq_pc", [128, 8], F32, kind=I).ap(),
        "bk_pc": nc.dram_tensor("bk_pc", [128, 8], F32, kind=I).ap(),
        "bv_row": nc.dram_tensor("bv_row", [1, C], F32, kind=I).ap(),
        "bo_row": nc.dram_tensor("bo_row", [1, C], F32, kind=I).ap(),
        "idn": nc.dram_tensor("idn", [128, 128], BF16, kind=I).ap(),
        "dmA": nc.dram_tensor("dmA", [128, 256], F32, kind=I).ap(),
        "x1": x1s,
    }
    dB = {
        "x1": x1s,
        "qkvT": nc.dram_tensor("qkvT", [C, 3 * C], F32, kind=I).ap(),
        "projT": nc.dram_tensor("projT", [C, C], F32, kind=I).ap(),
        "n1w_r": nc.dram_tensor("n1w_r", [128, C], F32, kind=I).ap(),
        "cosb": nc.dram_tensor("cosb", [S, C], BF16, kind=I).ap(),
        "sinb": nc.dram_tensor("sinb", [S, C], BF16, kind=I).ap(),
        "idn": nc.dram_tensor("idn_b", [128, 128], BF16, kind=I).ap(),
        "dmB": nc.dram_tensor("dmB", [128, 128], F32, kind=I).ap(),
        "dmB2": nc.dram_tensor("dmB2", [128, 512], F32, kind=I).ap(),
        "a": a_s,
    }
    dC = {
        "x1": x1s,
        "a": a_s,
        "n2w_r": nc.dram_tensor("n2w_r", [128, C], F32, kind=I).ap(),
        "rwT": nc.dram_tensor("rwT", [C, 8], F32, kind=I).ap(),
        "wgT": nc.dram_tensor("wgT", [8, C, HF], F32, kind=I).ap(),
        "wuT": nc.dram_tensor("wuT", [8, C, HF], F32, kind=I).ap(),
        "wdT": nc.dram_tensor("wdT", [8, HF, C], F32, kind=I).ap(),
        "idn": nc.dram_tensor("idn_c", [128, 128], BF16, kind=I).ap(),
        "idn_f": nc.dram_tensor("idn_f", [128, 128], F32, kind=I).ap(),
        "wgT_bf": nc.dram_tensor("wgT_bf", [8, C, HF], BF16, kind="Internal").ap(),
        "wuT_bf": nc.dram_tensor("wuT_bf", [8, C, HF], BF16, kind="Internal").ap(),
        "wdT_bf": nc.dram_tensor("wdT_bf", [8, HF, C], BF16, kind="Internal").ap(),
        "x2": x2s,
    }
    dD = {
        "x2": x2s,
        "Wqkv": nc.dram_tensor("Wqkv", [C, 3 * C], F32, kind=I).ap(),
        "Wo": nc.dram_tensor("Wo", [C, C], F32, kind=I).ap(),
        "idn": nc.dram_tensor("idn_d", [128, 128], BF16, kind=I).ap(),
        "x3": x3s,
    }
    dE = {
        "x3": x3s,
        "WrT": nc.dram_tensor("WrT", [C, C], F32, kind=I).ap(),
        "WkT": nc.dram_tensor("WkT", [C, C], F32, kind=I).ap(),
        "WvT": nc.dram_tensor("WvT", [C, C], F32, kind=I).ap(),
        "mu_r_r": nc.dram_tensor("mu_r_r", [128, C], F32, kind=I).ap(),
        "mu_k_r": nc.dram_tensor("mu_k_r", [128, C], F32, kind=I).ap(),
        "idn": nc.dram_tensor("idn_e", [128, 128], BF16, kind=I).ap(),
        "y": y,
    }
    return dA, dB, dC, dD, dE


class StageCtx:

    def __init__(self, nc, block, sem, prev=None):
        self.nc = nc
        self.block = block
        self.chain = sem
        self.n = 0
        if prev is not None:
            psem, pval = prev
            for name in ("sync", "gpsimd", "tensor", "vector", "scalar"):
                @getattr(block, name)
                def _(eng, psem=psem, pval=pval):
                    eng.wait_ge(psem, pval)

    def inc(self, instr, k=1):
        instr.then_inc(self.chain, k)
        self.n += k
        return instr

    def gate(self, eng, val=None):
        eng.wait_ge(self.chain, self.n if val is None else val)


def build_fused(S):
    nc = bass.Bass("TRN2", target_bir_lowering=False)
    dA, dB, dC, dD, dE = declare_fused(nc, S)
    with ExitStack() as es:
        sems = [es.enter_context(nc.semaphore(f"chain{i}")) for i in range(5)]
        s_cast = es.enter_context(nc.semaphore("s_cast"))
        block = es.enter_context(nc.Block())

        @block.gpsimd
        def _(g):
            for e in range(8):
                g.dma_start(out=dC["wgT_bf"][e], in_=dC["wgT"][e]).then_inc(s_cast, 16)
                g.dma_start(out=dC["wuT_bf"][e], in_=dC["wuT"][e]).then_inc(s_cast, 16)
                g.dma_start(out=dC["wdT_bf"][e], in_=dC["wdT"][e]).then_inc(s_cast, 16)
        dC["cast_done"] = (s_cast, 384)
        prev = None
        for i, (emit, d) in enumerate(((emit_stageA, dA), (emit_stageB, dB),
                                       (emit_stageC, dC), (emit_stageD, dD),
                                       (emit_stageE, dE))):
            ctx = StageCtx(nc, block, sems[i], prev=prev)
            emit(ctx, S, d)
            prev = (sems[i], ctx.n)

        @block.sync
        def _(sy):
            sy.wait_ge(prev[0], prev[1])
    return nc


from concourse.bass_utils import run_bass_kernel_spmd

S_FULL = 4096
N_CORES = 8


def kernel(**inputs):
    f = np.float32
    x0 = np.ascontiguousarray(np.asarray(inputs["L0_attn_x"], f))
    B = x0.shape[0]
    nc = build_fused(S_FULL)

    sh = {}
    sh.update(prep_stageA_inputs(
        S_FULL, np.zeros((1, 1), f),
        inputs["L0_attn_wq"], inputs["L0_attn_bq"],
        inputs["L0_attn_wk"], inputs["L0_attn_bk"],
        inputs["L0_attn_wv"], inputs["L0_attn_bv"],
        inputs["L0_attn_wo"], inputs["L0_attn_bo"]))
    del sh["x0"]
    sh.update(prep_stageB_inputs(S_FULL, inputs["L0_ffn_qkv_w"],
                                 inputs["L0_ffn_proj_w"], inputs["L0_ffn_norm1_w"]))
    sh.update(prep_stageC_inputs(S_FULL, inputs["L0_ffn_norm2_w"],
                                 inputs["L0_ffn_router_w"], inputs["L0_ffn_wg"],
                                 inputs["L0_ffn_wu"], inputs["L0_ffn_wd"]))
    sh.update(prep_stageDE_inputs(inputs["L1_attn_Wqkv"], inputs["L1_attn_Wo"],
                                  inputs["L1_ffn_W_r"], inputs["L1_ffn_W_k"],
                                  inputs["L1_ffn_W_v"], inputs["L1_ffn_mu_r"],
                                  inputs["L1_ffn_mu_k"]))
    maps = []
    for c in range(N_CORES):
        m = dict(sh)
        m["x0"] = np.ascontiguousarray(x0[c // 2])
        maps.append(m)
    res = run_bass_kernel_spmd(nc, maps, core_ids=list(range(N_CORES)))
    out = np.stack([np.asarray(res.results[2 * b]["y"]) for b in range(B)])
    return out.astype(np.float32)


# version 08 (final)
# speedup vs baseline: 1.1591x; 1.1591x over previous
import numpy as np
import ml_dtypes
from contextlib import ExitStack

import concourse.bass as bass
import concourse.mybir as mybir
from concourse.bass_core_rust import IndirectOffsetOnAxis

F32 = mybir.dt.float32
BF16 = mybir.dt.bfloat16
I32 = mybir.dt.int32
AF = mybir.ActivationFunctionType
ALU = mybir.AluOpType
AX = mybir.AxisListType

C = 1024
C8 = 8


class Ctx:
    def __init__(self, nc, es):
        self.nc = nc
        self.es = es
        self.chain = es.enter_context(nc.semaphore("chain"))
        self.n = 0
        self.block = es.enter_context(nc.Block())

    def inc(self, instr, k=1):
        instr.then_inc(self.chain, k)
        self.n += k
        return instr

    def gate(self, eng, val=None):
        eng.wait_ge(self.chain, self.n if val is None else val)


def dcast_load(ctx, g, dst_sb, src_dram_ap):
    ctx.inc(g.dma_start(out=dst_sb, in_=src_dram_ap), 16)


def dcast_load_w(ctx, g, dst_sb, w_dram, dout=C):
    src = w_dram.rearrange("(c p) o -> p c o", p=128)
    dst = dst_sb.rearrange("p (c o) -> p c o", c=C8)
    ctx.inc(g.dma_start(out=dst, in_=src), 16)


def emit_ln(ctx, x_view, out_view, trash, cols, eps=1e-5, nfree=C):
    block = ctx.block

    @block.scalar
    def _(s):
        ctx.gate(s)
        s.activation(trash[:, 0:nfree], x_view, AF.Identity, accum_out=cols[:, 0:1])
        ctx.inc(s.activation(trash[:, 0:nfree], x_view, AF.Square, accum_out=cols[:, 1:2]))

    @block.vector
    def _(v):
        ctx.gate(v)
        ctx.inc(v.tensor_scalar(cols[:, 2:3], cols[:, 0:1], -1.0 / nfree, None, ALU.mult))
        ctx.gate(v)
        ctx.inc(v.tensor_tensor(cols[:, 3:4], cols[:, 2:3], cols[:, 2:3], ALU.mult))
        ctx.gate(v)
        ctx.inc(v.scalar_tensor_tensor(cols[:, 4:5], cols[:, 1:2], 1.0 / nfree,
                                       cols[:, 3:4], ALU.mult, ALU.subtract))
        ctx.gate(v)
        ctx.inc(v.tensor_scalar(cols[:, 4:5], cols[:, 4:5], eps, None, ALU.add))

    @block.scalar
    def _(s):
        ctx.gate(s)
        ctx.inc(s.activation(cols[:, 5:6], cols[:, 4:5], AF.Ln))
        ctx.gate(s)
        ctx.inc(s.activation(cols[:, 5:6], cols[:, 5:6], AF.Exp, scale=-0.5))

    @block.vector
    def _(v):
        ctx.gate(v)
        ctx.inc(v.tensor_tensor(cols[:, 6:7], cols[:, 2:3], cols[:, 5:6], ALU.mult))
        ctx.gate(v)
        ctx.inc(v.tensor_scalar(out_view, x_view, cols[:, 5:6], cols[:, 6:7],
                                ALU.mult, ALU.add))


def emit_transposes(ctx, src_views, psT, dst_views, idn, per_batch=4):
    block = ctx.block
    nb = len(src_views)
    i = 0
    while i < nb:
        batch = min(per_batch, nb - i)

        @block.tensor
        def _(te, i=i, batch=batch):
            ctx.gate(te)
            ins = None
            for j in range(batch):
                sv = src_views[i + j]
                p, f = sv.shape[0], sv.shape[1]
                ins = te.transpose(psT[0:f, j * 128: j * 128 + p], sv,
                                   idn[0:p, 0:p])
            ctx.inc(ins)

        @block.scalar
        def _(s, i=i, batch=batch):
            ctx.gate(s)
            last = None
            for j in range(batch):
                p = dst_views[i + j].shape[0]
                last = s.copy(dst_views[i + j], psT[0:p, j * 128:(j + 1) * 128])
            ctx.inc(last)
        i += batch


def emit_transposes3(ctx, src_views, psT, dst3, idn, per_batch=8, evac="scalar"):
    block = ctx.block
    nb = len(src_views)
    p = src_views[0].shape[0]
    f = src_views[0].shape[1]
    psv = psT[0:f, 0:per_batch * p].rearrange("q (b c) -> q b c", c=p)
    i = 0
    while i < nb:
        batch = min(per_batch, nb - i)

        @block.tensor
        def _(te, i=i, batch=batch):
            ctx.gate(te)
            ins = None
            for j in range(batch):
                ins = te.transpose(psT[0:f, j * p: (j + 1) * p], src_views[i + j],
                                   idn[0:p, 0:p])
            ctx.inc(ins)

        if evac == "scalar":
            @block.scalar
            def _(e, i=i, batch=batch):
                ctx.gate(e)
                ctx.inc(e.copy(dst3[:, i:i + batch, :], psv[:, 0:batch, :]))
        else:
            @block.vector
            def _(e, i=i, batch=batch):
                ctx.gate(e)
                ctx.inc(e.tensor_copy(dst3[:, i:i + batch, :], psv[:, 0:batch, :]))
        i += batch


def emit_stageA(ctx, S, d):
    nc = ctx.nc
    block = ctx.block
    T = S // 128
    NG = T // 4
    KB = S // 512
    assert T % 4 == 0 and S % 512 == 0

    es = ExitStack()
    with es:
        KT = es.enter_context(nc.sbuf_tensor("A_KT", [128, C8 * S], BF16))
        V = es.enter_context(nc.sbuf_tensor("A_V", [128, T * C], BF16))
        wA = es.enter_context(nc.sbuf_tensor("A_wA", [128, C8 * C], BF16))
        wB = es.enter_context(nc.sbuf_tensor("A_wB", [128, C8 * C], BF16))
        scores = es.enter_context(nc.sbuf_tensor("A_sc", [128, max(S, 2 * C)], F32))
        ebuf = es.enter_context(nc.sbuf_tensor("A_e", [128, max(S, C8 * 512)], BF16))
        QTg = es.enter_context(nc.sbuf_tensor("A_QTg", [128, C8 * 512], BF16))
        attnT = es.enter_context(nc.sbuf_tensor("A_aT", [128, S], BF16))
        x0t = scores[:, C:2 * C]
        lnb = es.enter_context(nc.sbuf_tensor("A_lnb", [128, C], BF16))
        trash = scores
        EB = max(S, C8 * 512)
        y1n = ebuf[:, EB - C:EB]
        y1t = ebuf[:, EB - 2 * C:EB - C]
        x1b = scores[:, 0:C]
        cols = es.enter_context(nc.sbuf_tensor("A_cols", [128, 16], F32))
        bqc = es.enter_context(nc.sbuf_tensor("A_bqc", [128, 8], F32))
        bkc = es.enter_context(nc.sbuf_tensor("A_bkc", [128, 8], F32))
        brow = es.enter_context(nc.sbuf_tensor("A_brow", [1, 2 * C], BF16))
        ones1 = es.enter_context(nc.sbuf_tensor("A_ones", [1, 128], BF16))
        idn = es.enter_context(nc.sbuf_tensor("A_idn", [128, 128], BF16))
        dmA = es.enter_context(nc.sbuf_tensor("A_dmA", [128, 256], F32))
        psS = es.enter_context(nc.psum_tensor("A_psS", [128, 2048], F32))
        psT = es.enter_context(nc.psum_tensor("A_psT", [128, 1024], BF16))
        psAV = es.enter_context(nc.psum_tensor("A_psAV", [128, 1024], F32))
        psX = es.enter_context(nc.psum_tensor("A_psX", [128, 512], F32))

        @block.sync
        def _(sy):
            ctx.gate(sy)
            ctx.inc(sy.dma_start(out=bqc[:], in_=d["bq_pc"]), 16)
            ctx.inc(sy.dma_start(out=bkc[:], in_=d["bk_pc"]), 16)
            ctx.inc(sy.dma_start(out=idn[:], in_=d["idn"]), 16)
            ctx.inc(sy.dma_start(out=dmA[:], in_=d["dmA"]), 16)

        @block.gpsimd
        def _(g):
            ctx.gate(g)
            dcast_load_w(ctx, g, wA, d["wk"])
            dcast_load_w(ctx, g, wB, d["wv"])
            dcast_load(ctx, g, brow[0:1, 0:C], d["bv_row"])
            dcast_load(ctx, g, brow[0:1, C:2 * C], d["bo_row"])
            ctx.inc(g.memset(ones1[:], 1.0))

        for g_i in range(NG):
            for j in range(4):
                t = 4 * g_i + j

                @block.sync
                def _(sy, t=t):
                    ctx.gate(sy)
                    ctx.inc(sy.dma_start(out=x0t, in_=d["x0"][t * 128:(t + 1) * 128, :]), 16)

                emit_ln(ctx, x0t, lnb[:], trash, cols)
                emit_transposes3(ctx, [lnb[:, c * 128:(c + 1) * 128] for c in range(C8)],
                                 psT,
                                 ebuf[:, 0:C8 * 512].rearrange("p (c t) -> p c t", c=C8)
                                 [:, :, j * 128:(j + 1) * 128],
                                 idn, evac="vector")

            for kd in range(C8):
                @block.tensor
                def _(te, kd=kd):
                    ctx.gate(te)
                    ins = None
                    for ci in range(C8):
                        ins = te.matmul(psX[:, 0:512],
                                        wA[:, ci * C + kd * 128: ci * C + (kd + 1) * 128],
                                        ebuf[:, ci * 512:(ci + 1) * 512],
                                        start=(ci == 0), stop=(ci == C8 - 1))
                    ctx.inc(ins)

                @block.scalar
                def _(s, kd=kd, g_i=g_i):
                    ctx.gate(s)
                    ctx.inc(s.activation(KT[:, kd * S + g_i * 512: kd * S + g_i * 512 + 512],
                                         psX[:, 0:512], AF.Identity,
                                         bias=bkc[:, kd:kd + 1]))

            for j in range(4):
                t = 4 * g_i + j
                for vb in range(2):
                    @block.tensor
                    def _(te, j=j, vb=vb):
                        ctx.gate(te)
                        for ci in range(C8):
                            te.matmul(psAV[:, vb * 512:(vb + 1) * 512],
                                      ebuf[:, ci * 512 + j * 128: ci * 512 + j * 128 + 128],
                                      wB[:, ci * C + vb * 512: ci * C + (vb + 1) * 512],
                                      start=(ci == 0), stop=False)
                        ins = te.matmul(psAV[:, vb * 512:(vb + 1) * 512],
                                        ones1[0:1, :],
                                        brow[0:1, vb * 512:(vb + 1) * 512],
                                        start=False, stop=True)
                        ctx.inc(ins)

                    @block.scalar
                    def _(s, t=t, vb=vb):
                        ctx.gate(s)
                        ctx.inc(s.copy(V[:, t * C + vb * 512: t * C + (vb + 1) * 512],
                                       psAV[:, vb * 512:(vb + 1) * 512]))

        @block.gpsimd
        def _(g):
            ctx.gate(g)
            dcast_load_w(ctx, g, wA, d["wq"])
            dcast_load_w(ctx, g, wB, d["wo"])

        for g_i in range(NG):
            for j in range(4):
                t = 4 * g_i + j

                @block.sync
                def _(sy, t=t):
                    ctx.gate(sy)
                    ctx.inc(sy.dma_start(out=x0t, in_=d["x0"][t * 128:(t + 1) * 128, :]), 16)

                emit_ln(ctx, x0t, lnb[:], trash, cols)
                emit_transposes3(ctx, [lnb[:, c * 128:(c + 1) * 128] for c in range(C8)],
                                 psT,
                                 ebuf[:, 0:C8 * 512].rearrange("p (c t) -> p c t", c=C8)
                                 [:, :, j * 128:(j + 1) * 128],
                                 idn, evac="vector")

            for kd in range(C8):
                @block.tensor
                def _(te, kd=kd):
                    ctx.gate(te)
                    ins = None
                    for ci in range(C8):
                        ins = te.matmul(psX[:, 0:512],
                                        wA[:, ci * C + kd * 128: ci * C + (kd + 1) * 128],
                                        ebuf[:, ci * 512:(ci + 1) * 512],
                                        start=(ci == 0), stop=(ci == C8 - 1))
                    ctx.inc(ins)

                @block.scalar
                def _(s, kd=kd):
                    ctx.gate(s)
                    ctx.inc(s.activation(QTg[:, kd * 512:(kd + 1) * 512], psX[:, 0:512],
                                         AF.Identity, bias=bqc[:, kd:kd + 1]))

            for j in range(4):
                t = 4 * g_i + j

                kb_lo = (t * 128) // 512 if t < T - 1 else 0
                kcl = kb_lo * 4
                for kb0 in range(kb_lo, KB, 4):
                    kbn = min(4, KB - kb0)

                    @block.tensor
                    def _(te, j=j, kb0=kb0, kbn=kbn):
                        ctx.gate(te)
                        ins = None
                        for kk in range(kbn):
                            kb = kb0 + kk
                            for kd in range(C8):
                                ins = te.matmul(
                                    psS[:, kk * 512:(kk + 1) * 512],
                                    QTg[:, kd * 512 + j * 128: kd * 512 + j * 128 + 128],
                                    KT[:, kd * S + kb * 512: kd * S + (kb + 1) * 512],
                                    start=(kd == 0), stop=(kd == C8 - 1))
                        ctx.inc(ins)

                    @block.scalar
                    def _(s, kb0=kb0, kbn=kbn):
                        ctx.gate(s)
                        ctx.inc(s.activation(scores[:, kb0 * 512: (kb0 + kbn) * 512],
                                             psS[:, 0:kbn * 512], AF.Identity,
                                             scale=1.0 / 32.0))

                if t * 128 > kb_lo * 512:
                    @block.gpsimd
                    def _(g, t=t, kb_lo=kb_lo):
                        ctx.gate(g)
                        ctx.inc(g.memset(scores[:, kb_lo * 512:t * 128], -1e30))
                dw = 256 if t < T - 1 else 128

                @block.vector
                def _(v, t=t, dw=dw, kb_lo=kb_lo):
                    ctx.gate(v)
                    ctx.inc(v.tensor_tensor(scores[:, t * 128:t * 128 + dw],
                                            scores[:, t * 128:t * 128 + dw],
                                            dmA[:, 0:dw], ALU.add))
                    ctx.gate(v)
                    ctx.inc(v.reduce_max(cols[:, 8:9], scores[:, kb_lo * 512:S],
                                         axis=AX.X, negate=True))

                @block.scalar
                def _(s, kb_lo=kb_lo):
                    ctx.gate(s)
                    ctx.inc(s.activation(ebuf[:, kb_lo * 512:S],
                                         scores[:, kb_lo * 512:S], AF.Exp,
                                         bias=cols[:, 8:9], scale=1.0,
                                         accum_out=cols[:, 9:10]))

                @block.vector
                def _(v):
                    ctx.gate(v)
                    ctx.inc(v.reciprocal(cols[:, 10:11], cols[:, 9:10]))

                emit_transposes3(ctx,
                                 [ebuf[:, kc * 128:(kc + 1) * 128] for kc in range(kcl, T)],
                                 psT,
                                 attnT[:, kcl * 128:S].rearrange("p (k c) -> p k c", c=128),
                                 idn, evac="vector")

                @block.tensor
                def _(te, kcl=kcl):
                    ctx.gate(te)
                    ins = None
                    for vb in range(2):
                        for kc in range(kcl, T):
                            ins = te.matmul(psAV[:, vb * 512:(vb + 1) * 512],
                                            attnT[:, kc * 128:(kc + 1) * 128],
                                            V[:, kc * C + vb * 512: kc * C + (vb + 1) * 512],
                                            start=(kc == kcl), stop=(kc == T - 1))
                    ctx.inc(ins)

                @block.vector
                def _(v):
                    ctx.gate(v)
                    ctx.inc(v.tensor_scalar(y1n, psAV[:, 0:C], cols[:, 10:11],
                                            None, ALU.mult))

                emit_transposes3(ctx,
                                 [y1n[:, c * 128:(c + 1) * 128] for c in range(C8)],
                                 psT,
                                 y1t.rearrange("p (c x) -> p c x", c=C8),
                                 idn, evac="vector")

                @block.sync
                def _(sy, t=t):
                    ctx.gate(sy)
                    ctx.inc(sy.dma_start(out=x0t, in_=d["x0"][t * 128:(t + 1) * 128, :]), 16)

                for ob in range(2):
                    @block.tensor
                    def _(te, ob=ob):
                        ctx.gate(te)
                        for ci in range(C8):
                            te.matmul(psX[:, 0:512],
                                      y1t[:, ci * 128:(ci + 1) * 128],
                                      wB[:, ci * C + ob * 512: ci * C + (ob + 1) * 512],
                                      start=(ci == 0), stop=False)
                        ins = te.matmul(psX[:, 0:512], ones1[0:1, :],
                                        brow[0:1, C + ob * 512: C + (ob + 1) * 512],
                                        start=False, stop=True)
                        ctx.inc(ins)

                    @block.vector
                    def _(v, ob=ob):
                        ctx.gate(v)
                        ctx.inc(v.scalar_tensor_tensor(
                            x1b[:, ob * 512:(ob + 1) * 512], psX[:, 0:512], 1.0,
                            x0t[:, ob * 512:(ob + 1) * 512], ALU.mult, ALU.add))

                @block.sync
                def _(sy, t=t):
                    ctx.gate(sy)
                    ctx.inc(sy.dma_start(out=d["x1"][t * 128:(t + 1) * 128, :],
                                         in_=x1b[:]), 16)


def prep_stageA_inputs(S, x0, wq, bq, wk, bk, wv, bv, wo, bo):
    f = np.float32
    return {
        "x0": np.ascontiguousarray(x0, f),
        "wq": np.ascontiguousarray(wq, f),
        "wk": np.ascontiguousarray(wk, f),
        "wv": np.ascontiguousarray(wv, f),
        "wo": np.ascontiguousarray(wo, f),
        "bq_pc": np.ascontiguousarray(np.asarray(bq, f).reshape(8, 128).T),
        "bk_pc": np.ascontiguousarray(np.asarray(bk, f).reshape(8, 128).T),
        "bv_row": np.ascontiguousarray(np.asarray(bv, f).reshape(1, C)),
        "bo_row": np.ascontiguousarray(np.asarray(bo, f).reshape(1, C)),
        "idn": np.eye(128).astype(ml_dtypes.bfloat16),
        "dmA": _dmaskA(),
    }


def _dmaskA():
    p = np.arange(128)[:, None]
    j = np.arange(256)[None, :]
    return np.where(j <= p + 1, -1e30, 0.0).astype(np.float32)


def declare_stageA(nc, S, kind_in="ExternalInput", kind_out="ExternalOutput"):
    return {
        "x0": nc.dram_tensor("x0", [S, C], F32, kind=kind_in).ap(),
        "wq": nc.dram_tensor("wq", [C, C], F32, kind=kind_in).ap(),
        "wk": nc.dram_tensor("wk", [C, C], F32, kind=kind_in).ap(),
        "wv": nc.dram_tensor("wv", [C, C], F32, kind=kind_in).ap(),
        "wo": nc.dram_tensor("wo", [C, C], F32, kind=kind_in).ap(),
        "bq_pc": nc.dram_tensor("bq_pc", [128, 8], F32, kind=kind_in).ap(),
        "bk_pc": nc.dram_tensor("bk_pc", [128, 8], F32, kind=kind_in).ap(),
        "bv_row": nc.dram_tensor("bv_row", [1, C], F32, kind=kind_in).ap(),
        "bo_row": nc.dram_tensor("bo_row", [1, C], F32, kind=kind_in).ap(),
        "idn": nc.dram_tensor("idn", [128, 128], BF16, kind=kind_in).ap(),
        "dmA": nc.dram_tensor("dmA", [128, 256], F32, kind=kind_in).ap(),
        "x1": nc.dram_tensor("x1", [S, C], F32, kind=kind_out).ap(),
    }


def build_stageA(S):
    nc = bass.Bass("TRN2", target_bir_lowering=False)
    d = declare_stageA(nc, S)
    with ExitStack() as es:
        ctx = Ctx(nc, es)
        emit_stageA(ctx, S, d)

        @ctx.block.sync
        def _(sy):
            ctx.gate(sy)
    return nc


def emit_stageB(ctx, S, d):
    nc = ctx.nc
    block = ctx.block
    T = S // 128
    H = 16

    es = ExitStack()
    with es:
        KTr = es.enter_context(nc.sbuf_tensor("B_KT", [128, C8 * S], BF16))
        Vb = es.enter_context(nc.sbuf_tensor("B_V", [128, T * C], BF16))
        wbuf = es.enter_context(nc.sbuf_tensor("B_w", [128, 16 * C], BF16))
        scores = es.enter_context(nc.sbuf_tensor("B_sc", [128, max(S, 4 * C)], F32))
        ebuf = es.enter_context(nc.sbuf_tensor("B_e", [128, max(S, C8 * 512)], BF16))
        attnT = es.enter_context(nc.sbuf_tensor("B_aT", [128, max(S, 2 * C)], BF16))
        qtb = es.enter_context(nc.sbuf_tensor("B_qt", [128, C], BF16))
        r1b = es.enter_context(nc.sbuf_tensor("B_r1", [128, C], BF16))
        krb = es.enter_context(nc.sbuf_tensor("B_kr", [128, C], BF16))
        n1w = es.enter_context(nc.sbuf_tensor("B_n1w", [128, C], BF16))
        cost = es.enter_context(nc.sbuf_tensor("B_cos", [128, C], BF16))
        sint = es.enter_context(nc.sbuf_tensor("B_sin", [128, C], BF16))
        cols = es.enter_context(nc.sbuf_tensor("B_cols", [128, 48], F32))
        idn = es.enter_context(nc.sbuf_tensor("B_idn", [128, 128], BF16))
        dmB = es.enter_context(nc.sbuf_tensor("B_dmB", [128, 128], F32))
        dmB2 = es.enter_context(nc.sbuf_tensor("B_dmB2", [128, 512], BF16))
        sbSC = es.enter_context(nc.semaphore("B_sc"))
        sbEV = es.enter_context(nc.semaphore("B_ev"))
        sbMX = es.enter_context(nc.semaphore("B_mx"))
        sbEX = es.enter_context(nc.semaphore("B_ex"))
        sbTT = es.enter_context(nc.semaphore("B_tt"))
        sbTE = es.enter_context(nc.semaphore("B_te"))
        sbAV = es.enter_context(nc.semaphore("B_av"))
        psS = es.enter_context(nc.psum_tensor("B_psS", [128, 2048], F32))
        psT = es.enter_context(nc.psum_tensor("B_psT", [128, 1024], BF16))
        psAV = es.enter_context(nc.psum_tensor("B_psAV", [128, 1024], F32))
        psX = es.enter_context(nc.psum_tensor("B_psX", [128, 512], F32))

        x1t = scores[:, 0:C]
        ub = scores[:, C:2 * C]
        trash = scores[:, 2 * C:3 * C]
        ktm = scores[:, 3 * C:4 * C]
        rot = scores[:, 0:C]
        y1n = attnT[:, 0:C]
        y1t = attnT[:, C:2 * C]

        def emit_ln_u(t):
            @block.sync
            def _(sy):
                ctx.gate(sy)
                ctx.inc(sy.dma_start(out=x1t, in_=d["x1"][t * 128:(t + 1) * 128, :]), 16)
            emit_ln(ctx, x1t, ub, scores[:, 2 * C:], cols)

        def emit_rms_r1():
            @block.scalar
            def _(s):
                ctx.gate(s)
                ctx.inc(s.activation(trash, ub, AF.Square, accum_out=cols[:, 0:1]))

            @block.vector
            def _(v):
                ctx.gate(v)
                ctx.inc(v.tensor_scalar(cols[:, 2:3], cols[:, 0:1], 1.0 / C, None,
                                        ALU.mult))
                ctx.gate(v)
                ctx.inc(v.tensor_scalar(cols[:, 2:3], cols[:, 2:3], 1e-6, None, ALU.add))

            @block.scalar
            def _(s):
                ctx.gate(s)
                ctx.inc(s.activation(cols[:, 3:4], cols[:, 2:3], AF.Ln))
                ctx.gate(s)
                ctx.inc(s.activation(cols[:, 3:4], cols[:, 3:4], AF.Exp, scale=-0.5))

            @block.vector
            def _(v):
                ctx.gate(v)
                ctx.inc(v.tensor_scalar(trash, ub, cols[:, 3:4], None, ALU.mult))
                ctx.gate(v)
                ctx.inc(v.tensor_tensor(r1b[:], trash, n1w[:], ALU.mult))

        def emit_rope(t, src_f32, dst_bf):
            @block.sync
            def _(sy):
                ctx.gate(sy)
                ctx.inc(sy.dma_start(out=cost[:], in_=d["cosb"][t * 128:(t + 1) * 128, :]), 16)
                ctx.inc(sy.dma_start(out=sint[:], in_=d["sinb"][t * 128:(t + 1) * 128, :]), 16)

            sv = src_f32.rearrange("p (h x j) -> p h x j", h=H, x=2)
            rv = rot.rearrange("p (h x j) -> p h x j", h=H, x=2)

            @block.vector
            def _(v):
                ctx.gate(v)
                v.tensor_scalar(rv[:, :, 0, :], sv[:, :, 1, :], -1.0, None, ALU.mult)
                ctx.inc(v.tensor_copy(rv[:, :, 1, :], sv[:, :, 0, :]))
                ctx.gate(v)
                ctx.inc(v.tensor_tensor(trash, src_f32, cost[:], ALU.mult))
                ctx.gate(v)
                ctx.inc(v.tensor_tensor(rot, rot, sint[:], ALU.mult))
                ctx.gate(v)
                ctx.inc(v.tensor_tensor(dst_bf, trash, rot, ALU.add))

        @block.sync
        def _(sy):
            ctx.gate(sy)
            ctx.inc(sy.dma_start(out=idn[:], in_=d["idn"]), 16)
            ctx.inc(sy.dma_start(out=dmB[:], in_=d["dmB"]), 16)

        @block.gpsimd
        def _(g):
            ctx.gate(g)
            dcast_load(ctx, g, dmB2[:], d["dmB2"])

        @block.gpsimd
        def _(g):
            ctx.gate(g)
            src = d["qkvT"].rearrange("(c p) o -> p c o", p=128)[:, :, C:3 * C]
            dst = wbuf.rearrange("p (c o) -> p c o", c=C8)
            ctx.inc(g.dma_start(out=dst, in_=src), 16)
            dcast_load(ctx, g, n1w[:], d["n1w_r"])

        for t in range(T):
            emit_ln_u(t)
            emit_rms_r1()
            emit_transposes3(ctx, [r1b[:, c * 128:(c + 1) * 128] for c in range(C8)],
                             psT,
                             ebuf[:, 0:C8 * 512].rearrange("p (c t) -> p c t", c=C8)
                             [:, :, (t % 4) * 128:(t % 4) * 128 + 128],
                             idn, evac="vector")
            jj = t % 4

            for vb in range(2):
                @block.tensor
                def _(te, jj=jj, vb=vb):
                    ctx.gate(te)
                    ins = None
                    for ci in range(C8):
                        ins = te.matmul(psAV[:, vb * 512:(vb + 1) * 512],
                                        ebuf[:, ci * 512 + jj * 128: ci * 512 + jj * 128 + 128],
                                        wbuf[:, ci * 2 * C + vb * 512: ci * 2 * C + (vb + 1) * 512],
                                        start=(ci == 0), stop=(ci == C8 - 1))
                    ctx.inc(ins)

            @block.scalar
            def _(s):
                ctx.gate(s)
                ctx.inc(s.copy(ktm, psAV[:, 0:C]))

            emit_rope(t, ktm, krb[:])
            emit_transposes3(ctx, [krb[:, c * 128:(c + 1) * 128] for c in range(C8)],
                             psT,
                             KTr.rearrange("p (c s) -> p c s", c=C8)
                             [:, :, t * 128:(t + 1) * 128],
                             idn, evac="vector")

            for vb in range(2):
                @block.tensor
                def _(te, jj=jj, vb=vb):
                    ctx.gate(te)
                    ins = None
                    for ci in range(C8):
                        ins = te.matmul(psAV[:, vb * 512:(vb + 1) * 512],
                                        ebuf[:, ci * 512 + jj * 128: ci * 512 + jj * 128 + 128],
                                        wbuf[:, ci * 2 * C + C + vb * 512: ci * 2 * C + C + (vb + 1) * 512],
                                        start=(ci == 0), stop=(ci == C8 - 1))
                    ctx.inc(ins)

                @block.scalar
                def _(s, t=t, vb=vb):
                    ctx.gate(s)
                    ctx.inc(s.copy(Vb[:, t * C + vb * 512: t * C + (vb + 1) * 512],
                                   psAV[:, vb * 512:(vb + 1) * 512]))

        @block.gpsimd
        def _(g):
            ctx.gate(g)
            src = d["qkvT"].rearrange("(c p) o -> p c o", p=128)[:, :, 0:C]
            dst = wbuf[:, 0:C8 * C].rearrange("p (c o) -> p c o", c=C8)
            ctx.inc(g.dma_start(out=dst, in_=src), 16)
            dcast_load_w(ctx, g, wbuf[:, C8 * C:16 * C], d["projT"])

        for t in range(T):
            kmax = (t + 1) * 128
            KBt = (kmax + 511) // 512

            emit_ln_u(t)
            emit_rms_r1()
            emit_transposes3(ctx, [r1b[:, c * 128:(c + 1) * 128] for c in range(C8)],
                             psT,
                             ebuf[:, 0:C8 * 512].rearrange("p (c t) -> p c t", c=C8)
                             [:, :, 0:128],
                             idn, evac="vector")

            for vb in range(2):
                @block.tensor
                def _(te, vb=vb):
                    ctx.gate(te)
                    ins = None
                    for ci in range(C8):
                        ins = te.matmul(psAV[:, vb * 512:(vb + 1) * 512],
                                        ebuf[:, ci * 512: ci * 512 + 128],
                                        wbuf[:, ci * C + vb * 512: ci * C + (vb + 1) * 512],
                                        start=(ci == 0), stop=(ci == C8 - 1))
                    ctx.inc(ins)

            @block.scalar
            def _(s):
                ctx.gate(s)
                ctx.inc(s.copy(ktm, psAV[:, 0:C]))

            emit_rope(t, ktm, krb[:])
            emit_transposes3(ctx, [krb[:, c * 128:(c + 1) * 128] for c in range(C8)],
                             psT,
                             qtb.rearrange("p (c x) -> p c x", c=C8),
                             idn, evac="vector")

            if t == 0:
                ctx.b_g = 0
                ctx.b_b = 0
                ctx.b_u = 0

            def b_groups(KBt):
                gs = []
                kb0 = 0
                while kb0 < KBt - 1:
                    kbn = min(4, KBt - 1 - kb0)
                    gs.append((kb0, kbn))
                    kb0 += kbn
                gs.append((KBt - 1, 1))
                return gs

            def emit_scores_te(u, h, first, g_base, kmax=kmax, KBt=KBt):
                pb = (h % 2) * 64
                ch = h // 2
                for gg, (kb0, kbn) in enumerate(b_groups(KBt)):
                    g_i = g_base + gg

                    @block.tensor
                    def _(te, h=h, pb=pb, ch=ch, kb0=kb0, kbn=kbn, g_i=g_i,
                          first=first, kmax=kmax):
                        if first and kb0 == 0:
                            ctx.gate(te)
                        if g_i >= 1:
                            te.wait_ge(sbEV, g_i)
                        ins = None
                        for kk in range(kbn):
                            kb = kb0 + kk
                            w = min(512, kmax - kb * 512)
                            ins = te.matmul(
                                psS[:, kk * 512: kk * 512 + w],
                                qtb[pb:pb + 64, ch * 128:(ch + 1) * 128],
                                KTr[pb:pb + 64, ch * S + kb * 512: ch * S + kb * 512 + w],
                                start=True, stop=True)
                        ins.then_inc(sbSC, 1)

            def emit_scores_evac(u, g_base, kmax=kmax, KBt=KBt):
                gl = b_groups(KBt)
                for gg, (kb0, kbn) in enumerate(gl):
                    g_i = g_base + gg
                    last = (gg == len(gl) - 1)
                    if not last:
                        @block.scalar
                        def _(s, kb0=kb0, kbn=kbn, g_i=g_i, kmax=kmax, u=u):
                            s.wait_ge(sbSC, g_i + 1)
                            if kb0 == 0 and u >= 1:
                                s.wait_ge(sbEX, u)
                            w = min(kbn * 512, kmax - kb0 * 512)
                            s.activation(scores[:, kb0 * 512: kb0 * 512 + w],
                                         psS[:, 0:w], AF.Identity,
                                         scale=1.0 / 8.0).then_inc(sbEV, 1)
                    else:
                        @block.vector
                        def _(v, kb0=kb0, g_i=g_i, kmax=kmax, u=u):
                            v.wait_ge(sbSC, g_i + 1)
                            if kb0 == 0 and u >= 1:
                                v.wait_ge(sbEX, u)
                            w = kmax - kb0 * 512
                            v.scalar_tensor_tensor(
                                scores[:, kb0 * 512: kmax], psS[:, 0:w], 1.0 / 8.0,
                                dmB2[:, 512 - w:512], ALU.mult,
                                ALU.add).then_inc(sbEV, 1)

            def emit_soft_mask(u, h, g_end, t=t, kmax=kmax):
                @block.vector
                def _(v, u=u, g_end=g_end, t=t, kmax=kmax):
                    v.wait_ge(sbEV, g_end)
                    v.reduce_max(cols[:, 8:9], scores[:, 0:kmax], axis=AX.X,
                                 negate=True).then_inc(sbMX, 1)

                @block.scalar
                def _(s, u=u, h=h, b_prev=ctx.b_b, kmax=kmax):
                    s.wait_ge(sbMX, u + 1)
                    s.wait_ge(sbTE, b_prev)
                    s.activation(ebuf[:, 0:kmax], scores[:, 0:kmax], AF.Exp,
                                 bias=cols[:, 8:9],
                                 accum_out=cols[:, 16 + h:17 + h]).then_inc(sbEX, 1)

            def emit_tp_av(u, h, t=t, kmax=kmax):
                nb = t + 1
                psv = psT[0:128, 0:8 * 128].rearrange("q (b c) -> q b c", c=128)
                dst3 = attnT[:, 0:nb * 128].rearrange("p (k c) -> p k c", c=128)
                i = 0
                while i < nb:
                    batch = min(8, nb - i)
                    b_i = ctx.b_b
                    ctx.b_b += 1

                    @block.tensor
                    def _(te, u=u, i=i, batch=batch, b_i=b_i):
                        te.wait_ge(sbEX, u + 1)
                        if b_i >= 1:
                            te.wait_ge(sbTE, b_i)
                        ins = None
                        for j in range(batch):
                            ins = te.transpose(psT[0:128, j * 128:(j + 1) * 128],
                                               ebuf[:, (i + j) * 128:(i + j + 1) * 128],
                                               idn[:, 0:128])
                        ins.then_inc(sbTT, 1)

                    @block.vector
                    def _(v, u=u, i=i, batch=batch, b_i=b_i):
                        if i == 0:
                            v.wait_ge(sbAV, u)
                        v.wait_ge(sbTT, b_i + 1)
                        v.tensor_copy(dst3[:, i:i + batch, :],
                                      psv[:, 0:batch, :]).then_inc(sbTE, 1)
                    i += batch

                @block.tensor
                def _(te, t=t, h=h, b_end=ctx.b_b):
                    te.wait_ge(sbTE, b_end)
                    ins = None
                    for kc in range(t + 1):
                        ins = te.matmul(psAV[:, h * 64:(h + 1) * 64],
                                        attnT[:, kc * 128:(kc + 1) * 128],
                                        Vb[:, kc * C + h * 64: kc * C + h * 64 + 64],
                                        start=(kc == 0), stop=(kc == t))
                    ins.then_inc(sbAV, 1)

            ngrp = len(b_groups(KBt))
            g0 = ctx.b_g
            ctx.b_g += H * ngrp
            emit_scores_te(ctx.b_u, 0, True, g0)
            emit_scores_evac(ctx.b_u, g0)
            for h in range(H):
                u = ctx.b_u
                if h + 1 < H:
                    emit_scores_te(u + 1, h + 1, False, g0 + (h + 1) * ngrp)
                emit_soft_mask(u, h, g0 + (h + 1) * ngrp)
                if h + 1 < H:
                    emit_scores_evac(u + 1, g0 + (h + 1) * ngrp)
                emit_tp_av(u, h)
                ctx.b_u += 1

            @block.vector
            def _(v, t=t):
                v.wait_ge(sbAV, 16 * (t + 1))
                ctx.gate(v)
                ctx.inc(v.reciprocal(cols[:, 32:48], cols[:, 16:32]))

            for h in range(H):
                @block.vector
                def _(v, h=h):
                    ctx.gate(v)
                    ctx.inc(v.tensor_scalar(y1n[:, h * 64:(h + 1) * 64],
                                            psAV[:, h * 64:(h + 1) * 64],
                                            cols[:, 32 + h:33 + h], None, ALU.mult))

            emit_transposes3(ctx, [y1n[:, c * 128:(c + 1) * 128] for c in range(C8)],
                             psT,
                             y1t.rearrange("p (c x) -> p c x", c=C8),
                             idn, evac="vector")

            emit_ln_u(t)

            for ob in range(2):
                @block.tensor
                def _(te, ob=ob):
                    ctx.gate(te)
                    ins = None
                    for ci in range(C8):
                        ins = te.matmul(psX[:, 0:512],
                                        y1t[:, ci * 128:(ci + 1) * 128],
                                        wbuf[:, C8 * C + ci * C + ob * 512: C8 * C + ci * C + (ob + 1) * 512],
                                        start=(ci == 0), stop=(ci == C8 - 1))
                    ctx.inc(ins)

                @block.vector
                def _(v, ob=ob):
                    ctx.gate(v)
                    ctx.inc(v.scalar_tensor_tensor(
                        trash[:, ob * 512:(ob + 1) * 512] if False else scores[:, 2 * C + ob * 512: 2 * C + (ob + 1) * 512],
                        psX[:, 0:512], 1.0,
                        ub[:, ob * 512:(ob + 1) * 512], ALU.mult, ALU.add))

            @block.sync
            def _(sy, t=t):
                ctx.gate(sy)
                ctx.inc(sy.dma_start(out=d["a"][t * 128:(t + 1) * 128, :],
                                     in_=scores[:, 2 * C:3 * C]), 16)


def prep_stageB_inputs(S, qkv_w, proj_w, norm1_w):
    f = np.float32
    hd = 64
    inv = 1.0 / (10000.0 ** (np.arange(0, hd, 2) / hd))
    pos = np.arange(S)[:, None].astype(np.float64)
    fr = pos * inv[None, :]
    emb = np.concatenate([fr, fr], axis=1)
    cos = np.cos(emb)
    sin = np.sin(emb)
    cosb = np.tile(cos, (1, 16)).astype(ml_dtypes.bfloat16)
    sinb = np.tile(sin, (1, 16)).astype(ml_dtypes.bfloat16)
    return {
        "qkvT": np.ascontiguousarray(np.asarray(qkv_w, f).T),
        "projT": np.ascontiguousarray(np.asarray(proj_w, f).T),
        "n1w_r": np.ascontiguousarray(np.broadcast_to(np.asarray(norm1_w, f), (128, C))),
        "cosb": np.ascontiguousarray(cosb),
        "sinb": np.ascontiguousarray(sinb),
        "idn_b": np.eye(128).astype(ml_dtypes.bfloat16),
        "dmB": _dmaskB(),
        "dmB2": np.concatenate([np.zeros((128, 384), np.float32), _dmaskB()], axis=1),
    }


def _dmaskB():
    p = np.arange(128)[:, None]
    j = np.arange(128)[None, :]
    return np.where(j <= p, 0.0, -1e30).astype(np.float32)


def declare_stageB(nc, S, kind_in="ExternalInput", kind_out="ExternalOutput",
                   x1_kind=None):
    return {
        "x1": nc.dram_tensor("x1", [S, C], F32, kind=x1_kind or kind_in).ap(),
        "qkvT": nc.dram_tensor("qkvT", [C, 3 * C], F32, kind=kind_in).ap(),
        "projT": nc.dram_tensor("projT", [C, C], F32, kind=kind_in).ap(),
        "n1w_r": nc.dram_tensor("n1w_r", [128, C], F32, kind=kind_in).ap(),
        "cosb": nc.dram_tensor("cosb", [S, C], BF16, kind=kind_in).ap(),
        "sinb": nc.dram_tensor("sinb", [S, C], BF16, kind=kind_in).ap(),
        "idn": nc.dram_tensor("idn_b", [128, 128], BF16, kind=kind_in).ap(),
        "dmB": nc.dram_tensor("dmB", [128, 128], F32, kind=kind_in).ap(),
        "dmB2": nc.dram_tensor("dmB2", [128, 512], F32, kind=kind_in).ap(),
        "a": nc.dram_tensor("a", [S, C], F32, kind=kind_out).ap(),
    }


def build_stageB(S):
    nc = bass.Bass("TRN2", target_bir_lowering=False)
    d = declare_stageB(nc, S)
    with ExitStack() as es:
        ctx = Ctx(nc, es)
        emit_stageB(ctx, S, d)

        @ctx.block.sync
        def _(sy):
            ctx.gate(sy)
    return nc


def emit_stageC(ctx, S, d):
    nc = ctx.nc
    block = ctx.block
    T = S // 128
    E, HF = 8, 4096
    TGT = min(8, T)
    TGN = T // TGT
    assert T % TGT == 0

    es = ExitStack()
    with es:
        r2Tg = es.enter_context(nc.sbuf_tensor("C_r2T", [128, C8 * TGT * 128], BF16))
        hbuf = es.enter_context(nc.sbuf_tensor("C_h", [128, 32 * TGT * 128], BF16))
        wgc = [es.enter_context(nc.sbuf_tensor(f"C_wg{i}", [128, C8 * 512], BF16))
               for i in range(2)]
        wuc = [es.enter_context(nc.sbuf_tensor(f"C_wu{i}", [128, C8 * 512], BF16))
               for i in range(2)]
        wdc = [es.enter_context(nc.sbuf_tensor(f"C_wd{i}", [128, 8 * C], BF16))
               for i in range(2)]
        s_guld = es.enter_context(nc.semaphore("C_guld"))
        sP = es.enter_context(nc.semaphore("C_sP"))
        sA = es.enter_context(nc.semaphore("C_sA"))
        sD = es.enter_context(nc.semaphore("C_sD"))
        s_bld = es.enter_context(nc.semaphore("C_bld"))
        s_bus = es.enter_context(nc.semaphore("C_bus"))
        s_tpa = es.enter_context(nc.semaphore("C_tpa"))
        s_tpb = es.enter_context(nc.semaphore("C_tpb"))
        r2c = es.enter_context(nc.sbuf_tensor("C_r2c", [128, C], BF16))
        s_guus = es.enter_context(nc.semaphore("C_guus"))
        s_wdld = es.enter_context(nc.semaphore("C_wdld"))
        s_wdus = es.enter_context(nc.semaphore("C_wdus"))
        acc = es.enter_context(nc.sbuf_tensor("C_acc", [128, TGT * C], F32))
        gts = es.enter_context(nc.sbuf_tensor("C_gts", [128, TGT * 8], F32))
        rw = es.enter_context(nc.sbuf_tensor("C_rw", [128, C8 * 8], F32))
        n2w = es.enter_context(nc.sbuf_tensor("C_n2w", [128, C], BF16))
        n2wf = es.enter_context(nc.sbuf_tensor("C_n2wf", [128, C], F32))
        r2f = es.enter_context(nc.sbuf_tensor("C_r2f", [128, C], F32))
        r2Tf = es.enter_context(nc.sbuf_tensor("C_r2Tf", [128, C], F32))
        idnf = es.enter_context(nc.sbuf_tensor("C_idnf", [128, 128], F32))
        at = es.enter_context(nc.sbuf_tensor("C_at", [128, C], F32))
        x1t = es.enter_context(nc.sbuf_tensor("C_x1t", [128, C], F32))
        r2b = es.enter_context(nc.sbuf_tensor("C_r2b", [128, C], BF16))
        gtmp = es.enter_context(nc.sbuf_tensor("C_gtmp", [128, C], BF16))
        trash = es.enter_context(nc.sbuf_tensor("C_tr", [128, C], F32))
        lg = es.enter_context(nc.sbuf_tensor("C_lg", [128, 8], F32))
        l2 = es.enter_context(nc.sbuf_tensor("C_l2", [128, 8], F32))
        msk1 = es.enter_context(nc.sbuf_tensor("C_m1", [128, 8], F32))
        cols = es.enter_context(nc.sbuf_tensor("C_cols", [128, 16], F32))
        idn = es.enter_context(nc.sbuf_tensor("C_idn", [128, 128], BF16))
        psALL = es.enter_context(nc.psum_tensor("C_psA", [128, 3584], F32))
        psT = es.enter_context(nc.psum_tensor("C_psT", [128, 1024], BF16))
        psG = psALL[:, 0:1024]
        psU = psALL[:, 1024:2048]
        psR = psALL[:, 3072:3080]

        @block.gpsimd
        def _(g):
            ctx.gate(g)
            if "cast_done" not in d:
                for e in range(E):
                    ctx.inc(g.dma_start(out=d["wgT_bf"][e], in_=d["wgT"][e]), 16)
                    ctx.inc(g.dma_start(out=d["wuT_bf"][e], in_=d["wuT"][e]), 16)
                    ctx.inc(g.dma_start(out=d["wdT_bf"][e], in_=d["wdT"][e]), 16)
            dcast_load(ctx, g, n2w[:], d["n2w_r"])

        @block.sync
        def _(sy):
            ctx.gate(sy)
            src = d["rwT"].rearrange("(c p) o -> p c o", p=128)
            ctx.inc(sy.dma_start(out=rw.rearrange("p (c o) -> p c o", c=C8), in_=src), 16)
            ctx.inc(sy.dma_start(out=n2wf[:], in_=d["n2w_r"]), 16)
            ctx.inc(sy.dma_start(out=idnf[:], in_=d["idn_f"]), 16)

        @block.sync
        def _(sy):
            ctx.gate(sy)
            ctx.inc(sy.dma_start(out=idn[:], in_=d["idn"]), 16)

        for tg in range(TGN):
            for tt in range(TGT):
                t = tg * TGT + tt

                @block.sync
                def _(sy, t=t):
                    ctx.gate(sy)
                    ctx.inc(sy.dma_start(out=at[:], in_=d["a"][t * 128:(t + 1) * 128, :]), 16)

                @block.scalar
                def _(s):
                    ctx.gate(s)
                    ctx.inc(s.activation(trash[:], at[:], AF.Square, accum_out=cols[:, 0:1]))

                @block.vector
                def _(v):
                    ctx.gate(v)
                    ctx.inc(v.tensor_scalar(cols[:, 1:2], cols[:, 0:1], 1.0 / C, None, ALU.mult))
                    ctx.gate(v)
                    ctx.inc(v.tensor_scalar(cols[:, 1:2], cols[:, 1:2], 1e-6, None, ALU.add))

                @block.scalar
                def _(s):
                    ctx.gate(s)
                    ctx.inc(s.activation(cols[:, 2:3], cols[:, 1:2], AF.Ln))
                    ctx.gate(s)
                    ctx.inc(s.activation(cols[:, 2:3], cols[:, 2:3], AF.Exp, scale=-0.5))

                @block.vector
                def _(v):
                    ctx.gate(v)
                    ctx.inc(v.tensor_scalar(trash[:], at[:], cols[:, 2:3], None, ALU.mult))
                    ctx.gate(v)
                    ctx.inc(v.tensor_tensor(r2f[:], trash[:], n2wf[:], ALU.mult))
                    ctx.gate(v)
                    ctx.inc(v.tensor_copy(r2b[:], r2f[:]))

                emit_transposes3(ctx, [r2f[:, c * 128:(c + 1) * 128] for c in range(C8)],
                                 psALL[:, 2048:2560],
                                 r2Tf.rearrange("p (c x) -> p c x", c=C8),
                                 idnf, per_batch=4, evac="vector")
                emit_transposes3(ctx, [r2b[:, c * 128:(c + 1) * 128] for c in range(C8)],
                                 psT,
                                 r2Tg.rearrange("p (c t) -> p c t", c=C8)
                                 [:, :, tt * 128:(tt + 1) * 128],
                                 idn, evac="vector")

                @block.tensor
                def _(te):
                    ctx.gate(te)
                    ins = None
                    for ci in range(C8):
                        ins = te.matmul(psR,
                                        r2Tf[:, ci * 128:(ci + 1) * 128],
                                        rw[:, ci * 8:(ci + 1) * 8],
                                        start=(ci == 0), stop=(ci == C8 - 1))
                    ctx.inc(ins)

                @block.scalar
                def _(s):
                    ctx.gate(s)
                    ctx.inc(s.copy(lg[:], psR))

                @block.vector
                def _(v):
                    ctx.gate(v)
                    ctx.inc(v.reduce_max(cols[:, 4:5], lg[:], axis=AX.X))
                    ctx.gate(v)
                    ctx.inc(v.tensor_scalar(msk1[:], lg[:], cols[:, 4:5], None, ALU.is_ge))
                    ctx.gate(v)
                    ctx.inc(v.scalar_tensor_tensor(l2[:], msk1[:], -1e30, lg[:],
                                                   ALU.mult, ALU.add))
                    ctx.gate(v)
                    ctx.inc(v.reduce_max(cols[:, 5:6], l2[:], axis=AX.X))
                    ctx.gate(v)
                    ctx.inc(v.tensor_tensor(cols[:, 6:7], cols[:, 5:6], cols[:, 4:5],
                                            ALU.subtract))

                @block.scalar
                def _(s):
                    ctx.gate(s)
                    ctx.inc(s.activation(cols[:, 7:8], cols[:, 6:7], AF.Exp))

                if "dbgC" in d:
                    @block.sync
                    def _(sy, t=t):
                        ctx.gate(sy)
                        ctx.inc(sy.dma_start(out=d["dbgC"][0, t * 128:(t + 1) * 128, 0:8],
                                             in_=lg[:]), 16)

                @block.vector
                def _(v, tt=tt):
                    ctx.gate(v)
                    ctx.inc(v.tensor_scalar(cols[:, 8:9], cols[:, 7:8], 1.0, None, ALU.add))
                    ctx.gate(v)
                    ctx.inc(v.reciprocal(cols[:, 9:10], cols[:, 8:9]))
                    ctx.gate(v)
                    ctx.inc(v.tensor_tensor(cols[:, 10:11], cols[:, 7:8], cols[:, 9:10],
                                            ALU.mult))
                    ctx.gate(v)
                    ctx.inc(v.tensor_scalar(msk1[:], msk1[:], cols[:, 9:10], None, ALU.mult))
                    ctx.gate(v)
                    ctx.inc(v.tensor_scalar(l2[:], l2[:], cols[:, 5:6], None, ALU.is_ge))
                    ctx.gate(v)
                    ctx.inc(v.scalar_tensor_tensor(gts[:, tt * 8:(tt + 1) * 8], l2[:],
                                                   cols[:, 10:11], msk1[:],
                                                   ALU.mult, ALU.add))

                if "dbgC" in d:
                    @block.sync
                    def _(sy, t=t, tt=tt):
                        ctx.gate(sy)
                        ctx.inc(sy.dma_start(out=d["dbgC"][1, t * 128:(t + 1) * 128, 0:8],
                                             in_=gts[:, tt * 8:(tt + 1) * 8]), 16)

            gu_idx = tg * E * 8
            n_tsg = (TGT + 2) // 3
            wd_idx = tg * E * n_tsg * 4

            @block.sync
            def _(sy, tg=tg, gu_idx=gu_idx, wd_idx=wd_idx):
                ctx.gate(sy)
                if tg == 0 and "cast_done" in d:
                    sy.wait_ge(d["cast_done"][0], d["cast_done"][1])
                for e in range(E):
                    for hb in range(8):
                        k = gu_idx + e * 8 + hb
                        if k >= 2:
                            sy.wait_ge(s_guus, k - 1)
                        srcg = d["wgT_bf"][e].rearrange("(c p) h -> p c h", p=128)[:, :, hb * 512:(hb + 1) * 512]
                        sy.dma_start(out=wgc[k % 2].rearrange("p (c h) -> p c h", c=C8),
                                     in_=srcg).then_inc(s_guld, 16)
                        srcu = d["wuT_bf"][e].rearrange("(c p) h -> p c h", p=128)[:, :, hb * 512:(hb + 1) * 512]
                        sy.dma_start(out=wuc[k % 2].rearrange("p (c h) -> p c h", c=C8),
                                     in_=srcu).then_inc(s_guld, 16)
                    for i in range(n_tsg * 4):
                        lci = i % 4
                        k = wd_idx + e * n_tsg * 4 + i
                        if k >= 2:
                            sy.wait_ge(s_wdus, k - 1)
                        sy.dma_start(out=wdc[k % 2].rearrange("p (hc c) -> p hc c", hc=8),
                                     in_=d["wdT_bf"][e][lci * 1024:(lci + 1) * 1024, :]
                                     .rearrange("(hc p) c -> p hc c", p=128)
                                     ).then_inc(s_wdld, 16)

            H2 = TGT * 128 // 512
            units_per_e = 8 * 4 * H2
            for e in range(E):
                for hb in range(8):
                    k = gu_idx + e * 8 + hb
                    for hs in range(4):
                        hc = hb * 4 + hs
                        for half in range(H2):
                            u = ctx.c_unit = getattr(ctx, "c_unit", -1) + 1
                            par = u % 2
                            first_e = (hb == 0 and hs == 0 and half == 0)
                            first_hb = (hs == 0 and half == 0)
                            last_hb = (hs == 3 and half == H2 - 1)
                            last_e = (hb == 7 and last_hb)
                            gs = psALL[:, par * 1024: par * 1024 + 512]
                            us_ = psALL[:, par * 1024 + 512:(par + 1) * 1024]
                            gt = gtmp[:, par * 512:(par + 1) * 512]

                            @block.tensor
                            def _(te, hs=hs, half=half, k=k, u=u, gs=gs, us_=us_,
                                  first_e=first_e, first_hb=first_hb):
                                if first_e:
                                    ctx.gate(te)
                                if first_hb:
                                    te.wait_ge(s_guld, 32 * (k + 1))
                                if u >= 2:
                                    te.wait_ge(sD, u - 1)
                                ins = None
                                for ci in range(C8):
                                    ins = te.matmul(
                                        gs,
                                        wgc[k % 2][:, ci * 512 + hs * 128: ci * 512 + hs * 128 + 128],
                                        r2Tg[:, ci * TGT * 128 + half * 512: ci * TGT * 128 + (half + 1) * 512],
                                        start=(ci == 0), stop=(ci == C8 - 1))
                                for ci in range(C8):
                                    ins = te.matmul(
                                        us_,
                                        wuc[k % 2][:, ci * 512 + hs * 128: ci * 512 + hs * 128 + 128],
                                        r2Tg[:, ci * TGT * 128 + half * 512: ci * TGT * 128 + (half + 1) * 512],
                                        start=(ci == 0), stop=(ci == C8 - 1))
                                ins.then_inc(sP, 1)

                            @block.scalar
                            def _(s, u=u, gs=gs, gt=gt):
                                s.wait_ge(sP, u + 1)
                                s.activation(gt, gs, AF.Silu).then_inc(sA, 1)

                            @block.vector
                            def _(v, u=u, hc=hc, half=half, gt=gt, us_=us_,
                                  first_e=first_e, last_hb=last_hb, last_e=last_e):
                                if first_e:
                                    ctx.gate(v)
                                v.wait_ge(sA, u + 1)
                                if last_hb:
                                    v.engine_nop().then_inc(s_guus, 1)
                                ins = v.tensor_tensor(
                                    hbuf[:, hc * TGT * 128 + half * 512: hc * TGT * 128 + (half + 1) * 512],
                                    gt, us_, ALU.mult)
                                ins.then_inc(sD, 1)
                                if last_e:
                                    v.wait_ge(sD, u + 1)
                                    ctx.inc(v.engine_nop())

                ts0 = 0
                tsg_i = 0
                while ts0 < TGT:
                    nts = min(3, TGT - ts0)
                    for lci in range(4):
                        k = wd_idx + (e * n_tsg + tsg_i) * 4 + lci

                        @block.tensor
                        def _(te, lci=lci, ts0=ts0, nts=nts, k=k):
                            if lci == 0:
                                ctx.gate(te)
                            te.wait_ge(s_wdld, 16 * (k + 1))
                            ins = None
                            for hs in range(8):
                                hc = lci * 8 + hs
                                for tsl in range(nts):
                                    ts = ts0 + tsl
                                    for cb in range(2):
                                        ins = te.matmul(
                                            psALL[:, tsl * 1024 + cb * 512: tsl * 1024 + (cb + 1) * 512],
                                            hbuf[:, hc * TGT * 128 + ts * 128: hc * TGT * 128 + ts * 128 + 128],
                                            wdc[k % 2][:, hs * C + cb * 512: hs * C + (cb + 1) * 512],
                                            start=(hc == 0), stop=(hc == 31))
                            if lci == 3:
                                ctx.inc(ins)
                            else:
                                ins.then_inc(s_wdus, 1)

                    for tsl in range(nts):
                        ts = ts0 + tsl

                        @block.vector
                        def _(v, e=e, ts=ts, tsl=tsl, first=(tsl == 0)):
                            ctx.gate(v)
                            if first:
                                v.engine_nop().then_inc(s_wdus, 1)
                            gcol = gts[:, ts * 8 + e: ts * 8 + e + 1]
                            if e == 0:
                                ctx.inc(v.tensor_scalar(
                                    acc[:, ts * C:(ts + 1) * C],
                                    psALL[:, tsl * 1024:(tsl + 1) * 1024],
                                    gcol, None, ALU.mult))
                            else:
                                ctx.inc(v.scalar_tensor_tensor(
                                    acc[:, ts * C:(ts + 1) * C],
                                    psALL[:, tsl * 1024:(tsl + 1) * 1024],
                                    gcol, acc[:, ts * C:(ts + 1) * C],
                                    ALU.mult, ALU.add))
                    ts0 += nts
                    tsg_i += 1

                if "dbgC" in d and tg == 0 and e == 0:
                    @block.sync
                    def _(sy):
                        ctx.gate(sy)
                        for tt2 in range(TGT):
                            ctx.inc(sy.dma_start(
                                out=d["dbgC4"][tt2 * 128:(tt2 + 1) * 128, :],
                                in_=acc[:, tt2 * C:(tt2 + 1) * C]), 16)

            if "dbgC" in d and tg == 0:
                @block.sync
                def _(sy):
                    ctx.gate(sy)
                    for tt2 in range(TGT):
                        ctx.inc(sy.dma_start(
                            out=d["dbgC2"][tt2 * 128:(tt2 + 1) * 128, :],
                            in_=acc[:, tt2 * C:(tt2 + 1) * C]), 16)

                @block.gpsimd
                def _(g):
                    ctx.gate(g)
                    ctx.inc(g.dma_start(out=d["dbgC3"],
                                        in_=hbuf[:, 0:8 * TGT * 128]), 16)

            for tt in range(TGT):
                t = tg * TGT + tt

                @block.sync
                def _(sy, t=t):
                    ctx.gate(sy)
                    ctx.inc(sy.dma_start(out=x1t[:], in_=d["x1"][t * 128:(t + 1) * 128, :]), 16)
                    ctx.inc(sy.dma_start(out=at[:], in_=d["a"][t * 128:(t + 1) * 128, :]), 16)

                @block.vector
                def _(v, tt=tt):
                    ctx.gate(v)
                    ctx.inc(v.tensor_tensor(x1t[:], x1t[:], at[:], ALU.add))
                    ctx.gate(v)
                    ctx.inc(v.tensor_tensor(x1t[:], x1t[:], acc[:, tt * C:(tt + 1) * C],
                                            ALU.add))

                @block.sync
                def _(sy, t=t):
                    ctx.gate(sy)
                    ctx.inc(sy.dma_start(out=d["x2"][t * 128:(t + 1) * 128, :],
                                         in_=x1t[:]), 16)


def prep_stageC_inputs(S, norm2_w, router_w, wg, wu, wd):
    f = np.float32
    return {
        "n2w_r": np.ascontiguousarray(np.broadcast_to(np.asarray(norm2_w, f), (128, C))),
        "rwT": np.ascontiguousarray(np.asarray(router_w, f).T),
        "wgT": np.ascontiguousarray(np.asarray(wg, f).transpose(0, 2, 1)),
        "wuT": np.ascontiguousarray(np.asarray(wu, f).transpose(0, 2, 1)),
        "wdT": np.ascontiguousarray(np.asarray(wd, f).transpose(0, 2, 1)),
        "idn_c": np.eye(128).astype(ml_dtypes.bfloat16),
        "idn_f": np.eye(128, dtype=np.float32),
    }


def declare_stageC(nc, S, kind_in="ExternalInput", kind_out="ExternalOutput",
                   act_kind=None):
    HF = 4096
    return {
        "x1": nc.dram_tensor("x1c", [S, C], F32, kind=act_kind or kind_in).ap(),
        "a": nc.dram_tensor("a", [S, C], F32, kind=act_kind or kind_in).ap(),
        "n2w_r": nc.dram_tensor("n2w_r", [128, C], F32, kind=kind_in).ap(),
        "rwT": nc.dram_tensor("rwT", [C, 8], F32, kind=kind_in).ap(),
        "wgT": nc.dram_tensor("wgT", [8, C, HF], F32, kind=kind_in).ap(),
        "wuT": nc.dram_tensor("wuT", [8, C, HF], F32, kind=kind_in).ap(),
        "wdT": nc.dram_tensor("wdT", [8, HF, C], F32, kind=kind_in).ap(),
        "idn": nc.dram_tensor("idn_c", [128, 128], BF16, kind=kind_in).ap(),
        "idn_f": nc.dram_tensor("idn_f", [128, 128], F32, kind=kind_in).ap(),
        "wgT_bf": nc.dram_tensor("wgT_bf", [8, C, HF], BF16, kind="Internal").ap(),
        "wuT_bf": nc.dram_tensor("wuT_bf", [8, C, HF], BF16, kind="Internal").ap(),
        "wdT_bf": nc.dram_tensor("wdT_bf", [8, HF, C], BF16, kind="Internal").ap(),
        "x2": nc.dram_tensor("x2", [S, C], F32, kind=kind_out).ap(),
    }


def build_stageC(S):
    nc = bass.Bass("TRN2", target_bir_lowering=False)
    d = declare_stageC(nc, S)
    with ExitStack() as es:
        ctx = Ctx(nc, es)
        emit_stageC(ctx, S, d)

        @ctx.block.sync
        def _(sy):
            ctx.gate(sy)
    return nc


def emit_stageD(ctx, S, d):
    nc = ctx.nc
    block = ctx.block
    T = S // 128
    H = 16

    es = ExitStack()
    with es:
        wqkv = es.enter_context(nc.sbuf_tensor("D_wqkv", [128, C8 * 3 * C], BF16))
        wo = es.enter_context(nc.sbuf_tensor("D_wo", [128, C8 * C], BF16))
        Qbuf = es.enter_context(nc.sbuf_tensor("D_Q", [128, T * C], BF16))
        x2t = es.enter_context(nc.sbuf_tensor("D_x2t", [128, C], F32))
        lnb = es.enter_context(nc.sbuf_tensor("D_lnb", [128, C], BF16))
        lnT = es.enter_context(nc.sbuf_tensor("D_lnT", [128, C], BF16))
        trash = es.enter_context(nc.sbuf_tensor("D_tr", [128, C], F32))
        tmpf = es.enter_context(nc.sbuf_tensor("D_tmpf", [128, C], F32))
        k1 = es.enter_context(nc.sbuf_tensor("D_k1", [128, C], BF16))
        v1 = es.enter_context(nc.sbuf_tensor("D_v1", [128, C], BF16))
        kvb = es.enter_context(nc.sbuf_tensor("D_kv", [64, C], BF16))
        ksb = es.enter_context(nc.sbuf_tensor("D_ks", [64, 16], BF16))
        qT = es.enter_context(nc.sbuf_tensor("D_qT", [64, H * 128], BF16))
        y1n = es.enter_context(nc.sbuf_tensor("D_y1n", [128, C], BF16))
        y1t = es.enter_context(nc.sbuf_tensor("D_y1t", [128, C], BF16))
        nrm = es.enter_context(nc.sbuf_tensor("D_nrm", [128, 32], F32))
        cols = es.enter_context(nc.sbuf_tensor("D_cols", [128, 16], F32))
        ones_c = es.enter_context(nc.sbuf_tensor("D_ones", [128, 1], BF16))
        idn = es.enter_context(nc.sbuf_tensor("D_idn", [128, 128], BF16))
        psALL = es.enter_context(nc.psum_tensor("D_psA", [128, 2048], F32))
        psKV = es.enter_context(nc.psum_tensor("D_psKV", [64, 1536], F32))
        psT = es.enter_context(nc.psum_tensor("D_psT", [128, 1024], BF16))

        @block.sync
        def _(sy):
            ctx.gate(sy)
            ctx.inc(sy.dma_start(out=idn[:], in_=d["idn"]), 16)

        @block.gpsimd
        def _(g):
            ctx.gate(g)
            dcast_load_w(ctx, g, wqkv, d["Wqkv"], dout=3 * C)
            dcast_load_w(ctx, g, wo, d["Wo"])
            ctx.inc(g.memset(ones_c[:], 1.0))

        for t in range(T):
            @block.sync
            def _(sy, t=t):
                ctx.gate(sy)
                ctx.inc(sy.dma_start(out=x2t[:], in_=d["x2"][t * 128:(t + 1) * 128, :]), 16)

            emit_ln(ctx, x2t[:], lnb[:], trash, cols)
            emit_transposes3(ctx, [lnb[:, c * 128:(c + 1) * 128] for c in range(C8)],
                             psT,
                             lnT.rearrange("p (c x) -> p c x", c=C8),
                             idn, evac="vector")

            @block.tensor
            def _(te):
                ctx.gate(te)
                ins = None
                for qb in range(4):
                    for ci in range(C8):
                        ins = te.matmul(psALL[:, qb * 512:(qb + 1) * 512],
                                        lnT[:, ci * 128:(ci + 1) * 128],
                                        wqkv[:, ci * 3 * C + qb * 512: ci * 3 * C + (qb + 1) * 512],
                                        start=(ci == 0), stop=(ci == C8 - 1))
                ctx.inc(ins)

            for part, dst in ((0, None), (1, None)):
                @block.vector
                def _(v, part=part):
                    ctx.gate(v)
                    ctx.inc(v.tensor_scalar(tmpf[:], psALL[:, part * C:(part + 1) * C],
                                            0.0, None, ALU.min))

                @block.scalar
                def _(s):
                    ctx.gate(s)
                    ctx.inc(s.activation(trash[:], tmpf[:], AF.Exp))

                @block.vector
                def _(v, part=part, t=t):
                    ctx.gate(v)
                    out = Qbuf[:, t * C:(t + 1) * C] if part == 0 else k1[:]
                    ctx.inc(v.scalar_tensor_tensor(out, psALL[:, part * C:(part + 1) * C],
                                                   0.0, trash[:], ALU.max, ALU.add))

            @block.tensor
            def _(te):
                ctx.gate(te)
                ins = None
                for qb in range(2):
                    for ci in range(C8):
                        ins = te.matmul(psALL[:, qb * 512:(qb + 1) * 512],
                                        lnT[:, ci * 128:(ci + 1) * 128],
                                        wqkv[:, ci * 3 * C + 2 * C + qb * 512: ci * 3 * C + 2 * C + (qb + 1) * 512],
                                        start=(ci == 0), stop=(ci == C8 - 1))
                ctx.inc(ins)

            @block.scalar
            def _(s):
                ctx.gate(s)
                ctx.inc(s.copy(v1[:], psALL[:, 0:C]))

            @block.tensor
            def _(te, t=t):
                ctx.gate(te)
                ins = None
                for h in range(H):
                    ins = te.matmul(psKV[0:64, h * 64:(h + 1) * 64],
                                    k1[:, h * 64:(h + 1) * 64],
                                    v1[:, h * 64:(h + 1) * 64],
                                    start=(t == 0 and h % 8 == 0),
                                    stop=(t == T - 1))
                for h in range(H):
                    ins = te.matmul(psKV[0:64, 1024 + h:1025 + h],
                                    k1[:, h * 64:(h + 1) * 64],
                                    ones_c[:],
                                    start=(t == 0 and h == 0), stop=(t == T - 1))
                ctx.inc(ins)

        @block.scalar
        def _(s):
            ctx.gate(s)
            ctx.inc(s.copy(kvb[:], psKV[0:64, 0:C]))
            ctx.gate(s)
            ctx.inc(s.copy(ksb[:], psKV[0:64, 1024:1040]))

        if "dbgD" in d:
            @block.gpsimd
            def _(g):
                ctx.gate(g)
                ctx.inc(g.dma_start(out=d["dbgD"][0:64, 0:C], in_=kvb[:]), 16)
                ctx.inc(g.dma_start(out=d["dbgD"][64:128, 0:16], in_=ksb[:]), 16)
                ctx.inc(g.dma_start(out=d["dbgD"][64:128, 16:16 + C],
                                    in_=Qbuf[:, 0:C][64:128, :]), 16)

        for t in range(T):
            emit_transposes3(ctx,
                             [Qbuf[:, t * C + h * 64: t * C + (h + 1) * 64] for h in range(H)],
                             psT,
                             qT[0:64, :].rearrange("q (h x) -> q h x", x=128),
                             idn, evac="vector")

            @block.tensor
            def _(te):
                ctx.gate(te)
                ins = None
                for h in range(H):
                    ins = te.matmul(psALL[:, h * 64:(h + 1) * 64],
                                    qT[0:64, h * 128:(h + 1) * 128],
                                    kvb[0:64, h * 64:(h + 1) * 64],
                                    start=True, stop=True)
                for h in range(H):
                    ins = te.matmul(psALL[:, 1024 + h:1025 + h],
                                    qT[0:64, h * 128:(h + 1) * 128],
                                    ksb[0:64, h:h + 1],
                                    start=True, stop=True)
                ctx.inc(ins)

            @block.vector
            def _(v):
                ctx.gate(v)
                ctx.inc(v.tensor_scalar(nrm[:, 0:16], psALL[:, 1024:1040], 1e-6,
                                        None, ALU.add))
                ctx.gate(v)
                ctx.inc(v.reciprocal(nrm[:, 16:32], nrm[:, 0:16]))

            for h in range(H):
                @block.vector
                def _(v, h=h):
                    ctx.gate(v)
                    ctx.inc(v.tensor_scalar(y1n[:, h * 64:(h + 1) * 64],
                                            psALL[:, h * 64:(h + 1) * 64],
                                            nrm[:, 16 + h:17 + h], None, ALU.mult))

            emit_transposes3(ctx, [y1n[:, c * 128:(c + 1) * 128] for c in range(C8)],
                             psT,
                             y1t.rearrange("p (c x) -> p c x", c=C8),
                             idn, evac="vector")

            @block.sync
            def _(sy, t=t):
                ctx.gate(sy)
                ctx.inc(sy.dma_start(out=x2t[:], in_=d["x2"][t * 128:(t + 1) * 128, :]), 16)

            for ob in range(2):
                @block.tensor
                def _(te, ob=ob):
                    ctx.gate(te)
                    ins = None
                    for ci in range(C8):
                        ins = te.matmul(psALL[:, 1536:2048],
                                        y1t[:, ci * 128:(ci + 1) * 128],
                                        wo[:, ci * C + ob * 512: ci * C + (ob + 1) * 512],
                                        start=(ci == 0), stop=(ci == C8 - 1))
                    ctx.inc(ins)

                @block.vector
                def _(v, ob=ob):
                    ctx.gate(v)
                    ctx.inc(v.scalar_tensor_tensor(
                        trash[:, ob * 512:(ob + 1) * 512], psALL[:, 1536:2048], 1.0,
                        x2t[:, ob * 512:(ob + 1) * 512], ALU.mult, ALU.add))

            @block.sync
            def _(sy, t=t):
                ctx.gate(sy)
                ctx.inc(sy.dma_start(out=d["x3"][t * 128:(t + 1) * 128, :],
                                     in_=trash[:]), 16)


def declare_stageD(nc, S, kind_in="ExternalInput", kind_out="ExternalOutput",
                   act_kind=None):
    return {
        "x2": nc.dram_tensor("x2d", [S, C], F32, kind=act_kind or kind_in).ap(),
        "Wqkv": nc.dram_tensor("Wqkv", [C, 3 * C], F32, kind=kind_in).ap(),
        "Wo": nc.dram_tensor("Wo", [C, C], F32, kind=kind_in).ap(),
        "idn": nc.dram_tensor("idn_d", [128, 128], BF16, kind=kind_in).ap(),
        "x3": nc.dram_tensor("x3", [S, C], F32, kind=kind_out).ap(),
    }


def build_stageD(S):
    nc = bass.Bass("TRN2", target_bir_lowering=False)
    d = declare_stageD(nc, S)
    with ExitStack() as es:
        ctx = Ctx(nc, es)
        emit_stageD(ctx, S, d)

        @ctx.block.sync
        def _(sy):
            ctx.gate(sy)
    return nc


def emit_stageE(ctx, S, d):
    nc = ctx.nc
    block = ctx.block
    T = S // 128

    es = ExitStack()
    with es:
        wr = es.enter_context(nc.sbuf_tensor("E_wr", [128, C8 * C], BF16))
        wk = es.enter_context(nc.sbuf_tensor("E_wk", [128, C8 * C], BF16))
        wv = es.enter_context(nc.sbuf_tensor("E_wv", [128, C8 * C], BF16))
        mur = es.enter_context(nc.sbuf_tensor("E_mur", [128, C], F32))
        muk = es.enter_context(nc.sbuf_tensor("E_muk", [128, C], F32))
        mur1 = es.enter_context(nc.sbuf_tensor("E_mur1", [128, C], F32))
        muk1 = es.enter_context(nc.sbuf_tensor("E_muk1", [128, C], F32))
        x3t = es.enter_context(nc.sbuf_tensor("E_x3t", [128, C], F32))
        x3p = es.enter_context(nc.sbuf_tensor("E_x3p", [128, C], F32))
        ln3 = es.enter_context(nc.sbuf_tensor("E_ln3", [128, C], F32))
        ln3p = es.enter_context(nc.sbuf_tensor("E_ln3p", [128, C], F32))
        trash = es.enter_context(nc.sbuf_tensor("E_tr", [128, C], F32))
        tmpf = es.enter_context(nc.sbuf_tensor("E_tmpf", [128, C], F32))
        xb = es.enter_context(nc.sbuf_tensor("E_xb", [128, C], BF16))
        xbT = es.enter_context(nc.sbuf_tensor("E_xbT", [128, C], BF16))
        rs = es.enter_context(nc.sbuf_tensor("E_rs", [128, C], F32))
        cols = es.enter_context(nc.sbuf_tensor("E_cols", [128, 16], F32))
        idn = es.enter_context(nc.sbuf_tensor("E_idn", [128, 128], BF16))
        psM = es.enter_context(nc.psum_tensor("E_psM", [128, 1024], F32))
        psT = es.enter_context(nc.psum_tensor("E_psT", [128, 1024], BF16))

        @block.gpsimd
        def _(g):
            ctx.gate(g)
            dcast_load_w(ctx, g, wr, d["WrT"])
            dcast_load_w(ctx, g, wk, d["WkT"])
            dcast_load_w(ctx, g, wv, d["WvT"])

        @block.sync
        def _(sy):
            ctx.gate(sy)
            ctx.inc(sy.dma_start(out=idn[:], in_=d["idn"]), 16)
            ctx.inc(sy.dma_start(out=mur[:], in_=d["mu_r_r"]), 16)
            ctx.inc(sy.dma_start(out=muk[:], in_=d["mu_k_r"]), 16)

        @block.vector
        def _(v):
            ctx.gate(v)
            v.tensor_scalar(mur1[:], mur[:], -1.0, 1.0, ALU.mult, ALU.add)
            ctx.inc(v.tensor_scalar(muk1[:], muk[:], -1.0, 1.0, ALU.mult, ALU.add))

        for t in range(T):
            @block.sync
            def _(sy, t=t):
                ctx.gate(sy)
                ctx.inc(sy.dma_start(out=x3t[:], in_=d["x3"][t * 128:(t + 1) * 128, :]), 16)
                if t == 0:
                    ctx.inc(sy.dma_start(out=x3p[1:128, :], in_=d["x3"][0:127, :]), 16)
                else:
                    ctx.inc(sy.dma_start(out=x3p[:],
                                         in_=d["x3"][t * 128 - 1: (t + 1) * 128 - 1, :]), 16)

            if t == 0:
                @block.vector
                def _(v):
                    ctx.gate(v)
                    ctx.inc(v.memset(x3p[0:1, :], 0.0))

            emit_ln(ctx, x3t[:], ln3[:], trash, cols)
            emit_ln(ctx, x3p[:], ln3p[:], trash, cols)

            def mix_project(mu, mu1, w_sb, label):
                @block.vector
                def _(v, mu=mu, mu1=mu1):
                    ctx.gate(v)
                    ctx.inc(v.tensor_tensor(trash[:], ln3[:], mu, ALU.mult))
                    ctx.gate(v)
                    ctx.inc(v.tensor_tensor(tmpf[:], ln3p[:], mu1, ALU.mult))
                    ctx.gate(v)
                    ctx.inc(v.tensor_tensor(xb[:], trash[:], tmpf[:], ALU.add))
                emit_transposes3(ctx, [xb[:, c * 128:(c + 1) * 128] for c in range(C8)],
                                 psT,
                                 xbT.rearrange("p (c x) -> p c x", c=C8),
                                 idn, evac="vector")

                @block.tensor
                def _(te, w_sb=w_sb):
                    ctx.gate(te)
                    ins = None
                    for ob in range(2):
                        for ci in range(C8):
                            ins = te.matmul(psM[:, ob * 512:(ob + 1) * 512],
                                            xbT[:, ci * 128:(ci + 1) * 128],
                                            w_sb[:, ci * C + ob * 512: ci * C + (ob + 1) * 512],
                                            start=(ci == 0), stop=(ci == C8 - 1))
                    ctx.inc(ins)

            mix_project(mur[:], mur1[:], wr, "r")

            @block.scalar
            def _(s):
                ctx.gate(s)
                ctx.inc(s.activation(rs[:], psM[:, 0:C], AF.Sigmoid))

            mix_project(muk[:], muk1[:], wk, "k")

            @block.vector
            def _(v):
                ctx.gate(v)
                ctx.inc(v.tensor_scalar(tmpf[:], psM[:, 0:C], 0.0, None, ALU.max))

            @block.scalar
            def _(s):
                ctx.gate(s)
                ctx.inc(s.activation(xb[:], tmpf[:], AF.Square))

            emit_transposes3(ctx, [xb[:, c * 128:(c + 1) * 128] for c in range(C8)],
                             psT,
                             xbT.rearrange("p (c x) -> p c x", c=C8),
                             idn, evac="vector")

            @block.tensor
            def _(te):
                ctx.gate(te)
                ins = None
                for ob in range(2):
                    for ci in range(C8):
                        ins = te.matmul(psM[:, ob * 512:(ob + 1) * 512],
                                        xbT[:, ci * 128:(ci + 1) * 128],
                                        wv[:, ci * C + ob * 512: ci * C + (ob + 1) * 512],
                                        start=(ci == 0), stop=(ci == C8 - 1))
                ctx.inc(ins)

            @block.vector
            def _(v):
                ctx.gate(v)
                ctx.inc(v.tensor_tensor(tmpf[:], rs[:], psM[:, 0:C], ALU.mult))
                ctx.gate(v)
                ctx.inc(v.tensor_tensor(tmpf[:], tmpf[:], x3t[:], ALU.add))

            @block.sync
            def _(sy, t=t):
                ctx.gate(sy)
                ctx.inc(sy.dma_start(out=d["y"][t * 128:(t + 1) * 128, :],
                                     in_=tmpf[:]), 16)


def declare_stageE(nc, S, kind_in="ExternalInput", kind_out="ExternalOutput",
                   act_kind=None):
    return {
        "x3": nc.dram_tensor("x3e", [S, C], F32, kind=act_kind or kind_in).ap(),
        "WrT": nc.dram_tensor("WrT", [C, C], F32, kind=kind_in).ap(),
        "WkT": nc.dram_tensor("WkT", [C, C], F32, kind=kind_in).ap(),
        "WvT": nc.dram_tensor("WvT", [C, C], F32, kind=kind_in).ap(),
        "mu_r_r": nc.dram_tensor("mu_r_r", [128, C], F32, kind=kind_in).ap(),
        "mu_k_r": nc.dram_tensor("mu_k_r", [128, C], F32, kind=kind_in).ap(),
        "idn": nc.dram_tensor("idn_e", [128, 128], BF16, kind=kind_in).ap(),
        "y": nc.dram_tensor("y", [S, C], F32, kind=kind_out).ap(),
    }


def prep_stageDE_inputs(Wqkv, Wo, W_r, W_k, W_v, mu_r, mu_k):
    f = np.float32
    return {
        "Wqkv": np.ascontiguousarray(np.asarray(Wqkv, f)),
        "Wo": np.ascontiguousarray(np.asarray(Wo, f)),
        "idn_d": np.eye(128).astype(ml_dtypes.bfloat16),
        "WrT": np.ascontiguousarray(np.asarray(W_r, f).T),
        "WkT": np.ascontiguousarray(np.asarray(W_k, f).T),
        "WvT": np.ascontiguousarray(np.asarray(W_v, f).T),
        "mu_r_r": np.ascontiguousarray(np.broadcast_to(np.asarray(mu_r, f), (128, C))),
        "mu_k_r": np.ascontiguousarray(np.broadcast_to(np.asarray(mu_k, f), (128, C))),
        "idn_e": np.eye(128).astype(ml_dtypes.bfloat16),
    }


def build_stageE(S):
    nc = bass.Bass("TRN2", target_bir_lowering=False)
    d = declare_stageE(nc, S)
    with ExitStack() as es:
        ctx = Ctx(nc, es)
        emit_stageE(ctx, S, d)

        @ctx.block.sync
        def _(sy):
            ctx.gate(sy)
    return nc


def declare_fused(nc, S, K_cap=None):
    HF = 4096
    I = "ExternalInput"
    T = S // 128
    if K_cap is None:
        K_cap = 2048 if S >= 4096 else 256
    x1s = nc.dram_tensor("x1", [S, C], F32, kind="Internal").ap()
    a_s = nc.dram_tensor("a", [S, C], F32, kind="Internal").ap()
    x2s = nc.dram_tensor("x2", [S, C], F32, kind="Internal").ap()
    x3s = nc.dram_tensor("x3", [S, C], F32, kind="Internal").ap()
    y = nc.dram_tensor("y", [S, C], F32, kind="ExternalOutput").ap()
    dA = {
        "x0": nc.dram_tensor("x0", [S, C], F32, kind=I).ap(),
        "wq": nc.dram_tensor("wq", [C, C], F32, kind=I).ap(),
        "wk": nc.dram_tensor("wk", [C, C], F32, kind=I).ap(),
        "wv": nc.dram_tensor("wv", [C, C], F32, kind=I).ap(),
        "wo": nc.dram_tensor("wo", [C, C], F32, kind=I).ap(),
        "bq_pc": nc.dram_tensor("bq_pc", [128, 8], F32, kind=I).ap(),
        "bk_pc": nc.dram_tensor("bk_pc", [128, 8], F32, kind=I).ap(),
        "bv_row": nc.dram_tensor("bv_row", [1, C], F32, kind=I).ap(),
        "bo_row": nc.dram_tensor("bo_row", [1, C], F32, kind=I).ap(),
        "idn": nc.dram_tensor("idn", [128, 128], BF16, kind=I).ap(),
        "dmA": nc.dram_tensor("dmA", [128, 256], F32, kind=I).ap(),
        "x1": x1s,
    }
    dB = {
        "x1": x1s,
        "qkvT": nc.dram_tensor("qkvT", [C, 3 * C], F32, kind=I).ap(),
        "projT": nc.dram_tensor("projT", [C, C], F32, kind=I).ap(),
        "n1w_r": nc.dram_tensor("n1w_r", [128, C], F32, kind=I).ap(),
        "cosb": nc.dram_tensor("cosb", [S, C], BF16, kind=I).ap(),
        "sinb": nc.dram_tensor("sinb", [S, C], BF16, kind=I).ap(),
        "idn": nc.dram_tensor("idn_b", [128, 128], BF16, kind=I).ap(),
        "dmB": nc.dram_tensor("dmB", [128, 128], F32, kind=I).ap(),
        "dmB2": nc.dram_tensor("dmB2", [128, 512], F32, kind=I).ap(),
        "a": a_s,
    }
    dC = {
        "x1": x1s,
        "a": a_s,
        "n2w_r": nc.dram_tensor("n2w_r", [128, C], F32, kind=I).ap(),
        "rwT": nc.dram_tensor("rwT", [C, 8], F32, kind=I).ap(),
        "wgT": nc.dram_tensor("wgT", [8, C, HF], F32, kind=I).ap(),
        "wuT": nc.dram_tensor("wuT", [8, C, HF], F32, kind=I).ap(),
        "wdT": nc.dram_tensor("wdT", [8, HF, C], F32, kind=I).ap(),
        "idn": nc.dram_tensor("idn_c", [128, 128], BF16, kind=I).ap(),
        "idn_f": nc.dram_tensor("idn_f", [128, 128], F32, kind=I).ap(),
        "wgT_bf": nc.dram_tensor("wgT_bf", [8, C, HF], BF16, kind="Internal").ap(),
        "wuT_bf": nc.dram_tensor("wuT_bf", [8, C, HF], BF16, kind="Internal").ap(),
        "wdT_bf": nc.dram_tensor("wdT_bf", [8, HF, C], BF16, kind="Internal").ap(),
        "x2": x2s,
        "K_cap": K_cap,
        "Lsb": nc.dram_tensor("Lsb", [128, 128], F32, kind=I).ap(),
        "ebrow": nc.dram_tensor("ebrow", [1, 8 * T], F32, kind=I).ap(),
        "r2d": nc.dram_tensor("r2d", [S, C], BF16, kind="Internal").ap(),
        "ebatch": nc.dram_tensor("ebatch", [8 * K_cap, C], BF16, kind="Internal").ap(),
        "ydram": nc.dram_tensor("ydram", [8 * K_cap, C], F32, kind="Internal").ap(),
    }
    dD = {
        "x2": x2s,
        "Wqkv": nc.dram_tensor("Wqkv", [C, 3 * C], F32, kind=I).ap(),
        "Wo": nc.dram_tensor("Wo", [C, C], F32, kind=I).ap(),
        "idn": nc.dram_tensor("idn_d", [128, 128], BF16, kind=I).ap(),
        "x3": x3s,
    }
    dE = {
        "x3": x3s,
        "WrT": nc.dram_tensor("WrT", [C, C], F32, kind=I).ap(),
        "WkT": nc.dram_tensor("WkT", [C, C], F32, kind=I).ap(),
        "WvT": nc.dram_tensor("WvT", [C, C], F32, kind=I).ap(),
        "mu_r_r": nc.dram_tensor("mu_r_r", [128, C], F32, kind=I).ap(),
        "mu_k_r": nc.dram_tensor("mu_k_r", [128, C], F32, kind=I).ap(),
        "idn": nc.dram_tensor("idn_e", [128, 128], BF16, kind=I).ap(),
        "y": y,
    }
    return dA, dB, dC, dD, dE


class StageCtx:

    def __init__(self, nc, block, sem, prev=None):
        self.nc = nc
        self.block = block
        self.chain = sem
        self.n = 0
        if prev is not None:
            psem, pval = prev
            for name in ("sync", "gpsimd", "tensor", "vector", "scalar"):
                @getattr(block, name)
                def _(eng, psem=psem, pval=pval):
                    eng.wait_ge(psem, pval)

    def inc(self, instr, k=1):
        instr.then_inc(self.chain, k)
        self.n += k
        return instr

    def gate(self, eng, val=None):
        eng.wait_ge(self.chain, self.n if val is None else val)


def build_fused(S):
    nc = bass.Bass("TRN2", target_bir_lowering=False)
    dA, dB, dC, dD, dE = declare_fused(nc, S)
    with ExitStack() as es:
        sems = [es.enter_context(nc.semaphore(f"chain{i}")) for i in range(5)]
        s_cast = es.enter_context(nc.semaphore("s_cast"))
        block = es.enter_context(nc.Block())

        @block.gpsimd
        def _(g):
            for e in range(8):
                g.dma_start(out=dC["wgT_bf"][e], in_=dC["wgT"][e]).then_inc(s_cast, 16)
                g.dma_start(out=dC["wuT_bf"][e], in_=dC["wuT"][e]).then_inc(s_cast, 16)
                g.dma_start(out=dC["wdT_bf"][e], in_=dC["wdT"][e]).then_inc(s_cast, 16)
        dC["cast_done"] = (s_cast, 384)
        prev = None
        for i, (emit, d) in enumerate(((emit_stageA, dA), (emit_stageB, dB),
                                       (emit_stageC_sparse, dC), (emit_stageD, dD),
                                       (emit_stageE, dE))):
            ctx = StageCtx(nc, block, sems[i], prev=prev)
            emit(ctx, S, d)
            prev = (sems[i], ctx.n)

        @block.sync
        def _(sy):
            sy.wait_ge(prev[0], prev[1])
    return nc


def emit_stageC_sparse(ctx, S, d):
    nc = ctx.nc
    block = ctx.block
    T = S // 128
    E = 8
    K_cap = d["K_cap"]
    HKC = K_cap // 2
    BT = HKC // 128
    SUBS = [(i * 512, min(512, HKC - i * 512)) for i in range((HKC + 511) // 512)]

    es = ExitStack()
    with es:
        hbuf = es.enter_context(nc.sbuf_tensor("C_h", [128, 32 * HKC], BF16))
        r2bT = es.enter_context(nc.sbuf_tensor("C_r2bT", [128, C8 * HKC], BF16))
        wgc = [es.enter_context(nc.sbuf_tensor(f"C_wg{i}", [128, C8 * 512], BF16))
               for i in range(2)]
        wuc = [es.enter_context(nc.sbuf_tensor(f"C_wu{i}", [128, C8 * 512], BF16))
               for i in range(2)]
        wdc = [es.enter_context(nc.sbuf_tensor(f"C_wd{i}", [128, 8 * C], BF16))
               for i in range(2)]
        s_guld = es.enter_context(nc.semaphore("C_guld"))
        s_guus = es.enter_context(nc.semaphore("C_guus"))
        s_wdld = es.enter_context(nc.semaphore("C_wdld"))
        s_wdus = es.enter_context(nc.semaphore("C_wdus"))
        sP = es.enter_context(nc.semaphore("C_sP"))
        sA = es.enter_context(nc.semaphore("C_sA"))
        sD = es.enter_context(nc.semaphore("C_sD"))
        s_bld = es.enter_context(nc.semaphore("C_bld"))
        s_bus = es.enter_context(nc.semaphore("C_bus"))
        s_tpa = es.enter_context(nc.semaphore("C_tpa"))
        s_tpb = es.enter_context(nc.semaphore("C_tpb"))
        r2c = es.enter_context(nc.sbuf_tensor("C_r2c", [128, C], BF16))
        rw = es.enter_context(nc.sbuf_tensor("C_rw", [128, C8 * 8], F32))
        n2w = es.enter_context(nc.sbuf_tensor("C_n2w", [128, C], BF16))
        n2wf = es.enter_context(nc.sbuf_tensor("C_n2wf", [128, C], F32))
        r2f = es.enter_context(nc.sbuf_tensor("C_r2f", [128, C], F32))
        r2b = es.enter_context(nc.sbuf_tensor("C_r2b", [128, C], BF16))
        r2Tf = es.enter_context(nc.sbuf_tensor("C_r2Tf", [128, C], F32))
        idnf = es.enter_context(nc.sbuf_tensor("C_idnf", [128, 128], F32))
        at = es.enter_context(nc.sbuf_tensor("C_at", [128, C], F32))
        gtmp = es.enter_context(nc.sbuf_tensor("C_gtmp", [128, C], BF16))
        trash = es.enter_context(nc.sbuf_tensor("C_tr", [128, C], F32))
        ybuf = es.enter_context(nc.sbuf_tensor("C_yb", [128, C], F32))
        lg = es.enter_context(nc.sbuf_tensor("C_lg", [128, 8], F32))
        l2 = es.enter_context(nc.sbuf_tensor("C_l2", [128, 8], F32))
        msk1 = es.enter_context(nc.sbuf_tensor("C_m1", [128, 8], F32))
        cols = es.enter_context(nc.sbuf_tensor("C_cols", [128, 16], F32))
        idn = es.enter_context(nc.sbuf_tensor("C_idn", [128, 128], BF16))
        Lsb = es.enter_context(nc.sbuf_tensor("C_Lsb", [128, 128], BF16))
        selB = es.enter_context(nc.sbuf_tensor("C_selB", [128, 8 * T], F32))
        cumB = es.enter_context(nc.sbuf_tensor("C_cumB", [128, 8 * T], F32))
        mk1B = es.enter_context(nc.sbuf_tensor("C_mk1B", [128, 8 * T], F32))
        mk2B = es.enter_context(nc.sbuf_tensor("C_mk2B", [128, 8 * T], F32))
        gwB = es.enter_context(nc.sbuf_tensor("C_gwB", [128, 2 * T], F32))
        gixB = es.enter_context(nc.sbuf_tensor("C_gix", [128, 2 * T], I32))
        totrow = es.enter_context(nc.sbuf_tensor("C_tot", [1, 3 * 8 * T], F32))
        selbb = es.enter_context(nc.sbuf_tensor("C_selbb", [128, 8], BF16))
        ones1 = es.enter_context(nc.sbuf_tensor("C_ones1", [1, 128], BF16))
        onescb = es.enter_context(nc.sbuf_tensor("C_onescb", [128, 1], BF16))
        ones1f = es.enter_context(nc.sbuf_tensor("C_ones1f", [1, 128], F32))
        ebsb = es.enter_context(nc.sbuf_tensor("C_ebsb", [1, 8 * T], F32))
        psALL = es.enter_context(nc.psum_tensor("C_psA", [128, 3584], F32))
        psT = es.enter_context(nc.psum_tensor("C_psT", [128, 1024], BF16))
        y2buf = trash
        x1t = r2f
        psG = psALL[:, 0:512]
        psU = psALL[:, 512:1024]
        psG2 = psALL[:, 1024:1536]
        psU2 = psALL[:, 1536:2048]
        psR = psALL[:, 3072:3080]
        psC = psALL[:, 3080:3088]
        psTt = psALL[:, 3088:3096]
        psO = psALL[:, 2560:2568]

        NT8 = 8 * T

        @block.gpsimd
        def _(g):
            ctx.gate(g)
            if "cast_done" not in d:
                for e in range(E):
                    ctx.inc(g.dma_start(out=d["wgT_bf"][e], in_=d["wgT"][e]), 16)
                    ctx.inc(g.dma_start(out=d["wuT_bf"][e], in_=d["wuT"][e]), 16)
                    ctx.inc(g.dma_start(out=d["wdT_bf"][e], in_=d["wdT"][e]), 16)
            dcast_load(ctx, g, n2w[:], d["n2w_r"])
            dcast_load(ctx, g, Lsb[:], d["Lsb"])
            g.memset(ones1[:], 1.0)
            g.memset(onescb[:], 1.0)
            ctx.inc(g.memset(ones1f[:], 1.0))

        @block.sync
        def _(sy):
            ctx.gate(sy)
            src = d["rwT"].rearrange("(c p) o -> p c o", p=128)
            ctx.inc(sy.dma_start(out=rw.rearrange("p (c o) -> p c o", c=C8), in_=src), 16)
            ctx.inc(sy.dma_start(out=n2wf[:], in_=d["n2w_r"]), 16)
            ctx.inc(sy.dma_start(out=idnf[:], in_=d["idn_f"]), 16)
            ctx.inc(sy.dma_start(out=idn[:], in_=d["idn"]), 16)
            ctx.inc(sy.dma_start(out=ebsb[:], in_=d["ebrow"]), 16)

        for t in range(T):
            @block.sync
            def _(sy, t=t):
                ctx.gate(sy)
                ctx.inc(sy.dma_start(out=at[:], in_=d["a"][t * 128:(t + 1) * 128, :]), 16)

            @block.scalar
            def _(s):
                ctx.gate(s)
                ctx.inc(s.activation(trash[:], at[:], AF.Square, accum_out=cols[:, 0:1]))

            @block.vector
            def _(v):
                ctx.gate(v)
                ctx.inc(v.tensor_scalar(cols[:, 1:2], cols[:, 0:1], 1.0 / C, None, ALU.mult))
                ctx.gate(v)
                ctx.inc(v.tensor_scalar(cols[:, 1:2], cols[:, 1:2], 1e-6, None, ALU.add))

            @block.scalar
            def _(s):
                ctx.gate(s)
                ctx.inc(s.activation(cols[:, 2:3], cols[:, 1:2], AF.Ln))
                ctx.gate(s)
                ctx.inc(s.activation(cols[:, 2:3], cols[:, 2:3], AF.Exp, scale=-0.5))

            @block.vector
            def _(v):
                ctx.gate(v)
                ctx.inc(v.tensor_scalar(trash[:], at[:], cols[:, 2:3], None, ALU.mult))
                ctx.gate(v)
                ctx.inc(v.tensor_tensor(r2f[:], trash[:], n2wf[:], ALU.mult))
                ctx.gate(v)
                ctx.inc(v.tensor_copy(r2b[:], r2f[:]))

            @block.sync
            def _(sy, t=t):
                ctx.gate(sy)
                ctx.inc(sy.dma_start(out=d["r2d"][t * 128:(t + 1) * 128, :],
                                     in_=r2b[:]), 16)

            emit_transposes3(ctx, [r2f[:, c * 128:(c + 1) * 128] for c in range(C8)],
                             psALL[:, 2048:2560],
                             r2Tf.rearrange("p (c x) -> p c x", c=C8),
                             idnf, per_batch=4, evac="vector")

            @block.tensor
            def _(te):
                ctx.gate(te)
                ins = None
                for ci in range(C8):
                    ins = te.matmul(psR, r2Tf[:, ci * 128:(ci + 1) * 128],
                                    rw[:, ci * 8:(ci + 1) * 8],
                                    start=(ci == 0), stop=(ci == C8 - 1))
                ctx.inc(ins)

            @block.scalar
            def _(s):
                ctx.gate(s)
                ctx.inc(s.copy(lg[:], psR))

            @block.vector
            def _(v, t=t):
                ctx.gate(v)
                ctx.inc(v.reduce_max(cols[:, 4:5], lg[:], axis=AX.X))
                ctx.gate(v)
                ctx.inc(v.tensor_scalar(mk1B[:, t * 8:(t + 1) * 8], lg[:], cols[:, 4:5],
                                        None, ALU.is_ge))
                ctx.gate(v)
                ctx.inc(v.scalar_tensor_tensor(l2[:], mk1B[:, t * 8:(t + 1) * 8], -1e30,
                                               lg[:], ALU.mult, ALU.add))
                ctx.gate(v)
                ctx.inc(v.reduce_max(cols[:, 5:6], l2[:], axis=AX.X))
                ctx.gate(v)
                ctx.inc(v.tensor_tensor(cols[:, 6:7], cols[:, 5:6], cols[:, 4:5],
                                        ALU.subtract))

            @block.scalar
            def _(s):
                ctx.gate(s)
                ctx.inc(s.activation(cols[:, 7:8], cols[:, 6:7], AF.Exp))

            @block.vector
            def _(v, t=t):
                ctx.gate(v)
                ctx.inc(v.tensor_scalar(cols[:, 8:9], cols[:, 7:8], 1.0, None, ALU.add))
                ctx.gate(v)
                ctx.inc(v.reciprocal(gwB[:, t * 2:t * 2 + 1], cols[:, 8:9]))
                ctx.gate(v)
                ctx.inc(v.tensor_tensor(gwB[:, t * 2 + 1:t * 2 + 2], cols[:, 7:8],
                                        gwB[:, t * 2:t * 2 + 1], ALU.mult))
                ctx.gate(v)
                ctx.inc(v.tensor_scalar(mk2B[:, t * 8:(t + 1) * 8], l2[:], cols[:, 5:6],
                                        None, ALU.is_ge))
                ctx.gate(v)
                ctx.inc(v.tensor_tensor(selB[:, t * 8:(t + 1) * 8],
                                        mk1B[:, t * 8:(t + 1) * 8],
                                        mk2B[:, t * 8:(t + 1) * 8], ALU.add))
                ctx.gate(v)
                ctx.inc(v.tensor_copy(selbb[:], selB[:, t * 8:(t + 1) * 8]))

            @block.tensor
            def _(te):
                ctx.gate(te)
                te.matmul(psC, Lsb[:], selbb[:], start=True, stop=True)
                ctx.inc(te.matmul(psTt[0:1, :], onescb[:], selbb[:], start=True, stop=True))

            @block.scalar
            def _(s, t=t):
                ctx.gate(s)
                ctx.inc(s.copy(cumB[:, t * 8:(t + 1) * 8], psC))
                ctx.gate(s)
                ctx.inc(s.copy(totrow[0:1, t * 8:(t + 1) * 8], psTt[0:1, :]))

        @block.vector
        def _(v):
            ctx.gate(v)
            ctx.inc(v.memset(totrow[0:1, NT8:NT8 + 8], 0.0))
            ctx.gate(v)
            ctx.inc(v.tensor_copy(totrow[0:1, NT8 + 8:2 * NT8], totrow[0:1, 0:NT8 - 8]))
            s_off = NT8
            step = 8
            steps = []
            while step < NT8:
                steps.append(step)
                step *= 2
            for step in steps:
                d_off = 2 * NT8 if s_off == NT8 else NT8
                ctx.gate(v)
                ctx.inc(v.tensor_copy(totrow[0:1, d_off:d_off + step],
                                      totrow[0:1, s_off:s_off + step]))
                ctx.gate(v)
                ctx.inc(v.tensor_tensor(totrow[0:1, d_off + step:d_off + NT8],
                                        totrow[0:1, s_off + step:s_off + NT8],
                                        totrow[0:1, s_off:s_off + NT8 - step], ALU.add))
                s_off = d_off
            ctx.gate(v)
            ctx.inc(v.tensor_tensor(totrow[0:1, s_off:s_off + NT8],
                                    totrow[0:1, s_off:s_off + NT8],
                                    ebsb[0:1, :], ALU.add))
            ctx.sparse_scan_off = s_off

        for t in range(T):
            @block.tensor
            def _(te, t=t):
                ctx.gate(te)
                so = ctx.sparse_scan_off
                ctx.inc(te.matmul(psO, ones1f[0:1, :],
                                  totrow[0:1, so + t * 8: so + (t + 1) * 8],
                                  start=True, stop=True))

            @block.vector
            def _(v, t=t):
                ctx.gate(v)
                ctx.inc(v.scalar_tensor_tensor(cols[:, 0:8], psO, 1.0,
                                               cumB[:, t * 8:(t + 1) * 8],
                                               ALU.mult, ALU.add))
                ctx.gate(v)
                ctx.inc(v.tensor_tensor(cols[:, 8:16], cols[:, 0:8],
                                        selB[:, t * 8:(t + 1) * 8], ALU.mult))
                ctx.gate(v)
                ctx.inc(v.tensor_tensor(lg[:], mk1B[:, t * 8:(t + 1) * 8],
                                        cols[:, 0:8], ALU.mult))
                ctx.gate(v)
                ctx.inc(v.reduce_sum(cols[:, 12:13], lg[:], axis=AX.X))
                ctx.gate(v)
                ctx.inc(v.tensor_tensor(lg[:], mk2B[:, t * 8:(t + 1) * 8],
                                        cols[:, 0:8], ALU.mult))
                ctx.gate(v)
                ctx.inc(v.reduce_sum(cols[:, 13:14], lg[:], axis=AX.X))
                ctx.gate(v)
                ctx.inc(v.tensor_copy(gixB[:, t * 2:t * 2 + 2], cols[:, 12:14]))

            @block.sync
            def _(sy, t=t):
                ctx.gate(sy)
                ctx.inc(sy.dma_start(out=r2b[:], in_=d["r2d"][t * 128:(t + 1) * 128, :]), 16)

            @block.gpsimd
            def _(g, t=t):
                ctx.gate(g)
                ctx.inc(g.indirect_dma_start(
                    d["ebatch"],
                    IndirectOffsetOnAxis(ap=gixB[:, t * 2:t * 2 + 1], axis=0),
                    r2b[:], None, oob_is_err=False), 16)
                ctx.inc(g.indirect_dma_start(
                    d["ebatch"],
                    IndirectOffsetOnAxis(ap=gixB[:, t * 2 + 1:t * 2 + 2], axis=0),
                    r2b[:], None, oob_is_err=False), 16)

        if d.get("stop_after") == "S":
            return
        gu_idx = 0
        n_tsg = (BT + 2) // 3
        wd_idx = 0

        @block.sync
        def _(sy):
            ctx.gate(sy)
            if "cast_done" in d:
                sy.wait_ge(d["cast_done"][0], d["cast_done"][1])
            for e8 in range(E * 2):
                e, bh = divmod(e8, 2)
                base = e * K_cap + bh * HKC
                for bt in range(BT):
                    gbt = e8 * BT + bt
                    if gbt >= 2:
                        sy.wait_ge(s_bus, gbt - 1)
                    sy.dma_start(out=(r2b, r2c)[gbt % 2][:],
                                 in_=d["ebatch"][base + bt * 128: base + (bt + 1) * 128, :]
                                 ).then_inc(s_bld, 16)
                if d.get("only_batches"):
                    continue
                for hb in range(8):
                    k = e8 * 8 + hb
                    if k >= 2:
                        sy.wait_ge(s_guus, k - 1)
                    srcg = d["wgT_bf"][e].rearrange("(c p) h -> p c h", p=128)[:, :, hb * 512:(hb + 1) * 512]
                    sy.dma_start(out=wgc[k % 2].rearrange("p (c h) -> p c h", c=C8),
                                 in_=srcg).then_inc(s_guld, 16)
                    srcu = d["wuT_bf"][e].rearrange("(c p) h -> p c h", p=128)[:, :, hb * 512:(hb + 1) * 512]
                    sy.dma_start(out=wuc[k % 2].rearrange("p (c h) -> p c h", c=C8),
                                 in_=srcu).then_inc(s_guld, 16)
                if d.get("no_wd"):
                    continue
                for i in range(n_tsg * 4):
                    lci = i % 4
                    k = e8 * n_tsg * 4 + i
                    if k >= 2:
                        sy.wait_ge(s_wdus, k - 1)
                    sy.dma_start(out=wdc[k % 2].rearrange("p (hc c) -> p hc c", hc=8),
                                 in_=d["wdT_bf"][e][lci * 1024:(lci + 1) * 1024, :]
                                 .rearrange("(hc p) c -> p hc c", p=128)
                                 ).then_inc(s_wdld, 16)

        for e8 in range(E * 2):
            e, bh = divmod(e8, 2)
            base = e * K_cap + bh * HKC
            psv8 = psT[0:128, 0:1024].rearrange("q (b c) -> q b c", c=128)
            dst3e = r2bT.rearrange("p (c x) -> p c x", c=C8)
            for bt in range(BT):
                gbt = e8 * BT + bt

                @block.tensor
                def _(te, bt=bt, gbt=gbt):
                    if bt == 0:
                        ctx.gate(te)
                    te.wait_ge(s_bld, 16 * (gbt + 1))
                    if gbt >= 1:
                        te.wait_ge(s_tpb, gbt)
                    src = (r2b, r2c)[gbt % 2]
                    ins = None
                    for c in range(C8):
                        ins = te.transpose(psT[0:128, c * 128:(c + 1) * 128],
                                           src[:, c * 128:(c + 1) * 128], idn[:, 0:128])
                    ins.then_inc(s_tpa, 1)

                @block.vector
                def _(v, bt=bt, gbt=gbt, last=(bt == BT - 1)):
                    v.wait_ge(s_tpa, gbt + 1)
                    v.engine_nop().then_inc(s_bus, 1)
                    ins = v.tensor_copy(dst3e[:, :, bt * 128:(bt + 1) * 128],
                                        psv8[:, 0:C8, :])
                    ins.then_inc(s_tpb, 1)
                    if last:
                        v.wait_ge(s_tpb, gbt + 1)
                        ctx.inc(v.engine_nop())

            if d.get("skip_phase1"):
                continue
            for hb in range(8):
                k = e8 * 8 + hb
                for hs in range(4):
                    hc = hb * 4 + hs
                    for si, (sub0, subw) in enumerate(SUBS):
                        u = ctx.cs_unit = getattr(ctx, "cs_unit", -1) + 1
                        par = u % 2
                        first_e = (hb == 0 and hs == 0 and si == 0)
                        first_hb = (hs == 0 and si == 0)
                        last_hb = (hs == 3 and si == len(SUBS) - 1)
                        last_e = (hb == 7 and last_hb)
                        gs = (psG, psG2)[par][:, 0:subw]
                        us_ = (psU, psU2)[par][:, 0:subw]
                        gt = gtmp[:, par * 512:par * 512 + subw]

                        @block.tensor
                        def _(te, hs=hs, k=k, u=u, gs=gs, us_=us_, sub0=sub0,
                              subw=subw, first_e=first_e, first_hb=first_hb):
                            if first_e:
                                ctx.gate(te)
                            if first_hb:
                                te.wait_ge(s_guld, 32 * (k + 1))
                            if u >= 2:
                                te.wait_ge(sD, u - 1)
                            ins = None
                            for ci in range(C8):
                                ins = te.matmul(
                                    gs,
                                    wgc[k % 2][:, ci * 512 + hs * 128: ci * 512 + hs * 128 + 128],
                                    r2bT[:, ci * HKC + sub0: ci * HKC + sub0 + subw],
                                    start=(ci == 0), stop=(ci == C8 - 1))
                            for ci in range(C8):
                                ins = te.matmul(
                                    us_,
                                    wuc[k % 2][:, ci * 512 + hs * 128: ci * 512 + hs * 128 + 128],
                                    r2bT[:, ci * HKC + sub0: ci * HKC + sub0 + subw],
                                    start=(ci == 0), stop=(ci == C8 - 1))
                            ins.then_inc(sP, 1)

                        @block.scalar
                        def _(s, u=u, gs=gs, gt=gt):
                            s.wait_ge(sP, u + 1)
                            s.activation(gt, gs, AF.Silu).then_inc(sA, 1)

                        @block.vector
                        def _(v, u=u, hc=hc, sub0=sub0, subw=subw, gt=gt, us_=us_,
                              first_e=first_e, last_hb=last_hb, last_e=last_e):
                            if first_e:
                                ctx.gate(v)
                            v.wait_ge(sA, u + 1)
                            if last_hb:
                                v.engine_nop().then_inc(s_guus, 1)
                            ins = v.tensor_tensor(
                                hbuf[:, hc * HKC + sub0: hc * HKC + sub0 + subw],
                                gt, us_, ALU.mult)
                            ins.then_inc(sD, 1)
                            if last_e:
                                v.wait_ge(sD, u + 1)
                                ctx.inc(v.engine_nop())

            if d.get("skip_down"):
                continue
            ts0 = 0
            tsg_i = 0
            while ts0 < BT:
                nts = min(3, BT - ts0)
                for lci in range(4):
                    k = (e8 * n_tsg + tsg_i) * 4 + lci

                    @block.tensor
                    def _(te, lci=lci, ts0=ts0, nts=nts, k=k):
                        if lci == 0:
                            ctx.gate(te)
                        te.wait_ge(s_wdld, 16 * (k + 1))
                        ins = None
                        for hs in range(8):
                            hc = lci * 8 + hs
                            for tsl in range(nts):
                                ts = ts0 + tsl
                                for cb in range(2):
                                    ins = te.matmul(
                                        psALL[:, tsl * 1024 + cb * 512: tsl * 1024 + (cb + 1) * 512],
                                        hbuf[:, hc * HKC + ts * 128: hc * HKC + ts * 128 + 128],
                                        wdc[k % 2][:, hs * C + cb * 512: hs * C + (cb + 1) * 512],
                                        start=(hc == 0), stop=(hc == 31))
                        if lci == 3:
                            ctx.inc(ins)
                        else:
                            ins.then_inc(s_wdus, 1)

                for tsl in range(nts):
                    ts = ts0 + tsl

                    @block.vector
                    def _(v, tsl=tsl, first=(tsl == 0)):
                        ctx.gate(v)
                        if first:
                            v.engine_nop().then_inc(s_wdus, 1)
                        ctx.inc(v.tensor_copy(ybuf[:], psALL[:, tsl * 1024:(tsl + 1) * 1024]))

                    @block.gpsimd
                    def _(g, base=base, ts=ts):
                        ctx.gate(g)
                        ctx.inc(g.dma_start(
                            out=d["ydram"][base + ts * 128: base + (ts + 1) * 128, :],
                            in_=ybuf[:]), 16)
                ts0 += nts
                tsg_i += 1

        if d.get("stop_after") == "E":
            return
        for t in range(T):
            @block.gpsimd
            def _(g, t=t):
                ctx.gate(g)
                ctx.inc(g.indirect_dma_start(
                    ybuf[:], None, d["ydram"],
                    IndirectOffsetOnAxis(ap=gixB[:, t * 2:t * 2 + 1], axis=0),
                    oob_is_err=False), 16)
                ctx.inc(g.indirect_dma_start(
                    y2buf[:], None, d["ydram"],
                    IndirectOffsetOnAxis(ap=gixB[:, t * 2 + 1:t * 2 + 2], axis=0),
                    oob_is_err=False), 16)

            @block.sync
            def _(sy, t=t):
                ctx.gate(sy)
                ctx.inc(sy.dma_start(out=x1t[:], in_=d["x1"][t * 128:(t + 1) * 128, :]), 16)
                ctx.inc(sy.dma_start(out=at[:], in_=d["a"][t * 128:(t + 1) * 128, :]), 16)

            @block.vector
            def _(v, t=t):
                ctx.gate(v)
                ctx.inc(v.tensor_scalar(ybuf[:], ybuf[:], gwB[:, t * 2:t * 2 + 1],
                                        None, ALU.mult))
                ctx.gate(v)
                ctx.inc(v.scalar_tensor_tensor(ybuf[:], y2buf[:],
                                               gwB[:, t * 2 + 1:t * 2 + 2],
                                               ybuf[:], ALU.mult, ALU.add))
                ctx.gate(v)
                ctx.inc(v.tensor_tensor(x1t[:], x1t[:], at[:], ALU.add))
                ctx.gate(v)
                ctx.inc(v.tensor_tensor(x1t[:], x1t[:], ybuf[:], ALU.add))

            @block.sync
            def _(sy, t=t):
                ctx.gate(sy)
                ctx.inc(sy.dma_start(out=d["x2"][t * 128:(t + 1) * 128, :],
                                     in_=x1t[:]), 16)


def declare_stageC_sparse(nc, S, K_cap, kind_in="ExternalInput",
                          kind_out="ExternalOutput", act_kind=None):
    T = S // 128
    d = declare_stageC(nc, S, kind_in=kind_in, kind_out=kind_out, act_kind=act_kind)
    d["K_cap"] = K_cap
    d["Lsb"] = nc.dram_tensor("Lsb", [128, 128], F32, kind=kind_in).ap()
    d["ebrow"] = nc.dram_tensor("ebrow", [1, 8 * T], F32, kind=kind_in).ap()
    d["r2d"] = nc.dram_tensor("r2d", [S, C], BF16, kind="Internal").ap()
    d["ebatch"] = nc.dram_tensor("ebatch", [8 * K_cap, C], BF16, kind="Internal").ap()
    d["ydram"] = nc.dram_tensor("ydram", [8 * K_cap, C], F32, kind="Internal").ap()
    return d


def prep_stageC_sparse_extra(S, K_cap):
    T = S // 128
    return {
        "Lsb": np.triu(np.ones((128, 128), np.float32), 1),
        "ebrow": np.ascontiguousarray(
            np.tile(np.arange(8, dtype=np.float32) * K_cap, T).reshape(1, 8 * T)),
    }


def build_stageC_sparse(S, K_cap):
    nc = bass.Bass("TRN2", target_bir_lowering=False)
    d = declare_stageC_sparse(nc, S, K_cap)
    with ExitStack() as es:
        ctx = Ctx(nc, es)
        emit_stageC_sparse(ctx, S, d)

        @ctx.block.sync
        def _(sy):
            ctx.gate(sy)
    return nc


from concourse.bass_utils import run_bass_kernel_spmd

S_FULL = 4096
N_CORES = 8


def kernel(**inputs):
    f = np.float32
    x0 = np.ascontiguousarray(np.asarray(inputs["L0_attn_x"], f))
    B = x0.shape[0]
    nc = build_fused(S_FULL)

    sh = {}
    sh.update(prep_stageA_inputs(
        S_FULL, np.zeros((1, 1), f),
        inputs["L0_attn_wq"], inputs["L0_attn_bq"],
        inputs["L0_attn_wk"], inputs["L0_attn_bk"],
        inputs["L0_attn_wv"], inputs["L0_attn_bv"],
        inputs["L0_attn_wo"], inputs["L0_attn_bo"]))
    del sh["x0"]
    sh.update(prep_stageB_inputs(S_FULL, inputs["L0_ffn_qkv_w"],
                                 inputs["L0_ffn_proj_w"], inputs["L0_ffn_norm1_w"]))
    sh.update(prep_stageC_inputs(S_FULL, inputs["L0_ffn_norm2_w"],
                                 inputs["L0_ffn_router_w"], inputs["L0_ffn_wg"],
                                 inputs["L0_ffn_wu"], inputs["L0_ffn_wd"]))
    sh.update(prep_stageC_sparse_extra(S_FULL, 2048))
    sh.update(prep_stageDE_inputs(inputs["L1_attn_Wqkv"], inputs["L1_attn_Wo"],
                                  inputs["L1_ffn_W_r"], inputs["L1_ffn_W_k"],
                                  inputs["L1_ffn_W_v"], inputs["L1_ffn_mu_r"],
                                  inputs["L1_ffn_mu_k"]))
    maps = []
    for c in range(N_CORES):
        m = dict(sh)
        m["x0"] = np.ascontiguousarray(x0[c // 2])
        maps.append(m)
    res = run_bass_kernel_spmd(nc, maps, core_ids=list(range(N_CORES)))
    out = np.stack([np.asarray(res.results[2 * b]["y"]) for b in range(B)])
    return out.astype(np.float32)
